# version 23
# baseline (speedup 1.0000x reference)
from contextlib import ExitStack

import numpy as np
import ml_dtypes

import concourse.bass as bass
import concourse.mybir as mybir
from concourse.bass_utils import run_bass_kernel_spmd

F32 = mybir.dt.float32
BF16 = mybir.dt.bfloat16
AF = mybir.ActivationFunctionType
ALU = mybir.AluOpType
AX = mybir.AxisListType

MAGIC = 12582912.0
NCORES = 8
T = 2048
D = 1024
DCH = 8
I = 2816
ITCH = 22
NH = 16
QMAX = 127.0


class Prog:
    ENG_OF = {"pe": "tensor", "dv": "vector", "ac": "scalar", "gp": "gpsimd"}

    def __init__(self, nc, ctx):
        self.sem = {}
        self.cnt = {}
        self.last_wait = {}
        for name in ["pe", "dv", "ac", "gp", "cc",
                     "s_const", "wpl0", "wpl1", "s_qg", "s_qu", "s_qd",
                     "s_wa", "s_wb", "s_x0", "s_x1", "s_mx", "s_kvz",
                     "s_gl0", "s_gl1", "s_qdT", "s_qh", "s_h0", "s_h1",
                     "s_xo0", "s_xo1", "s_so0", "s_so1", "so"]:
            self.sem[name] = ctx.enter_context(nc.semaphore(name))
            self.cnt[name] = 0

    def inc(self, instr, s, n=1):
        self.cnt[s] += n
        instr.then_inc(self.sem[s], n)
        return (s, self.cnt[s])

    def now(self, s):
        return (s, self.cnt[s])

    def wait(self, eng, engname, *evs):
        for ev in evs:
            if ev is None:
                continue
            s, v = ev
            if v <= 0 or self.ENG_OF.get(s) == engname:
                continue
            key = (engname, s)
            if self.last_wait.get(key, -1) >= v:
                continue
            self.last_wait[key] = v
            eng.wait_ge(self.sem[s], v)


def build_program(tap=None):
    nc = bass.Bass("TRN2", target_bir_lowering=False,
                   dynamic_dma_scratch_size=24576,
                   detect_race_conditions=False)
    dt = nc.dram_tensor

    xfm = dt("xfm", [D, T], F32, kind="ExternalInput").ap()
    wqT = dt("wqT", [D, D], F32, kind="ExternalInput").ap()
    wkT = dt("wkT", [D, D], F32, kind="ExternalInput").ap()
    wvT = dt("wvT", [D, D], F32, kind="ExternalInput").ap()
    woT = dt("woT", [D, D], F32, kind="ExternalInput").ap()
    wg = dt("wg", [I, D], F32, kind="ExternalInput").ap()
    wu = dt("wu", [I, D], F32, kind="ExternalInput").ap()
    wd = dt("wd", [D, I], F32, kind="ExternalInput").ap()
    bq_c = dt("bq_c", [128, DCH], F32, kind="ExternalInput").ap()
    bo_c = dt("bo_c", [128, DCH], F32, kind="ExternalInput").ap()
    bk_r = dt("bk_r", [1, D], F32, kind="ExternalInput").ap()
    bv_r = dt("bv_r", [1, D], F32, kind="ExternalInput").ap()
    clip2 = dt("clip2", [1, 2], F32, kind="ExternalInput").ap()
    bselin = dt("bselin", [128, 2], F32, kind="ExternalInput").ap()
    identin = dt("identin", [128, 128], F32, kind="ExternalInput").ap()
    pairselin = dt("pairselin", [16, 1024], BF16, kind="ExternalInput").ap()
    out = dt("out", [D, T], F32, kind="ExternalOutput").ap()

    qg_d = dt("qg_d", [I, D], BF16)
    qu_d = dt("qu_d", [I, D], BF16)
    qd_d = dt("qd_d", [D, I], BF16)
    h_d = dt("h_d", [I, T], BF16)
    kvz_in = dt("kvz_in", [128, 1056], F32)
    kvz_out = dt("kvz_out", [128, 1056], F32)
    mx1_in = dt("mx1_in", [1, 8], F32)
    mx1_out = dt("mx1_out", [1, 8], F32)
    mx2_in = dt("mx2_in", [1, 8], F32)
    mx2_out = dt("mx2_out", [1, 8], F32)
    kvzbf_d = dt("kvzbf_d", [64, 1056], BF16)

    ctx = ExitStack()
    with ctx:
        def sb(name, shape, dty):
            return ctx.enter_context(nc.sbuf_tensor(name, shape, dty))

        def alloc(name, shape, dty):
            cm = nc.sbuf_tensor(name, shape, dty, side="right")
            return cm, cm.__enter__()

        block = ctx.enter_context(nc.Block())
        P = Prog(nc, ctx)
        RG = [list(range(NCORES))]
        ev = {}

        xn_qx = sb("xn_qx", [128, DCH * T], BF16)
        bk_bc = sb("bk_bc", [128, D], F32)
        bv_bc = sb("bv_bc", [128, D], F32)
        bqc = sb("bqc", [128, DCH], F32)
        boc = sb("boc", [128, DCH], F32)
        clip_s = sb("clip_s", [1, 2], F32)
        bsel = sb("bsel", [128, 2], F32)
        ident = sb("ident", [128, 128], F32)
        ones_b = sb("ones_b", [128, 1], BF16)
        ones_c = sb("ones_c", [128, 1], F32)
        ones_r = sb("ones_r", [1, 128], F32)
        wsg = sb("wsg", [128, ITCH], F32)
        wsu = sb("wsu", [128, ITCH], F32)
        wsd = sb("wsd", [128, DCH], F32)
        gs_c = sb("gs_c", [128, ITCH], F32)
        us_c = sb("us_c", [128, ITCH], F32)
        ds_c = sb("ds_c", [128, DCH], F32)
        rcol = sb("rcol", [128, 2], F32)
        shcol = sb("shcol", [128, 1], F32)
        wrec = sb("wrec", [128, 16], F32)
        negm = sb("negm", [128, 1], F32)
        posm = sb("posm", [128, 1], F32)
        rows = sb("rows", [1, 1536], F32)
        srow = sb("srow", [1, 16], F32)
        mxs = sb("mxs", [128, 8], F32)
        mxg = sb("mxg", [1, 8], F32)
        mxs2 = sb("mxs2", [128, 8], F32)
        mxg2 = sb("mxg2", [1, 8], F32)
        hrun = sb("hrun", [128, 8], F32)
        fst = [sb(f"fst{i}", [128, 512], F32) for i in range(4)]
        bst = [sb(f"bst{i}", [128, 512], BF16) for i in range(4)]

        PB = [ctx.enter_context(nc.psum_tensor(f"pb{i}", [128, 512], F32))
              for i in range(8)]
        bank_free = [None] * 8

        q_a = sb("q_a", [128, DCH * T], BF16)
        pairs = sb("pairs", [16, 1024], BF16)
        kvz_sb = sb("kvz_sb", [64, 1056], F32)
        kvz_m = sb("kvz_m", [64, 1056], F32)
        kvz_bf = sb("kvz_bf", [128, 1056], BF16)
        wf320 = sb("wf320", [128, I], F32)
        cm_wa1, wa1 = alloc("wa1", [128, DCH * 1024], BF16)
        cm_wa0, wa0 = alloc("wa0", [128, DCH * 1024], BF16)
        wa = [wa0, wa1]
        cm_wf1, wf321 = alloc("wf321", [128, I], F32)
        wf32 = [wf320, wf321]
        cm_xsl0, xsl0 = alloc("xsl0", [128, DCH * 512], F32)
        cm_ktm0, ktm0 = alloc("ktm0", [128, D], BF16)
        cm_ktm1, ktm1 = alloc("ktm1", [128, D], BF16)
        ktm = [ktm0, ktm1]
        cm_vtm0, vtm0 = alloc("vtm0", [128, D], BF16)
        cm_vtm1, vtm1 = alloc("vtm1", [128, D], BF16)
        vtm = [vtm0, vtm1]
        cm_xsl1, xsl1 = alloc("xsl1", [128, DCH * 512], F32)
        xsl = [xsl0, xsl1]

        bkrow = kvz_sb[0:1, 0:D]
        bvrow = kvz_m[0:1, 0:D]

        @block.sync
        def _(sync):
            e = None
            for d_, s_ in [(bqc, bq_c), (boc, bo_c),
                           (clip_s, clip2), (bsel, bselin), (ident, identin),
                           (pairs, pairselin)]:
                e = P.inc(sync.dma_start(out=d_[:], in_=s_), "s_const", 16)
            ev["const"] = e

        @block.gpsimd
        def _(gpsimd):
            ev["wk"] = P.inc(gpsimd.dma_start(
                out=wa[0][:].rearrange("p (c n) -> p c n", c=DCH), in_=wkT.rearrange("(c p) n -> p c n", p=128)), "s_wa", 16)
            P.wait(gpsimd, "gpsimd", ev["wk"])
            ev["wv"] = P.inc(gpsimd.dma_start(
                out=wa[1][:].rearrange("p (c n) -> p c n", c=DCH), in_=wvT.rearrange("(c p) n -> p c n", p=128)), "s_wb", 16)

        @block.vector
        def _(vector):
            vector.memset(ones_b[:], 1.0)
            vector.memset(ones_c[:], 1.0)
            vector.memset(ones_r[:], 1.0)
            vector.memset(negm[:], -MAGIC)
            vector.memset(posm[:], MAGIC)
            vector.memset(hrun[:], 0.0)
            vector.memset(mxs[:], 0.0)
            vector.memset(mxs2[:], 0.0)
            P.wait(vector, "vector", ev["const"])
            ev["memset"] = P.inc(vector.memset(kvz_sb[:], 0.0), "dv")

        @block.sync
        def _(sync):
            P.wait(sync, "sync", ev["memset"])
            P.inc(sync.dma_start(out=bkrow, in_=bk_r), "s_const", 16)
            ev["brows"] = P.inc(sync.dma_start(out=bvrow, in_=bv_r), "s_const", 16)

        @block.scalar
        def _(scalar):
            P.wait(scalar, "scalar", ev["const"])
            scalar.activation(srow[0:1, 0:1], clip_s[0:1, 0:1], AF.Exp, scale=-1.0)
            ev["sige"] = P.inc(scalar.activation(srow[0:1, 8:9], clip_s[0:1, 1:2],
                                                 AF.Exp, scale=-1.0), "ac")

        @block.vector
        def _(vector):
            P.wait(vector, "vector", ev["sige"])
            vector.tensor_scalar_add(srow[0:1, 0:1], srow[0:1, 0:1], 1.0)
            e = P.inc(vector.tensor_scalar_add(srow[0:1, 8:9], srow[0:1, 8:9], 1.0),
                      "dv")
            vector.wait_ge(P.sem["dv"], e[1])
            vector.reciprocal(srow[0:1, 1:2], srow[0:1, 0:1])
            ev["sig"] = P.inc(vector.reciprocal(srow[0:1, 9:10], srow[0:1, 8:9]), "dv")

        for ih in range(2):
            @block.tensor
            def _(tensor, ih=ih):
                P.wait(tensor, "tensor", ev["const"], ev["memset"], ev["brows"])
                if ih == 1:
                    P.wait(tensor, "tensor", ev["bbce0"])
                sl5 = slice(ih * 512, (ih + 1) * 512)
                tensor.matmul(PB[6][:, :], ones_r[:], kvz_sb[0:1, sl5], start=True, stop=True)
                ev[f"bbc{ih}"] = P.inc(tensor.matmul(PB[7][:, :], ones_r[:],
                                                     kvz_m[0:1, sl5],
                                                     start=True, stop=True), "pe")

            @block.vector
            def _(vector, ih=ih):
                P.wait(vector, "vector", ev[f"bbc{ih}"])
                sl5 = slice(ih * 512, (ih + 1) * 512)
                vector.tensor_copy(bk_bc[:, sl5], PB[6][:, :])
                ev[f"bbce{ih}"] = P.inc(vector.tensor_copy(bv_bc[:, sl5], PB[7][:, :]), "dv")
        bank_free[6] = ev["bbce1"]
        bank_free[7] = ev["bbce1"]

        wprep_tiles = ([("g", t) for t in range(ITCH)] +
                       [("u", t) for t in range(ITCH)] +
                       [("d", t) for t in range(DCH)])
        wp = {"cur": 0, "ld": {}, "p1": {}, "st": {}, "stn": {}}
        wp_src = {"g": wg, "u": wu, "d": wd}
        wp_dst = {"g": qg_d, "u": qu_d, "d": qd_d}
        wp_cols = {"g": wsg, "u": wsu, "d": wsd}
        wp_fd = {"g": D, "u": D, "d": I}

        def wprep_emit(n):
            start = wp["cur"]
            end = min(start + n, len(wprep_tiles))
            if end == start:
                return
            wp["cur"] = end
            for k in range(start, end):
                w, t = wprep_tiles[k]
                par = k % 2
                fd = wp_fd[w]

                @block.sync
                def _(sync, k=k, w=w, t=t, par=par, fd=fd):
                    P.wait(sync, "sync", wp["st"].get(k - 2))
                    wp["ld"][k] = P.inc(sync.dma_start(
                        out=wf32[par][:, 0:fd],
                        in_=wp_src[w][t * 128:(t + 1) * 128, :]), f"wpl{k % 2}", 16)

                @block.vector
                def _(vector, k=k, w=w, t=t, par=par, fd=fd):
                    P.wait(vector, "vector", wp["ld"][k])
                    e1 = P.inc(vector.tensor_reduce(
                        wrec[:, 0:1], wf32[par][:, 0:fd],
                        AX.X, ALU.max, apply_absolute_value=True), "dv")
                    vector.wait_ge(P.sem["dv"], e1[1])
                    e2 = P.inc(vector.tensor_scalar_mul(
                        wp_cols[w][:, t:t + 1], wrec[:, 0:1], 1.0 / QMAX), "dv")
                    vector.wait_ge(P.sem["dv"], e2[1])
                    wp["rec"] = P.inc(vector.reciprocal(wrec[:, 8 + k % 2:9 + k % 2],
                                                        wp_cols[w][:, t:t + 1]), "dv")

                @block.scalar
                def _(scalar, k=k, w=w, t=t, par=par, fd=fd):
                    P.wait(scalar, "scalar", wp["rec"])
                    scalar.activation(wf32[par][:, 0:fd], wf32[par][:, 0:fd],
                                      AF.Identity, scale=wrec[:, 8 + k % 2:9 + k % 2],
                                      bias=posm[:, 0:1])
                    wp["p1"][k] = P.inc(scalar.activation(
                        wf32[par][:, 0:fd], wf32[par][:, 0:fd],
                        AF.Identity, bias=negm[:, 0:1]), "ac")

                @block.gpsimd
                def _(gpsimd, k=k, w=w, t=t, par=par, fd=fd):
                    P.wait(gpsimd, "gpsimd", wp["p1"][k])
                    wp["st"][k] = P.inc(gpsimd.dma_start(
                        out=wp_dst[w][t * 128:(t + 1) * 128, :],
                        in_=wf32[par][:, 0:fd]), {"g": "s_qg", "u": "s_qu",
                                                  "d": "s_qd"}[w], 16)
                    wp["stn"][w] = wp["st"][k]

        wprep_emit(2)

        xv = xfm.rearrange("(c p) n -> p c n", p=128)
        ev_xn = [None] * 4
        ev_xsl_use = [None, None]

        @block.scalar
        def _(scalar):
            for s_ in range(2):
                ev[f"xsl{s_}"] = P.inc(scalar.dma_start(
                    out=xsl[s_][:], in_=xv[:, :, s_ * 512:(s_ + 1) * 512]),
                    f"s_x{s_ % 2}", 16)

        for sl in range(4):
            par = sl % 2
            for c in range(DCH):
                @block.scalar
                def _(scalar, sl=sl, par=par, c=c):
                    P.wait(scalar, "scalar", ev[f"xsl{sl}"])
                    key = (f"sqmm_{sl}_{c - 2}" if c >= 2
                           else f"sqmm_{sl - 1}_{DCH + c - 2}")
                    P.wait(scalar, "scalar", ev.get(key))
                    ev[f"sq_{sl}_{c}"] = P.inc(scalar.activation(
                        fst[2 + c % 2][:], xsl[par][:, c * 512:(c + 1) * 512],
                        AF.Square), "ac")

                @block.tensor
                def _(tensor, sl=sl, par=par, c=c):
                    if c == 0:
                        P.wait(tensor, "tensor", bank_free[0 + par], bank_free[2 + par])
                        P.wait(tensor, "tensor", ev[f"xsl{sl}"])
                    ev[f"smm_{sl}_{c}"] = P.inc(
                        tensor.matmul(PB[0 + par][0:1, :], ones_c[:],
                                      xsl[par][:, c * 512:(c + 1) * 512],
                                      start=(c == 0), stop=(c == DCH - 1)), "pe")
                    P.wait(tensor, "tensor", ev[f"sq_{sl}_{c}"])
                    ev[f"sqmm_{sl}_{c}"] = P.inc(
                        tensor.matmul(PB[2 + par][0:1, :], ones_c[:], fst[2 + c % 2][:],
                                      start=(c == 0), stop=(c == DCH - 1)), "pe")
            ev[f"stats_{sl}"] = ev[f"sqmm_{sl}_{DCH - 1}"]

            rm = rows[0:1, 0:512]
            rv = rows[0:1, 512:1024]
            rr = rows[0:1, 1024:1536]

            @block.vector
            def _(vector, sl=sl, par=par):
                P.wait(vector, "vector", ev[f"stats_{sl}"])
                vector.tensor_scalar_mul(rm, PB[0 + par][0:1, :], 1.0 / D)
                vector.tensor_scalar_mul(rv, PB[2 + par][0:1, :], 1.0 / D)
                vector.tensor_tensor(rr, rm, rm, ALU.mult)
                ev[f"rows_v_{sl}"] = P.inc(vector.scalar_tensor_tensor(
                    rv, rv, 1e-5, rr, ALU.add, ALU.subtract), "dv")

            @block.scalar
            def _(scalar, sl=sl):
                P.wait(scalar, "scalar", ev[f"rows_v_{sl}"])
                scalar.activation(rr, rv, AF.Ln)
                ev[f"rows_ln_{sl}"] = P.inc(scalar.activation(rv, rr, AF.Exp, scale=-0.5),
                                            "ac")

            @block.vector
            def _(vector, sl=sl):
                P.wait(vector, "vector", ev[f"rows_ln_{sl}"])
                ev[f"rows_m_{sl}"] = P.inc(vector.tensor_tensor(rm, rm, rv, ALU.mult),
                                           "dv")

            @block.tensor
            def _(tensor, sl=sl, par=par):
                P.wait(tensor, "tensor", ev[f"rows_m_{sl}"],
                       bank_free[4 + par], bank_free[6 + par])
                tensor.matmul(PB[4 + par][:, :], ones_r[:], rv, start=True, stop=True)
                ev[f"bc_{sl}"] = P.inc(tensor.matmul(PB[6 + par][:, :], ones_r[:], rm,
                                                     start=True, stop=True), "pe")

            @block.vector
            def _(vector, sl=sl, par=par):
                P.wait(vector, "vector", ev[f"bc_{sl}"])
                e = None
                for c in range(DCH):
                    f = fst[c % 2]
                    vector.tensor_tensor(f[:], xsl[par][:, c * 512:(c + 1) * 512],
                                         PB[4 + par][:, :], ALU.mult)
                    e = P.inc(vector.tensor_tensor(
                        xn_qx[:, c * T + sl * 512: c * T + (sl + 1) * 512],
                        f[:], PB[6 + par][:, :], ALU.subtract), "dv")
                ev_xn[sl] = e
                ev_xsl_use[par] = e
                bank_free[4 + par] = e
                bank_free[6 + par] = e
                bank_free[0 + par] = ev[f"rows_v_{sl}"]
                bank_free[2 + par] = ev[f"rows_v_{sl}"]

            if sl + 2 < 4:
                @block.scalar
                def _(scalar, sl=sl, par=par):
                    P.wait(scalar, "scalar", ev_xsl_use[par])
                    ev[f"xsl{sl + 2}"] = P.inc(scalar.dma_start(
                        out=xsl[par][:], in_=xv[:, :, (sl + 2) * 512:(sl + 3) * 512]),
                        f"s_x{par}", 16)

            wprep_emit(3)

        cm_xsl1.__exit__(None, None, None)
        cm_wq, wq_sb = alloc("wq_sb", [128, DCH * 1024], BF16)

        @block.gpsimd
        def _(gpsimd):
            P.wait(gpsimd, "gpsimd", ev_xn[0], ev_xn[1], ev_xn[2], ev_xn[3])
            eR = P.inc(gpsimd.tensor_reduce(mxs[0:1, 0:1], xn_qx[:, :], AX.XYZWC,
                                            ALU.abs_max), "gp")
            gpsimd.wait_ge(P.sem["gp"], eR[1])
            P.inc(gpsimd.dma_start(out=mx1_in.ap(), in_=mxs[0:1, :]), "s_mx", 16)
            P.wait(gpsimd, "gpsimd", P.now("s_mx"))
            P.inc(gpsimd.collective_compute(
                "AllReduce", ALU.max, replica_groups=RG,
                ins=[mx1_in.ap().opt()], outs=[mx1_out.ap().opt()]), "cc")
            P.wait(gpsimd, "gpsimd", P.now("cc"))
            ev["mxg"] = P.inc(gpsimd.dma_start(out=mxg[:], in_=mx1_out.ap()), "s_mx", 16)

        if tap == "xn":
            @block.gpsimd
            def _(gpsimd):
                P.wait(gpsimd, "gpsimd", ev_xn[3])
                P.inc(gpsimd.dma_start(
                    out=out.rearrange("(c p) n -> p c n", p=128), in_=xn_qx[:].rearrange("p (c n) -> p c n", c=DCH)),
                    "so", 16)
                gpsimd.wait_ge(P.sem["so"], 16)
            return nc

        ev_ktm = [None, None]
        ev_vtm = [None, None]
        ev_kvzmm = [None, None]
        ev_ke = [None, None]

        def emit_kvz(nt):
            par = nt % 2

            @block.tensor
            def _(tensor, nt=nt, par=par):
                P.wait(tensor, "tensor", ev_ktm[par], ev_vtm[par])
                if nt == 0:
                    P.wait(tensor, "tensor", bank_free[4], bank_free[5],
                           bank_free[6])
                for h in range(NH):
                    pb = PB[4 + h // 8]
                    cs = (h % 8) * 64
                    tensor.matmul(pb[0:64, cs:cs + 64],
                                  ktm[par][:, h * 64:(h + 1) * 64],
                                  vtm[par][:, h * 64:(h + 1) * 64],
                                  start=(nt == 0 and h % 8 == 0),
                                  stop=(nt == 15 and h % 8 == 7),
                                  skip_group_check=True)
                e = None
                for h in range(NH):
                    e = P.inc(tensor.matmul(
                        PB[6][0:64, h:h + 1], ktm[par][:, h * 64:(h + 1) * 64],
                        ones_b[:], start=(nt == 0 and h == 0),
                        stop=(nt == 15 and h == 15),
                        skip_group_check=True), "pe")
                ev_kvzmm[par] = e

        for nt in range(16):
            par = nt % 2
            b0, b1 = 0 + 2 * par, 1 + 2 * par

            @block.tensor
            def _(tensor, nt=nt, par=par, b0=b0, b1=b1):
                P.wait(tensor, "tensor", ev["wk"], ev_xn[nt // 4],
                       bank_free[b0], bank_free[b1])
                e = None
                for ih in range(2):
                    for c in range(DCH):
                        e = P.inc(tensor.matmul(
                            PB[b0 + ih][:, :],
                            xn_qx[:, c * T + nt * 128: c * T + (nt + 1) * 128],
                            wa[0][:, c * 1024 + ih * 512: c * 1024 + (ih + 1) * 512],
                            start=(c == 0), stop=(c == DCH - 1)), "pe")
                ev[f"kmm_{nt}"] = e

            @block.vector
            def _(vector, nt=nt, par=par, b0=b0):
                P.wait(vector, "vector", ev[f"kmm_{nt}"], ev["bbce1"],
                       ev_kvzmm[par], ev_ke[1 - par])
                e = None
                for ih in range(2):
                    e = P.inc(vector.tensor_tensor(
                        fst[2 + ih][:], PB[b0 + ih][:, :],
                        bk_bc[:, ih * 512:(ih + 1) * 512], ALU.add), "dv")
                ev[f"kt_{nt}"] = e

            @block.vector
            def _(vector, nt=nt):
                e = None
                for ih in range(2):
                    e = P.inc(vector.tensor_scalar_max(bst[2 + ih][:],
                                                       fst[2 + ih][:], 0.0), "dv")
                ev[f"kr_{nt}"] = e

            @block.scalar
            def _(scalar, nt=nt):
                P.wait(scalar, "scalar", ev[f"kt_{nt}"], ev[f"kr_{nt}"])
                e = None
                for ih in range(2):
                    scalar.activation(fst[2 + ih][:], fst[2 + ih][:], AF.Relu, scale=-1.0)
                for ih in range(2):
                    e = P.inc(scalar.activation(bst[ih][:], fst[2 + ih][:],
                                                AF.Exp, scale=-1.0), "ac")
                ev[f"ke_{nt}"] = e
                ev_ke[nt % 2] = e

            @block.vector
            def _(vector, nt=nt, par=par, b0=b0, b1=b1):
                P.wait(vector, "vector", ev[f"ke_{nt}"])
                e = None
                for ih in range(2):
                    e = P.inc(vector.tensor_tensor(
                        ktm[par][:, ih * 512:(ih + 1) * 512], bst[ih][:],
                        bst[2 + ih][:], ALU.add), "dv")
                ev_ktm[par] = e
                bank_free[b0] = e
                bank_free[b1] = e

            @block.tensor
            def _(tensor, nt=nt, par=par, b0=b0, b1=b1):
                P.wait(tensor, "tensor", ev["wv"], bank_free[b0], bank_free[b1])
                e = None
                for ih in range(2):
                    for c in range(DCH):
                        e = P.inc(tensor.matmul(
                            PB[b0 + ih][:, :],
                            xn_qx[:, c * T + nt * 128: c * T + (nt + 1) * 128],
                            wa[1][:, c * 1024 + ih * 512: c * 1024 + (ih + 1) * 512],
                            start=(c == 0), stop=(c == DCH - 1)), "pe")
                ev[f"vmm_{nt}"] = e

            @block.vector
            def _(vector, nt=nt, par=par, b0=b0, b1=b1):
                P.wait(vector, "vector", ev[f"vmm_{nt}"])
                e = None
                for ih in range(2):
                    e = P.inc(vector.tensor_tensor(
                        vtm[par][:, ih * 512:(ih + 1) * 512], PB[b0 + ih][:, :],
                        bv_bc[:, ih * 512:(ih + 1) * 512], ALU.add), "dv")
                ev_vtm[par] = e
                bank_free[b0] = e
                bank_free[b1] = e

            if nt >= 1:
                emit_kvz(nt - 1)
            wprep_emit(2)
            if nt == 10:
                @block.gpsimd
                def _(gpsimd):
                    P.wait(gpsimd, "gpsimd", ev_xn[3])
                    ev["wq"] = P.inc(gpsimd.dma_start(
                        out=wq_sb[:].rearrange("p (c n) -> p c n", c=DCH),
                        in_=wqT.rearrange("(c p) n -> p c n", p=128)), "s_wa", 16)
        emit_kvz(15)
        ev["kvzdone"] = ev_kvzmm[1]

        @block.vector
        def _(vector):
            P.wait(vector, "vector", ev["kvzdone"], ev["memset"], ev["bbce1"])
            e0 = None
            for h in range(NH):
                pb = PB[4 + h // 8]
                cs = (h % 8) * 64
                vector.tensor_copy(kvz_sb[:, h * 66:h * 66 + 64], pb[0:64, cs:cs + 64])
            for h in range(NH):
                e0 = P.inc(vector.tensor_copy(kvz_sb[:, h * 66 + 64:h * 66 + 65],
                                              PB[6][0:64, h:h + 1]), "dv")
            bank_free[4] = e0
            bank_free[5] = e0
            bank_free[6] = e0
            ev["kvz_evac"] = e0
            ev["kvzm0"] = P.inc(vector.tensor_scalar_mul(kvz_m[:], kvz_sb[:],
                                                         bsel[0:64, 0:1]), "dv")

        if tap == "kvz0":
            @block.gpsimd
            def _(gpsimd):
                P.wait(gpsimd, "gpsimd", ev["kvz_evac"])
                P.inc(gpsimd.dma_start(out=out[0:64, 0:1056], in_=kvz_sb[:]), "so", 16)
                gpsimd.wait_ge(P.sem["so"], 16)
            return nc

        @block.gpsimd
        def _(gpsimd):
            P.wait(gpsimd, "gpsimd", ev["kvzm0"])
            e = P.inc(gpsimd.dma_start(out=kvz_in.ap()[0:64, :], in_=kvz_m[:]),
                      "s_kvz", 16)
            ev["kvzs0"] = e

        @block.vector
        def _(vector):
            P.wait(vector, "vector", ev["kvzs0"])
            ev["kvzm1"] = P.inc(vector.tensor_scalar_mul(kvz_m[:], kvz_sb[:],
                                                         bsel[0:64, 1:2]), "dv")

        @block.gpsimd
        def _(gpsimd):
            P.wait(gpsimd, "gpsimd", ev["kvzm1"])
            P.inc(gpsimd.dma_start(out=kvz_in.ap()[64:128, :], in_=kvz_m[:]),
                  "s_kvz", 16)
            P.wait(gpsimd, "gpsimd", P.now("s_kvz"))
            P.inc(gpsimd.collective_compute(
                "AllReduce", ALU.add, replica_groups=RG,
                ins=[kvz_in.ap().opt()], outs=[kvz_out.ap().opt()]), "cc")
            P.wait(gpsimd, "gpsimd", P.now("cc"))
            P.inc(gpsimd.dma_start(out=kvz_sb[:], in_=kvz_out.ap()[0:64, :]),
                  "s_kvz", 16)
            ev["kvzr"] = P.inc(gpsimd.dma_start(
                out=kvz_m[:], in_=kvz_out.ap()[64:128, :]), "s_kvz", 16)

        @block.vector
        def _(vector):
            P.wait(vector, "vector", ev["kvzr"])
            vector.tensor_scalar_mul(kvz_sb[:], kvz_sb[:], bsel[0:64, 0:1])
            vector.scalar_tensor_tensor(kvz_sb[:], kvz_m[:], bsel[0:64, 1:2], kvz_sb[:],
                                        ALU.mult, ALU.add)
            ev["kvzbf"] = P.inc(vector.tensor_copy(kvz_bf[0:64, :], kvz_sb[:]), "dv")

        @block.gpsimd
        def _(gpsimd):
            P.wait(gpsimd, "gpsimd", ev["kvzbf"])
            P.inc(gpsimd.dma_start(out=kvzbf_d.ap(), in_=kvz_bf[0:64, :]),
                  "s_kvz", 16)
            P.wait(gpsimd, "gpsimd", P.now("s_kvz"))
            ev["kvzbf2"] = P.inc(gpsimd.dma_start(out=kvz_bf[64:128, :],
                                                  in_=kvzbf_d.ap()), "s_kvz", 16)


        if tap == "kvz":
            @block.gpsimd
            def _(gpsimd):
                P.wait(gpsimd, "gpsimd", ev["kvzbf"])
                P.inc(gpsimd.dma_start(out=out[0:64, 0:1056], in_=kvz_sb[:]), "so", 16)
                gpsimd.wait_ge(P.sem["so"], 16)
            return nc

        qit = -1
        for ich in range(DCH):
            for nt in range(4):
                qit += 1
                pb = qit % 4
                ip = qit % 2

                @block.tensor
                def _(tensor, ich=ich, nt=nt, pb=pb):
                    P.wait(tensor, "tensor", ev["wq"], bank_free[pb])
                    e = None
                    for c in range(DCH):
                        e = P.inc(tensor.matmul(
                            PB[pb][:, :],
                            wq_sb[:, c * 1024 + ich * 128: c * 1024 + (ich + 1) * 128],
                            xn_qx[:, c * T + nt * 512: c * T + (nt + 1) * 512],
                            start=(c == 0), stop=(c == DCH - 1)), "pe")
                    ev[f"qmm_{qit}"] = e

                @block.vector
                def _(vector, ich=ich, qit=qit, pb=pb, ip=ip):
                    P.wait(vector, "vector", ev[f"qmm_{qit}"], ev.get(f"qe_{qit - 2}"))
                    ev[f"qt_{qit}"] = P.inc(vector.tensor_scalar(
                        fst[ip][:], PB[pb][:, :], bqc[:, ich:ich + 1], None, ALU.add),
                        "dv")

                if qit % 2 == 1:
                    @block.scalar
                    def _(scalar, qit=qit):
                        P.wait(scalar, "scalar", ev[f"qt_{qit - 1}"],
                               ev[f"qt_{qit}"])
                        for q2 in (qit - 1, qit):
                            ip2 = q2 % 2
                            scalar.activation(fst[2 + ip2][:], fst[ip2][:],
                                              AF.Relu, scale=-1.0)
                        for q2 in (qit - 1, qit):
                            ip2 = q2 % 2
                            ev[f"qe_{q2}"] = P.inc(scalar.activation(
                                bst[ip2][:], fst[2 + ip2][:],
                                AF.Exp, scale=-1.0), "ac")

                if qit % 2 == 1:
                    @block.vector
                    def _(vector, qit=qit):
                        for q2 in (qit - 1, qit):
                            ip2 = q2 % 2
                            pb2 = q2 % 4
                            ich2, nt2 = q2 // 4, q2 % 4
                            P.wait(vector, "vector", ev[f"qe_{q2}"])
                            vector.tensor_scalar_max(bst[2 + ip2][:],
                                                     fst[ip2][:], 0.0)
                            e = P.inc(vector.tensor_tensor(
                                q_a[:, ich2 * T + nt2 * 512:
                                    ich2 * T + (nt2 + 1) * 512],
                                bst[ip2][:], bst[2 + ip2][:], ALU.add), "dv")
                            ev[f"qph_{q2}"] = e
                            bank_free[pb2] = e
            if ich % 2 == 0:
                wprep_emit(2)
        ev["qdone"] = ev[f"qph_{qit}"]

        if tap == "q":
            @block.gpsimd
            def _(gpsimd):
                P.wait(gpsimd, "gpsimd", ev["qdone"])
                P.inc(gpsimd.dma_start(
                    out=out.rearrange("(c p) n -> p c n", p=128), in_=q_a[:].rearrange("p (c n) -> p c n", c=DCH)),
                    "so", 16)
                gpsimd.wait_ge(P.sem["so"], 16)
            return nc

        wprep_emit(len(wprep_tiles))
        assert wp["cur"] == len(wprep_tiles)

        if tap == "w52":
            @block.gpsimd
            def _(gpsimd):
                P.wait(gpsimd, "gpsimd", wp["st"][len(wprep_tiles) - 1],
                       wp["st"][len(wprep_tiles) - 2], ev["qdone"], ev["mxg"])
                P.inc(gpsimd.dma_start(out=out[0:128, 0:8], in_=wsd[:, 0:8]), "so", 16)
                gpsimd.wait_ge(P.sem["so"], 16)
            return nc

        @block.vector
        def _(vector):
            P.wait(vector, "vector", ev["mxg"], ev["sig"])
            e1 = P.inc(vector.tensor_tensor(srow[0:1, 2:3], srow[0:1, 1:2],
                                            mxg[0:1, 0:1], ALU.mult), "dv")
            vector.wait_ge(P.sem["dv"], e1[1])
            e2 = P.inc(vector.tensor_scalar_mul(srow[0:1, 3:4], srow[0:1, 2:3],
                                                1.0 / QMAX), "dv")
            vector.wait_ge(P.sem["dv"], e2[1])
            ev["rx"] = P.inc(vector.reciprocal(srow[0:1, 4:5], srow[0:1, 3:4]), "dv")

        @block.tensor
        def _(tensor):
            P.wait(tensor, "tensor", ev["rx"], bank_free[4])
            tensor.matmul(PB[4][:, 0:1], ones_r[:], srow[0:1, 4:5], start=True,
                          stop=True)
            ev["rxbc"] = P.inc(tensor.matmul(PB[4][:, 1:2], ones_r[:], srow[0:1, 3:4],
                                             start=True, stop=True), "pe")

        @block.vector
        def _(vector):
            P.wait(vector, "vector", ev["rxbc"])
            vector.tensor_copy(rcol[:, 0:1], PB[4][:, 0:1])
            e = P.inc(vector.tensor_copy(rcol[:, 1:2], PB[4][:, 1:2]), "dv")
            bank_free[4] = e
            vector.wait_ge(P.sem["dv"], e[1])
            vector.tensor_scalar_mul(gs_c[:], wsg[:], rcol[:, 1:2])
            ev["gscols"] = P.inc(vector.tensor_scalar_mul(us_c[:], wsu[:], rcol[:, 1:2]),
                                 "dv")

        if tap == "rx":
            @block.gpsimd
            def _(gpsimd):
                P.wait(gpsimd, "gpsimd", ev["gscols"], ev["qdone"],
                       wp["st"][len(wprep_tiles) - 1], wp["st"][len(wprep_tiles) - 2])
                P.inc(gpsimd.dma_start(out=out[0:128, 0:22], in_=gs_c[:, :]), "so", 16)
                gpsimd.wait_ge(P.sem["so"], 16)
            return nc

        ev_qx = [None] * DCH
        for c in range(DCH):
            @block.vector
            def _(vector, c=c):
                P.wait(vector, "vector", ev["gscols"], ev.get(f"qxp_{c - 2}"))
                ev[f"qxv_{c}"] = P.inc(vector.tensor_scalar(
                    xsl[0][:, (c % 2) * 2048:(c % 2) * 2048 + 2048],
                    xn_qx[:, c * T:(c + 1) * T], rcol[:, 0:1], MAGIC,
                    ALU.mult, ALU.add), "dv")

            @block.vector
            def _(vector, c=c):
                vector.tensor_scalar(
                    xsl[0][:, (c % 2) * 2048:(c % 2) * 2048 + 2048],
                    xsl[0][:, (c % 2) * 2048:(c % 2) * 2048 + 2048],
                    MAGIC, QMAX, ALU.subtract, ALU.min)
                e = P.inc(vector.tensor_scalar_max(
                    xn_qx[:, c * T:(c + 1) * T],
                    xsl[0][:, (c % 2) * 2048:(c % 2) * 2048 + 2048],
                    -(QMAX + 1.0)), "dv")
                ev[f"qxp_{c}"] = e
                ev_qx[c] = e

        if tap == "qx":
            @block.gpsimd
            def _(gpsimd):
                P.wait(gpsimd, "gpsimd", *[ev_qx[c] for c in range(DCH)])
                P.inc(gpsimd.dma_start(
                    out=out.rearrange("(c p) n -> p c n", p=128), in_=xn_qx[:].rearrange("p (c n) -> p c n", c=DCH)),
                    "so", 16)
                gpsimd.wait_ge(P.sem["so"], 16)
            return nc

        cm_wq.__exit__(None, None, None)
        cm_vtm1.__exit__(None, None, None)
        cm_vtm0.__exit__(None, None, None)
        cm_ktm1.__exit__(None, None, None)
        cm_ktm0.__exit__(None, None, None)
        cm_xsl0.__exit__(None, None, None)
        cm_wf1.__exit__(None, None, None)
        cm_wa0.__exit__(None, None, None)

        cm_qgT0, qgT0 = alloc("qgT0", [128, 4096], BF16)
        cm_quT0, quT0 = alloc("quT0", [128, 4096], BF16)
        cm_hst0, hst0 = alloc("hst0", [128, T], BF16)
        cm_hst1, hst1 = alloc("hst1", [128, T], BF16)
        hstage = [hst0, hst1]
        cm_qgT1, qgT1 = alloc("qgT1", [128, 4096], BF16)
        cm_quT1, quT1 = alloc("quT1", [128, 4096], BF16)
        qgT = [qgT0, qgT1]
        quT = [quT0, quT1]

        GROUPS = [(0, 1), (1, 1), (2, 2), (4, 4), (8, 4), (12, 4), (16, 4), (20, 2)]
        ev_gld = [None] * len(GROUPS)
        ev_gu_use = [None, None]

        def emit_gload(g):
            t0, ntile = GROUPS[g]
            par = g % 2

            @block.sync
            def _(sync, g=g, t0=t0, ntile=ntile, par=par):
                P.wait(sync, "sync", wp["st"].get(t0 + ntile - 1),
                       wp["st"].get(ITCH + t0 + ntile - 1),
                       ev["kmm_15"], ev_gu_use[par])
                if par == 1:
                    P.wait(sync, "sync", wp["st"].get(len(wprep_tiles) - 1),
                           ev_qx[DCH - 1], ev_xn[3])
                rows_ = ntile * 128
                e = None
                for c in range(DCH):
                    P.inc(sync.dma_start(
                        out=qgT[par][:, c * 512:c * 512 + rows_],
                        in_=qg_d[t0 * 128:t0 * 128 + rows_, c * 128:(c + 1) * 128],
                        transpose=True), f"s_gl{par}", 16)
                    e = P.inc(sync.dma_start(
                        out=quT[par][:, c * 512:c * 512 + rows_],
                        in_=qu_d[t0 * 128:t0 * 128 + rows_, c * 128:(c + 1) * 128],
                        transpose=True), f"s_gl{par}", 16)
                ev_gld[g] = e

        emit_gload(0)
        ev_h = {}
        for g, (t0, ntile) in enumerate(GROUPS):
            par = g % 2
            if g + 1 < len(GROUPS):
                emit_gload(g + 1)
            for tl in range(ntile):
                it = t0 + tl
                hpar = it % 2
                for nt in range(4):
                    bg = (nt % 2) * 2
                    bu = bg + 1

                    @block.tensor
                    def _(tensor, g=g, par=par, tl=tl, it=it, nt=nt, bg=bg, bu=bu,
                          ntile=ntile):
                        P.wait(tensor, "tensor", ev_gld[g], ev_qx[DCH - 1],
                               bank_free[bg], bank_free[bu])
                        for c in range(DCH):
                            tensor.matmul(
                                PB[bg][:, :],
                                qgT[par][:, c * 512 + tl * 128: c * 512 + (tl + 1) * 128],
                                xn_qx[:, c * T + nt * 512: c * T + (nt + 1) * 512],
                                start=(c == 0), stop=(c == DCH - 1))
                        e = None
                        for c in range(DCH):
                            e = P.inc(tensor.matmul(
                                PB[bu][:, :],
                                quT[par][:, c * 512 + tl * 128: c * 512 + (tl + 1) * 128],
                                xn_qx[:, c * T + nt * 512: c * T + (nt + 1) * 512],
                                start=(c == 0), stop=(c == DCH - 1)), "pe")
                        ev[f"gumm_{it}_{nt}"] = e
                        if tl == ntile - 1 and nt == 3:
                            ev_gu_use[par] = e

                    @block.scalar
                    def _(scalar, it=it, nt=nt, bg=bg):
                        P.wait(scalar, "scalar", ev[f"gumm_{it}_{nt}"], ev["gscols"],
                               ev.get(f"h_{it}_{nt - 2}") if nt >= 2
                               else ev.get(f"h_{it - 1}_{nt + 2}"))
                        ev[f"sg_{it}_{nt}"] = P.inc(scalar.activation(
                            fst[nt % 2][:], PB[bg][:, :], AF.Silu,
                            scale=gs_c[:, it:it + 1]), "ac")

                    @block.vector
                    def _(vector, it=it, nt=nt, hpar=hpar, bg=bg, bu=bu):
                        P.wait(vector, "vector", ev[f"sg_{it}_{nt}"])
                        if it == 0 and nt == 0:
                            P.wait(vector, "vector",
                                   wp["st"].get(len(wprep_tiles) - 1))
                        if nt == 0:
                            P.wait(vector, "vector", ev_h.get(it - 2))
                        e = P.inc(vector.scalar_tensor_tensor(
                            hstage[hpar][:, nt * 512:(nt + 1) * 512],
                            PB[bu][:, :], us_c[:, it:it + 1], fst[nt % 2][:],
                            ALU.mult, ALU.mult), "dv")
                        ev[f"h_{it}_{nt}"] = e
                        bank_free[bg] = e
                        bank_free[bu] = e

                @block.gpsimd
                def _(gpsimd, it=it, hpar=hpar):
                    P.wait(gpsimd, "gpsimd", ev[f"h_{it}_3"])
                    if it > 0:
                        P.wait(gpsimd, "gpsimd", ev[f"hmxa_{it - 1}"])
                    ev[f"hmx_{it}"] = P.inc(gpsimd.tensor_reduce(
                        hrun[0:1, 1:2], hstage[hpar][:, :],
                        AX.XYZWC, ALU.abs_max), "gp")

                @block.vector
                def _(vector, it=it):
                    P.wait(vector, "vector", ev[f"hmx_{it}"])
                    ev[f"hmxa_{it}"] = P.inc(vector.tensor_tensor(
                        hrun[0:1, 0:1], hrun[0:1, 0:1], hrun[0:1, 1:2], ALU.max), "dv")

                @block.sync
                def _(sync, it=it, hpar=hpar):
                    P.wait(sync, "sync", ev[f"hmx_{it}"])
                    ev_h[it] = P.inc(sync.dma_start(
                        out=h_d[it * 128:(it + 1) * 128, :], in_=hstage[hpar][:, :]),
                        f"s_h{it % 2}", 16)

        @block.gpsimd
        def _(gpsimd):
            P.wait(gpsimd, "gpsimd", ev[f"hmxa_{ITCH - 1}"])
            eC = P.inc(gpsimd.tensor_copy(mxs2[0:1, 0:1], hrun[0:1, 0:1]), "gp")
            gpsimd.wait_ge(P.sem["gp"], eC[1])
            P.inc(gpsimd.dma_start(out=mx2_in.ap(), in_=mxs2[0:1, :]), "s_mx", 16)
            P.wait(gpsimd, "gpsimd", P.now("s_mx"))
            P.inc(gpsimd.collective_compute(
                "AllReduce", ALU.max, replica_groups=RG,
                ins=[mx2_in.ap().opt()], outs=[mx2_out.ap().opt()]), "cc")
            P.wait(gpsimd, "gpsimd", P.now("cc"))
            ev["mxg2"] = P.inc(gpsimd.dma_start(out=mxg2[:], in_=mx2_out.ap()), "s_mx", 16)

        if tap == "gup":
            @block.gpsimd
            def _(gpsimd):
                P.wait(gpsimd, "gpsimd", ev_h[ITCH - 1], ev_h[ITCH - 2], ev["mxg2"])
                P.inc(gpsimd.dma_start(out=out[0:128, 0:8], in_=hrun[:, 0:8]), "so", 16)
                gpsimd.wait_ge(P.sem["so"], 16)
            return nc

        cm_quT1.__exit__(None, None, None)
        cm_qgT1.__exit__(None, None, None)
        cm_hst1.__exit__(None, None, None)
        cm_hst0.__exit__(None, None, None)
        cm_quT0.__exit__(None, None, None)
        cm_qgT0.__exit__(None, None, None)
        cm_qdT, qdT = alloc("qdT", [128, ITCH * 1024], BF16)
        cm_rtm, rec_tm = alloc("rec_tm", [128, 64], F32)
        cm_rtm2, rec_tm2 = alloc("rec_tm2", [128, 64], F32)
        cm_recT, recT = alloc("recT", [16, T], BF16)
        cm_xo0, xo0 = alloc("xo0", [128, T], F32)
        xoch = [xo0, xo0]

        @block.sync
        def _(sync):
            P.wait(sync, "sync", wp["stn"].get("d"), ev_gu_use[0], ev_gu_use[1],
                   ev_h.get(ITCH - 1), ev_h.get(ITCH - 2))
            e = None
            for ic in range(ITCH):
                e = P.inc(sync.dma_start(out=qdT[:, ic * 1024:(ic + 1) * 1024],
                                         in_=qd_d[:, ic * 128:(ic + 1) * 128],
                                         transpose=True), "s_qdT", 16)
            ev["qdT"] = e

        @block.gpsimd
        def _(gpsimd):
            P.wait(gpsimd, "gpsimd", ev["kvzdone"])
            ev["wo"] = P.inc(gpsimd.dma_start(
                out=wa[1][:].rearrange("p (c n) -> p c n", c=DCH), in_=woT.rearrange("(c p) n -> p c n", p=128)), "s_wb", 16)

        @block.scalar
        def _(scalar):
            P.wait(scalar, "scalar", ev_gu_use[0], ev_gu_use[1])
            ev["xo0"] = P.inc(scalar.dma_start(out=xoch[0][:], in_=xfm[0:128, :]),
                              "s_xo0", 16)
            ev["xo1"] = ev["xo0"]
        ev_xoch = [ev["xo0"]] + [None] * (DCH - 1)

        if tap == "preattn":
            @block.gpsimd
            def _(gpsimd):
                P.wait(gpsimd, "gpsimd", ev["qdT"], ev["wo"], ev["xo1"], ev["kvzbf2"])
                P.inc(gpsimd.dma_start(out=out[0:128, 0:1024],
                                       in_=qdT[:, 0:1024]), "so", 16)
                gpsimd.wait_ge(P.sem["so"], 16)
            return nc

        for nt in range(4):
            @block.tensor
            def _(tensor, nt=nt):
                P.wait(tensor, "tensor", ev["kvzbf2"], ev["qdone"], bank_free[4])
                P.wait(tensor, "tensor", ev["xo0"],
                       ev["xo1"], ev_h[ITCH - 1], ev_h[ITCH - 2])
                e = None
                for sub in range(4):
                    for h in range(NH):
                        e = P.inc(tensor.matmul(
                            PB[4][:, sub * 16 + h:sub * 16 + h + 1],
                            q_a[(h % 2) * 64:(h % 2) * 64 + 64,
                                (h // 2) * T + nt * 512 + sub * 128:
                                (h // 2) * T + nt * 512 + (sub + 1) * 128],
                            kvz_bf[(h % 2) * 64:(h % 2) * 64 + 64,
                                   h * 66 + 64:h * 66 + 65],
                            start=True, stop=True), "pe")
                ev[f"znorm_{nt}"] = e

            if tap is not None and tap.startswith("znorm") and nt == 0:
                @block.vector
                def _(vector):
                    P.wait(vector, "vector", ev["znorm_0"])
                    P.inc(vector.tensor_copy(fst[0][:, 0:64], PB[4][:, 0:64]), "dv")
                    ev["zdump"] = P.now("dv")

                @block.gpsimd
                def _(gpsimd):
                    P.wait(gpsimd, "gpsimd", ev["zdump"])
                    P.inc(gpsimd.dma_start(out=out[0:128, 0:512], in_=fst[0][:]),
                          "so", 16)
                    gpsimd.wait_ge(P.sem["so"], 16)
                return nc

            @block.vector
            def _(vector, nt=nt):
                P.wait(vector, "vector", ev[f"znorm_{nt}"], ev_h[ITCH - 1],
                       ev_h[ITCH - 2], ev_gu_use[0], ev_gu_use[1])
                vector.tensor_scalar_add(rec_tm[:, :], PB[4][:, 0:64], 1e-6)
                e = P.inc(vector.reciprocal(rec_tm2[:, :], rec_tm[:, :]), "dv")
                ev[f"rec_{nt}"] = e
                bank_free[4] = e

            for sub in range(4):
                @block.tensor
                def _(tensor, nt=nt, sub=sub):
                    P.wait(tensor, "tensor", ev[f"rec_{nt}"], bank_free[5])
                    ev[f"recT_{nt}_{sub}"] = P.inc(tensor.transpose(
                        PB[5][0:16, 0:128],
                        rec_tm2[:, sub * 16:(sub + 1) * 16], ident[:]), "pe")

                @block.vector
                def _(vector, nt=nt, sub=sub):
                    P.wait(vector, "vector", ev[f"recT_{nt}_{sub}"])
                    e = P.inc(vector.tensor_copy(
                        recT[:, nt * 512 + sub * 128:nt * 512 + (sub + 1) * 128],
                        PB[5][0:16, 0:128]), "dv")
                    bank_free[5] = e
            ev[f"recTe_{nt}"] = P.now("dv")

            if tap == "rec" and nt == 3:
                @block.gpsimd
                def _(gpsimd):
                    P.wait(gpsimd, "gpsimd", ev["recTe_3"])
                    P.inc(gpsimd.dma_start(out=out[0:16, 0:2048], in_=recT[:, :]),
                          "so", 16)
                    gpsimd.wait_ge(P.sem["so"], 16)
                return nc

            for e2 in range(DCH):
                pb = 6 + e2 % 2
                pa = e2 % 2

                @block.tensor
                def _(tensor, nt=nt, e2=e2, pb=pb):
                    P.wait(tensor, "tensor", ev[f"recTe_{nt}"], bank_free[pb])
                    ev[f"rbc_{nt}_{e2}"] = P.inc(tensor.matmul(
                        PB[pb][:, :], pairs[:, e2 * 128:(e2 + 1) * 128],
                        recT[:, nt * 512:(nt + 1) * 512], start=True, stop=True), "pe")

                @block.vector
                def _(vector, nt=nt, e2=e2, pb=pb):
                    P.wait(vector, "vector", ev[f"rbc_{nt}_{e2}"])
                    e = P.inc(vector.tensor_tensor(
                        q_a[:, e2 * T + nt * 512:e2 * T + (nt + 1) * 512],
                        q_a[:, e2 * T + nt * 512:e2 * T + (nt + 1) * 512],
                        PB[pb][:, :], ALU.mult), "dv")
                    ev[f"qp_{nt}_{e2}"] = e
                    bank_free[pb] = e

                @block.tensor
                def _(tensor, nt=nt, e2=e2, pa=pa):
                    P.wait(tensor, "tensor", ev[f"qp_{nt}_{e2}"], bank_free[pa])
                    e = None
                    for hh in range(2):
                        h = 2 * e2 + hh
                        e = P.inc(tensor.matmul(
                            PB[pa][hh * 64:(hh + 1) * 64, :],
                            kvz_bf[hh * 64:hh * 64 + 64, h * 66:h * 66 + 64],
                            q_a[hh * 64:(hh + 1) * 64,
                                e2 * T + nt * 512:e2 * T + (nt + 1) * 512],
                            start=True, stop=True), "pe")
                    ev[f"amm_{nt}_{e2}"] = e

                @block.vector
                def _(vector, nt=nt, e2=e2, pa=pa):
                    P.wait(vector, "vector", ev[f"amm_{nt}_{e2}"])
                    e = P.inc(vector.tensor_copy(
                        q_a[:, e2 * T + nt * 512:e2 * T + (nt + 1) * 512],
                        PB[pa][:, :]), "dv")
                    ev[f"ae_{nt}_{e2}"] = e
                    bank_free[pa] = e
        ev["adone"] = ev["ae_3_7"]

        if tap == "aphase":
            @block.gpsimd
            def _(gpsimd):
                P.wait(gpsimd, "gpsimd", ev["adone"])
                P.inc(gpsimd.dma_start(
                    out=out.rearrange("(c p) n -> p c n", p=128),
                    in_=q_a[:].rearrange("p (c n) -> p c n", c=DCH)), "so", 16)
                gpsimd.wait_ge(P.sem["so"], 16)
            return nc

        for oc in range(DCH):
            for nt in range(4):
                pb = (oc * 4 + nt) % 4

                @block.tensor
                def _(tensor, oc=oc, nt=nt, pb=pb):
                    P.wait(tensor, "tensor", ev["wo"], ev["adone"], bank_free[pb])
                    e = None
                    for c in range(DCH):
                        e = P.inc(tensor.matmul(
                            PB[pb][:, :],
                            wa[1][:, c * 1024 + oc * 128: c * 1024 + (oc + 1) * 128],
                            q_a[:, c * T + nt * 512: c * T + (nt + 1) * 512],
                            start=(c == 0), stop=(c == DCH - 1)), "pe")
                    ev[f"womm_{oc}_{nt}"] = e

                @block.scalar
                def _(scalar, oc=oc, nt=nt, pb=pb):
                    key = (f"attnx_{oc}_{nt - 2}" if nt >= 2
                           else f"attnx_{oc - 1}_{nt + 2}")
                    P.wait(scalar, "scalar", ev[f"womm_{oc}_{nt}"], ev.get(key))
                    ev[f"woe_{oc}_{nt}"] = P.inc(scalar.activation(
                        fst[2 + nt % 2][:], PB[pb][:, :], AF.Identity,
                        bias=boc[:, oc:oc + 1]), "ac")

                @block.vector
                def _(vector, oc=oc, nt=nt, pb=pb):
                    P.wait(vector, "vector", ev[f"woe_{oc}_{nt}"], ev_xoch[oc])
                    e = P.inc(vector.tensor_tensor(
                        xn_qx[:, oc * T + nt * 512: oc * T + (nt + 1) * 512],
                        fst[2 + nt % 2][:], xoch[0][:, nt * 512:(nt + 1) * 512],
                        ALU.add), "dv")
                    ev[f"attnx_{oc}_{nt}"] = e
                    bank_free[pb] = e

            if oc + 1 < DCH:
                @block.scalar
                def _(scalar, oc=oc):
                    P.wait(scalar, "scalar", ev[f"attnx_{oc}_3"])
                    ev_xoch[oc + 1] = P.inc(scalar.dma_start(
                        out=xoch[0][:],
                        in_=xfm[(oc + 1) * 128:(oc + 2) * 128, :]), "s_xo0", 16)
        ev["attnx"] = ev["attnx_7_3"]

        if tap == "attn":
            @block.gpsimd
            def _(gpsimd):
                P.wait(gpsimd, "gpsimd", ev["attnx"])
                P.inc(gpsimd.dma_start(
                    out=out.rearrange("(c p) n -> p c n", p=128), in_=xn_qx[:].rearrange("p (c n) -> p c n", c=DCH)),
                    "so", 16)
                gpsimd.wait_ge(P.sem["so"], 16)
            return nc

        @block.vector
        def _(vector):
            P.wait(vector, "vector", ev["mxg2"])
            e1 = P.inc(vector.tensor_tensor(srow[0:1, 10:11], srow[0:1, 9:10],
                                            mxg2[0:1, 0:1], ALU.mult), "dv")
            vector.wait_ge(P.sem["dv"], e1[1])
            e2 = P.inc(vector.tensor_scalar_mul(srow[0:1, 11:12], srow[0:1, 10:11],
                                                1.0 / QMAX), "dv")
            vector.wait_ge(P.sem["dv"], e2[1])
            ev["rh"] = P.inc(vector.reciprocal(srow[0:1, 12:13],
                                               srow[0:1, 11:12]), "dv")

        @block.tensor
        def _(tensor):
            P.wait(tensor, "tensor", ev["rh"], bank_free[4])
            tensor.matmul(PB[4][:, 0:1], ones_r[:], srow[0:1, 12:13], start=True,
                          stop=True)
            ev["rhbc"] = P.inc(tensor.matmul(PB[4][:, 1:2], ones_r[:],
                                             srow[0:1, 11:12], start=True, stop=True),
                               "pe")

        @block.vector
        def _(vector):
            P.wait(vector, "vector", ev["rhbc"])
            vector.tensor_copy(rcol[:, 1:2], PB[4][:, 0:1])
            e0 = P.inc(vector.tensor_copy(shcol[:, 0:1], PB[4][:, 1:2]), "dv")
            vector.wait_ge(P.sem["dv"], e0[1])
            e = P.inc(vector.tensor_scalar_mul(ds_c[:], wsd[:], shcol[:, 0:1]), "dv")
            ev["dscols"] = e
            bank_free[4] = e

        cm_xo0.__exit__(None, None, None)
        cm_recT.__exit__(None, None, None)
        cm_rtm2.__exit__(None, None, None)
        cm_rtm.__exit__(None, None, None)
        cm_qh0, qh0 = alloc("qh0", [128, ITCH * 512], BF16)
        qhb = [qh0, q_a]

        hv = h_d.rearrange("(c p) n -> p c n", p=128)
        ev_out = {}
        for nt in range(4):
            qh = qhb[nt % 2]

            @block.scalar
            def _(scalar, nt=nt, qh=qh):
                P.wait(scalar, "scalar", ev_h[ITCH - 1], ev_h[ITCH - 2])
                if nt < 2:
                    P.wait(scalar, "scalar", ev["attnx"])
                else:
                    P.wait(scalar, "scalar", ev[f"dnmm_{nt - 2}_7"])
                ev[f"qhld_{nt}"] = P.inc(scalar.dma_start(
                    out=qh[:, 0:ITCH * 512],
                    in_=hv[:, :, nt * 512:(nt + 1) * 512]), "s_qh", 16)

            for pc in range(4):
                csl = slice(pc * 2816, (pc + 1) * 2816)

                @block.vector
                def _(vector, nt=nt, pc=pc, csl=csl, qh=qh):
                    P.wait(vector, "vector", ev[f"qhld_{nt}"], ev["dscols"],
                           ev.get(f"qhq_{nt}_{pc - 1}")
                           if pc > 0 else ev.get(f"qhq_{nt - 1}_3"))
                    ev[f"qhv_{nt}_{pc}"] = P.inc(vector.tensor_scalar(
                        wf320[:, :], qh[:, csl], rcol[:, 1:2], MAGIC,
                        ALU.mult, ALU.add), "dv")

                @block.vector
                def _(vector, nt=nt, pc=pc, csl=csl, qh=qh):
                    vector.tensor_scalar(wf320[:, :], wf320[:, :],
                                         MAGIC, QMAX, ALU.subtract, ALU.min)
                    ev[f"qhq_{nt}_{pc}"] = P.inc(vector.tensor_scalar_max(
                        qh[:, csl], wf320[:, :], -(QMAX + 1.0)), "dv")

            for hc in range(DCH):
                pb = hc % 4

                @block.tensor
                def _(tensor, nt=nt, hc=hc, pb=pb, qh=qh):
                    P.wait(tensor, "tensor", ev["qdT"], bank_free[pb],
                           *[ev[f"qhq_{nt}_{pc}"] for pc in range(4)])
                    e = None
                    for ic in range(ITCH):
                        e = P.inc(tensor.matmul(
                            PB[pb][:, :],
                            qdT[:, ic * 1024 + hc * 128: ic * 1024 + (hc + 1) * 128],
                            qh[:, ic * 512:(ic + 1) * 512],
                            start=(ic == 0), stop=(ic == ITCH - 1)), "pe")
                    ev[f"dnmm_{nt}_{hc}"] = e

                @block.vector
                def _(vector, nt=nt, hc=hc, pb=pb):
                    P.wait(vector, "vector", ev[f"dnmm_{nt}_{hc}"], ev["attnx"],
                           ev_out.get((nt, hc - 2)) if hc >= 2
                           else ev_out.get((nt - 1, hc + 6)))
                    e = P.inc(vector.scalar_tensor_tensor(
                        fst[hc % 2][:], PB[pb][:, :], ds_c[:, hc:hc + 1],
                        xn_qx[:, hc * T + nt * 512: hc * T + (nt + 1) * 512],
                        ALU.mult, ALU.add), "dv")
                    ev[f"oute_{nt}_{hc}"] = e
                    bank_free[pb] = e

                @block.sync
                def _(sync, nt=nt, hc=hc):
                    P.wait(sync, "sync", ev[f"oute_{nt}_{hc}"])
                    ev_out[(nt, hc)] = P.inc(sync.dma_start(
                        out=out[hc * 128:(hc + 1) * 128, nt * 512:(nt + 1) * 512],
                        in_=fst[hc % 2][:]), f"s_so{hc % 2}", 16)

        @block.sync
        def _(sync):
            sync.wait_ge(P.sem["s_so0"], P.cnt["s_so0"])
            sync.wait_ge(P.sem["s_so1"], P.cnt["s_so1"])

    return nc


_PAIRSEL = None


def _pairsel_const():
    global _PAIRSEL
    if _PAIRSEL is None:
        ps = np.zeros((16, 1024), dtype=np.float32)
        for e in range(8):
            for p in range(128):
                ps[2 * e + (1 if p >= 64 else 0), e * 128 + p] = 1.0
        _PAIRSEL = ps.astype(ml_dtypes.bfloat16)
    return _PAIRSEL


def kernel(L0_attn_x, L0_attn_Wq, L0_attn_bq, L0_attn_Wk, L0_attn_bk,
           L0_attn_Wv, L0_attn_bv, L0_attn_Wo, L0_attn_bo,
           L0_ffn_clip_x, L0_ffn_clip_down, L0_ffn_w_gate, L0_ffn_w_up,
           L0_ffn_w_down, _tap=None):
    f = np.float32
    x = np.asarray(L0_attn_x, dtype=f)
    shared = dict(
        wqT=np.ascontiguousarray(np.asarray(L0_attn_Wq, f).T),
        wkT=np.ascontiguousarray(np.asarray(L0_attn_Wk, f).T),
        wvT=np.ascontiguousarray(np.asarray(L0_attn_Wv, f).T),
        woT=np.ascontiguousarray(np.asarray(L0_attn_Wo, f).T),
        wg=np.ascontiguousarray(np.asarray(L0_ffn_w_gate, f)),
        wu=np.ascontiguousarray(np.asarray(L0_ffn_w_up, f)),
        wd=np.ascontiguousarray(np.asarray(L0_ffn_w_down, f)),
        bq_c=np.ascontiguousarray(np.asarray(L0_attn_bq, f).reshape(8, 128).T),
        bo_c=np.ascontiguousarray(np.asarray(L0_attn_bo, f).reshape(8, 128).T),
        bk_r=np.ascontiguousarray(np.asarray(L0_attn_bk, f).reshape(1, D)),
        bv_r=np.ascontiguousarray(np.asarray(L0_attn_bv, f).reshape(1, D)),
        clip2=np.array([[float(np.asarray(L0_ffn_clip_x).reshape(-1)[0]),
                         float(np.asarray(L0_ffn_clip_down).reshape(-1)[0])]],
                       dtype=f),
        identin=np.eye(128, dtype=f),
        pairselin=_pairsel_const(),
    )
    in_maps = []
    for c in range(NCORES):
        b, s0 = c // 4, (c % 4) * T
        m = dict(shared)
        m["xfm"] = np.ascontiguousarray(x[b, s0:s0 + T, :].T)
        bs = np.zeros((128, 2), dtype=f)
        bs[:, b] = 1.0
        m["bselin"] = bs
        in_maps.append(m)

    nc = build_program(tap=_tap)
    res = run_bass_kernel_spmd(nc, in_maps, core_ids=list(range(NCORES)))

    outp = np.empty((2, 8192, D), dtype=f)
    for c in range(NCORES):
        b, s0 = c // 4, (c % 4) * T
        outp[b, s0:s0 + T, :] = np.asarray(res.results[c]["out"]).T
    return outp


# version 25
# speedup vs baseline: 1.1060x; 1.1060x over previous
from contextlib import ExitStack

import numpy as np
import ml_dtypes

import concourse.bass as bass
import concourse.mybir as mybir
from concourse.bass_utils import run_bass_kernel_spmd

F32 = mybir.dt.float32
BF16 = mybir.dt.bfloat16
AF = mybir.ActivationFunctionType
ALU = mybir.AluOpType
AX = mybir.AxisListType

MAGIC = 12582912.0
NCORES = 8
T = 2048
D = 1024
DCH = 8
I = 2816
ITCH = 22
NH = 16
QMAX = 127.0


class Prog:
    ENG_OF = {"pe": "tensor", "dv": "vector", "ac": "scalar", "gp": "gpsimd"}

    def __init__(self, nc, ctx):
        self.sem = {}
        self.cnt = {}
        self.last_wait = {}
        for name in ["pe", "dv", "ac", "gp", "cc",
                     "s_const", "wpl0", "wpl1", "s_qg", "s_qu", "s_qd",
                     "s_wa", "s_wb", "s_x0", "s_x1", "s_mx", "s_kvz",
                     "s_gl0", "s_gl1", "s_qdT", "s_qh", "s_h0", "s_h1",
                     "s_xo0", "s_xo1", "s_so0", "s_so1", "so"]:
            self.sem[name] = ctx.enter_context(nc.semaphore(name))
            self.cnt[name] = 0

    def inc(self, instr, s, n=1):
        self.cnt[s] += n
        instr.then_inc(self.sem[s], n)
        return (s, self.cnt[s])

    def now(self, s):
        return (s, self.cnt[s])

    def wait(self, eng, engname, *evs):
        for ev in evs:
            if ev is None:
                continue
            s, v = ev
            if v <= 0 or self.ENG_OF.get(s) == engname:
                continue
            key = (engname, s)
            if self.last_wait.get(key, -1) >= v:
                continue
            self.last_wait[key] = v
            eng.wait_ge(self.sem[s], v)


def build_program(tap=None):
    nc = bass.Bass("TRN2", target_bir_lowering=False,
                   dynamic_dma_scratch_size=24576,
                   detect_race_conditions=False)
    dt = nc.dram_tensor

    xfm = dt("xfm", [D, T], F32, kind="ExternalInput").ap()
    wqT = dt("wqT", [D, D], F32, kind="ExternalInput").ap()
    wkT = dt("wkT", [D, D], F32, kind="ExternalInput").ap()
    wvT = dt("wvT", [D, D], F32, kind="ExternalInput").ap()
    woT = dt("woT", [D, D], F32, kind="ExternalInput").ap()
    wg = dt("wg", [I, D], F32, kind="ExternalInput").ap()
    wu = dt("wu", [I, D], F32, kind="ExternalInput").ap()
    wd = dt("wd", [D, I], F32, kind="ExternalInput").ap()
    bq_c = dt("bq_c", [128, DCH], F32, kind="ExternalInput").ap()
    bo_c = dt("bo_c", [128, DCH], F32, kind="ExternalInput").ap()
    bk_r = dt("bk_r", [1, D], F32, kind="ExternalInput").ap()
    bv_r = dt("bv_r", [1, D], F32, kind="ExternalInput").ap()
    clip2 = dt("clip2", [1, 2], F32, kind="ExternalInput").ap()
    bselin = dt("bselin", [128, 2], F32, kind="ExternalInput").ap()
    identin = dt("identin", [128, 128], F32, kind="ExternalInput").ap()
    pairselin = dt("pairselin", [16, 1024], BF16, kind="ExternalInput").ap()
    out = dt("out", [D, T], F32, kind="ExternalOutput").ap()

    qg_d = dt("qg_d", [I, D], BF16)
    qu_d = dt("qu_d", [I, D], BF16)
    qd_d = dt("qd_d", [D, I], BF16)
    h_d = dt("h_d", [I, T], BF16)
    kvz_in = dt("kvz_in", [128, 1056], F32)
    kvz_out = dt("kvz_out", [128, 1056], F32)
    mx1_in = dt("mx1_in", [1, 8], F32)
    mx1_out = dt("mx1_out", [1, 8], F32)
    mx2_in = dt("mx2_in", [1, 8], F32)
    mx2_out = dt("mx2_out", [1, 8], F32)
    kvzbf_d = dt("kvzbf_d", [64, 1056], BF16)

    ctx = ExitStack()
    with ctx:
        def sb(name, shape, dty):
            return ctx.enter_context(nc.sbuf_tensor(name, shape, dty))

        def alloc(name, shape, dty):
            cm = nc.sbuf_tensor(name, shape, dty, side="right")
            return cm, cm.__enter__()

        block = ctx.enter_context(nc.Block())
        P = Prog(nc, ctx)
        RG = [list(range(NCORES))]
        ev = {}

        xn_qx = sb("xn_qx", [128, DCH * T], BF16)
        bk_bc = sb("bk_bc", [128, D], F32)
        bv_bc = sb("bv_bc", [128, D], F32)
        bqc = sb("bqc", [128, DCH], F32)
        boc = sb("boc", [128, DCH], F32)
        clip_s = sb("clip_s", [1, 2], F32)
        bsel = sb("bsel", [128, 2], F32)
        ident = sb("ident", [128, 128], F32)
        ones_b = sb("ones_b", [128, 1], BF16)
        ones_c = sb("ones_c", [128, 1], F32)
        ones_r = sb("ones_r", [1, 128], F32)
        wsg = sb("wsg", [128, ITCH], F32)
        wsu = sb("wsu", [128, ITCH], F32)
        wsd = sb("wsd", [128, DCH], F32)
        gs_c = sb("gs_c", [128, ITCH], F32)
        us_c = sb("us_c", [128, ITCH], F32)
        ds_c = sb("ds_c", [128, DCH], F32)
        rcol = sb("rcol", [128, 2], F32)
        shcol = sb("shcol", [128, 1], F32)
        wrec = sb("wrec", [128, 16], F32)
        negm = sb("negm", [128, 1], F32)
        rows = sb("rows", [1, 1536], F32)
        srow = sb("srow", [1, 16], F32)
        mxs = sb("mxs", [128, 8], F32)
        mxg = sb("mxg", [1, 8], F32)
        mxs2 = sb("mxs2", [128, 8], F32)
        mxg2 = sb("mxg2", [1, 8], F32)
        hrun = sb("hrun", [128, 8], F32)
        fst = [sb(f"fst{i}", [128, 512], F32) for i in range(4)]
        bst = [sb(f"bst{i}", [128, 512], BF16) for i in range(4)]

        PB = [ctx.enter_context(nc.psum_tensor(f"pb{i}", [128, 512], F32))
              for i in range(8)]
        bank_free = [None] * 8

        q_a = sb("q_a", [128, DCH * T], BF16)
        pairs = sb("pairs", [16, 1024], BF16)
        kvz_sb = sb("kvz_sb", [64, 1056], F32)
        kvz_m = sb("kvz_m", [64, 1056], F32)
        kvz_bf = sb("kvz_bf", [128, 1056], BF16)
        wf320 = sb("wf320", [128, I], F32)
        cm_wa1, wa1 = alloc("wa1", [128, DCH * 1024], BF16)
        cm_wa0, wa0 = alloc("wa0", [128, DCH * 1024], BF16)
        wa = [wa0, wa1]
        cm_wf1, wf321 = alloc("wf321", [128, I], F32)
        wf32 = [wf320, wf321]
        cm_xsl0, xsl0 = alloc("xsl0", [128, DCH * 512], F32)
        cm_ktm0, ktm0 = alloc("ktm0", [128, D], BF16)
        cm_ktm1, ktm1 = alloc("ktm1", [128, D], BF16)
        ktm = [ktm0, ktm1]
        cm_vtm0, vtm0 = alloc("vtm0", [128, D], BF16)
        cm_vtm1, vtm1 = alloc("vtm1", [128, D], BF16)
        vtm = [vtm0, vtm1]
        cm_xsl1, xsl1 = alloc("xsl1", [128, DCH * 512], F32)
        xsl = [xsl0, xsl1]

        bkrow = kvz_sb[0:1, 0:D]
        bvrow = kvz_m[0:1, 0:D]

        @block.sync
        def _(sync):
            e = None
            for d_, s_ in [(bqc, bq_c), (boc, bo_c),
                           (clip_s, clip2), (bsel, bselin), (ident, identin),
                           (pairs, pairselin)]:
                e = P.inc(sync.dma_start(out=d_[:], in_=s_), "s_const", 16)
            ev["const"] = e

        @block.gpsimd
        def _(gpsimd):
            ev["wk"] = P.inc(gpsimd.dma_start(
                out=wa[0][:].rearrange("p (c n) -> p c n", c=DCH), in_=wkT.rearrange("(c p) n -> p c n", p=128)), "s_wa", 16)
            P.wait(gpsimd, "gpsimd", ev["wk"])
            ev["wv"] = P.inc(gpsimd.dma_start(
                out=wa[1][:].rearrange("p (c n) -> p c n", c=DCH), in_=wvT.rearrange("(c p) n -> p c n", p=128)), "s_wb", 16)

        @block.vector
        def _(vector):
            vector.memset(ones_b[:], 1.0)
            vector.memset(ones_c[:], 1.0)
            vector.memset(ones_r[:], 1.0)
            vector.memset(negm[:], -MAGIC)
            vector.memset(hrun[:], 0.0)
            vector.memset(mxs[:], 0.0)
            vector.memset(mxs2[:], 0.0)
            P.wait(vector, "vector", ev["const"])
            ev["memset"] = P.inc(vector.memset(kvz_sb[:], 0.0), "dv")

        @block.sync
        def _(sync):
            P.wait(sync, "sync", ev["memset"])
            P.inc(sync.dma_start(out=bkrow, in_=bk_r), "s_const", 16)
            ev["brows"] = P.inc(sync.dma_start(out=bvrow, in_=bv_r), "s_const", 16)

        @block.scalar
        def _(scalar):
            P.wait(scalar, "scalar", ev["const"])
            scalar.activation(srow[0:1, 0:1], clip_s[0:1, 0:1], AF.Exp, scale=-1.0)
            ev["sige"] = P.inc(scalar.activation(srow[0:1, 8:9], clip_s[0:1, 1:2],
                                                 AF.Exp, scale=-1.0), "ac")

        @block.vector
        def _(vector):
            P.wait(vector, "vector", ev["sige"])
            vector.tensor_scalar_add(srow[0:1, 0:1], srow[0:1, 0:1], 1.0)
            e = P.inc(vector.tensor_scalar_add(srow[0:1, 8:9], srow[0:1, 8:9], 1.0),
                      "dv")
            vector.wait_ge(P.sem["dv"], e[1])
            vector.reciprocal(srow[0:1, 1:2], srow[0:1, 0:1])
            ev["sig"] = P.inc(vector.reciprocal(srow[0:1, 9:10], srow[0:1, 8:9]), "dv")

        for ih in range(2):
            @block.tensor
            def _(tensor, ih=ih):
                P.wait(tensor, "tensor", ev["const"], ev["memset"], ev["brows"])
                if ih == 1:
                    P.wait(tensor, "tensor", ev["bbce0"])
                sl5 = slice(ih * 512, (ih + 1) * 512)
                tensor.matmul(PB[6][:, :], ones_r[:], kvz_sb[0:1, sl5], start=True, stop=True)
                ev[f"bbc{ih}"] = P.inc(tensor.matmul(PB[7][:, :], ones_r[:],
                                                     kvz_m[0:1, sl5],
                                                     start=True, stop=True), "pe")

            @block.vector
            def _(vector, ih=ih):
                P.wait(vector, "vector", ev[f"bbc{ih}"])
                sl5 = slice(ih * 512, (ih + 1) * 512)
                vector.tensor_copy(bk_bc[:, sl5], PB[6][:, :])
                ev[f"bbce{ih}"] = P.inc(vector.tensor_copy(bv_bc[:, sl5], PB[7][:, :]), "dv")
        bank_free[6] = ev["bbce1"]
        bank_free[7] = ev["bbce1"]

        wprep_tiles = ([("g", t) for t in range(ITCH)] +
                       [("u", t) for t in range(ITCH)] +
                       [("d", t) for t in range(DCH)])
        wp = {"cur": 0, "ld": {}, "p1": {}, "st": {}, "stn": {}}
        wp_src = {"g": wg, "u": wu, "d": wd}
        wp_dst = {"g": qg_d, "u": qu_d, "d": qd_d}
        wp_cols = {"g": wsg, "u": wsu, "d": wsd}
        wp_fd = {"g": D, "u": D, "d": I}

        def wprep_emit(n):
            start = wp["cur"]
            end = min(start + n, len(wprep_tiles))
            if end == start:
                return
            wp["cur"] = end
            for k in range(start, end):
                w, t = wprep_tiles[k]
                par = k % 2
                fd = wp_fd[w]

                @block.sync
                def _(sync, k=k, w=w, t=t, par=par, fd=fd):
                    P.wait(sync, "sync", wp["st"].get(k - 2))
                    wp["ld"][k] = P.inc(sync.dma_start(
                        out=wf32[par][:, 0:fd],
                        in_=wp_src[w][t * 128:(t + 1) * 128, :]), f"wpl{k % 2}", 16)

                @block.vector
                def _(vector, k=k, w=w, t=t, par=par, fd=fd):
                    P.wait(vector, "vector", wp["ld"][k])
                    e1 = P.inc(vector.tensor_reduce(
                        wrec[:, 0:1], wf32[par][:, 0:fd],
                        AX.X, ALU.max, apply_absolute_value=True), "dv")
                    vector.wait_ge(P.sem["dv"], e1[1])
                    e2 = P.inc(vector.tensor_scalar_mul(
                        wp_cols[w][:, t:t + 1], wrec[:, 0:1], 1.0 / QMAX), "dv")
                    vector.wait_ge(P.sem["dv"], e2[1])
                    e3 = P.inc(vector.reciprocal(wrec[:, 8:9],
                                                 wp_cols[w][:, t:t + 1]), "dv")
                    vector.wait_ge(P.sem["dv"], e3[1])
                    vector.tensor_scalar(
                        wf32[par][:, 0:fd], wf32[par][:, 0:fd], wrec[:, 8:9], MAGIC,
                        ALU.mult, ALU.add)
                    wp["p1"][k] = P.inc(vector.tensor_scalar_sub(
                        wf32[par][:, 0:fd], wf32[par][:, 0:fd], MAGIC), "dv")

                @block.gpsimd
                def _(gpsimd, k=k, w=w, t=t, par=par, fd=fd):
                    P.wait(gpsimd, "gpsimd", wp["p1"][k])
                    wp["st"][k] = P.inc(gpsimd.dma_start(
                        out=wp_dst[w][t * 128:(t + 1) * 128, :],
                        in_=wf32[par][:, 0:fd]), {"g": "s_qg", "u": "s_qu",
                                                  "d": "s_qd"}[w], 16)
                    wp["stn"][w] = wp["st"][k]

        wprep_emit(2)

        xv = xfm.rearrange("(c p) n -> p c n", p=128)
        ev_xn = [None] * 4
        ev_xsl_use = [None, None]

        @block.scalar
        def _(scalar):
            for s_ in range(2):
                ev[f"xsl{s_}"] = P.inc(scalar.dma_start(
                    out=xsl[s_][:], in_=xv[:, :, s_ * 512:(s_ + 1) * 512]),
                    f"s_x{s_ % 2}", 16)

        for sl in range(4):
            par = sl % 2
            for c in range(DCH):
                @block.scalar
                def _(scalar, sl=sl, par=par, c=c):
                    P.wait(scalar, "scalar", ev[f"xsl{sl}"])
                    key = (f"sqmm_{sl}_{c - 2}" if c >= 2
                           else f"sqmm_{sl - 1}_{DCH + c - 2}")
                    P.wait(scalar, "scalar", ev.get(key))
                    ev[f"sq_{sl}_{c}"] = P.inc(scalar.activation(
                        fst[2 + c % 2][:], xsl[par][:, c * 512:(c + 1) * 512],
                        AF.Square), "ac")

                @block.tensor
                def _(tensor, sl=sl, par=par, c=c):
                    if c == 0:
                        P.wait(tensor, "tensor", bank_free[0 + par], bank_free[2 + par])
                        P.wait(tensor, "tensor", ev[f"xsl{sl}"])
                    ev[f"smm_{sl}_{c}"] = P.inc(
                        tensor.matmul(PB[0 + par][0:1, :], ones_c[:],
                                      xsl[par][:, c * 512:(c + 1) * 512],
                                      start=(c == 0), stop=(c == DCH - 1)), "pe")
                    P.wait(tensor, "tensor", ev[f"sq_{sl}_{c}"])
                    ev[f"sqmm_{sl}_{c}"] = P.inc(
                        tensor.matmul(PB[2 + par][0:1, :], ones_c[:], fst[2 + c % 2][:],
                                      start=(c == 0), stop=(c == DCH - 1)), "pe")
            ev[f"stats_{sl}"] = ev[f"sqmm_{sl}_{DCH - 1}"]

            rm = rows[0:1, 0:512]
            rv = rows[0:1, 512:1024]
            rr = rows[0:1, 1024:1536]

            @block.vector
            def _(vector, sl=sl, par=par):
                P.wait(vector, "vector", ev[f"stats_{sl}"])
                vector.tensor_scalar_mul(rm, PB[0 + par][0:1, :], 1.0 / D)
                vector.tensor_scalar_mul(rv, PB[2 + par][0:1, :], 1.0 / D)
                vector.tensor_tensor(rr, rm, rm, ALU.mult)
                ev[f"rows_v_{sl}"] = P.inc(vector.scalar_tensor_tensor(
                    rv, rv, 1e-5, rr, ALU.add, ALU.subtract), "dv")

            @block.scalar
            def _(scalar, sl=sl):
                P.wait(scalar, "scalar", ev[f"rows_v_{sl}"])
                scalar.activation(rr, rv, AF.Ln)
                ev[f"rows_ln_{sl}"] = P.inc(scalar.activation(rv, rr, AF.Exp, scale=-0.5),
                                            "ac")

            @block.vector
            def _(vector, sl=sl):
                P.wait(vector, "vector", ev[f"rows_ln_{sl}"])
                ev[f"rows_m_{sl}"] = P.inc(vector.tensor_tensor(rm, rm, rv, ALU.mult),
                                           "dv")

            @block.tensor
            def _(tensor, sl=sl, par=par):
                P.wait(tensor, "tensor", ev[f"rows_m_{sl}"],
                       bank_free[4 + par], bank_free[6 + par])
                tensor.matmul(PB[4 + par][:, :], ones_r[:], rv, start=True, stop=True)
                ev[f"bc_{sl}"] = P.inc(tensor.matmul(PB[6 + par][:, :], ones_r[:], rm,
                                                     start=True, stop=True), "pe")

            @block.vector
            def _(vector, sl=sl, par=par):
                P.wait(vector, "vector", ev[f"bc_{sl}"])
                e = None
                for c in range(DCH):
                    f = fst[c % 2]
                    vector.tensor_tensor(f[:], xsl[par][:, c * 512:(c + 1) * 512],
                                         PB[4 + par][:, :], ALU.mult)
                    e = P.inc(vector.tensor_tensor(
                        xn_qx[:, c * T + sl * 512: c * T + (sl + 1) * 512],
                        f[:], PB[6 + par][:, :], ALU.subtract), "dv")
                ev_xn[sl] = e
                ev_xsl_use[par] = e
                bank_free[4 + par] = e
                bank_free[6 + par] = e
                bank_free[0 + par] = ev[f"rows_v_{sl}"]
                bank_free[2 + par] = ev[f"rows_v_{sl}"]

            if sl + 2 < 4:
                @block.scalar
                def _(scalar, sl=sl, par=par):
                    P.wait(scalar, "scalar", ev_xsl_use[par])
                    ev[f"xsl{sl + 2}"] = P.inc(scalar.dma_start(
                        out=xsl[par][:], in_=xv[:, :, (sl + 2) * 512:(sl + 3) * 512]),
                        f"s_x{par}", 16)

            wprep_emit(3)

        cm_xsl1.__exit__(None, None, None)
        cm_wq, wq_sb = alloc("wq_sb", [128, DCH * 1024], BF16)

        @block.gpsimd
        def _(gpsimd):
            P.wait(gpsimd, "gpsimd", ev_xn[0], ev_xn[1], ev_xn[2], ev_xn[3])
            eR = P.inc(gpsimd.tensor_reduce(mxs[0:1, 0:1], xn_qx[:, :], AX.XYZWC,
                                            ALU.abs_max), "gp")
            gpsimd.wait_ge(P.sem["gp"], eR[1])
            P.inc(gpsimd.dma_start(out=mx1_in.ap(), in_=mxs[0:1, :]), "s_mx", 16)
            P.wait(gpsimd, "gpsimd", P.now("s_mx"))
            P.inc(gpsimd.collective_compute(
                "AllReduce", ALU.max, replica_groups=RG,
                ins=[mx1_in.ap().opt()], outs=[mx1_out.ap().opt()]), "cc")
            P.wait(gpsimd, "gpsimd", P.now("cc"))
            ev["mxg"] = P.inc(gpsimd.dma_start(out=mxg[:], in_=mx1_out.ap()), "s_mx", 16)

        if tap == "xn":
            @block.gpsimd
            def _(gpsimd):
                P.wait(gpsimd, "gpsimd", ev_xn[3])
                P.inc(gpsimd.dma_start(
                    out=out.rearrange("(c p) n -> p c n", p=128), in_=xn_qx[:].rearrange("p (c n) -> p c n", c=DCH)),
                    "so", 16)
                gpsimd.wait_ge(P.sem["so"], 16)
            return nc

        ev_ktm = [None, None]
        ev_vtm = [None, None]
        ev_kvzmm = [None, None]
        ev_ke = [None, None]

        def emit_kvz(nt):
            par = nt % 2

            @block.tensor
            def _(tensor, nt=nt, par=par):
                P.wait(tensor, "tensor", ev_ktm[par], ev_vtm[par])
                if nt == 0:
                    P.wait(tensor, "tensor", bank_free[4], bank_free[5],
                           bank_free[6])
                for h in range(NH):
                    pb = PB[4 + h // 8]
                    cs = (h % 8) * 64
                    tensor.matmul(pb[0:64, cs:cs + 64],
                                  ktm[par][:, h * 64:(h + 1) * 64],
                                  vtm[par][:, h * 64:(h + 1) * 64],
                                  start=(nt == 0 and h % 8 == 0),
                                  stop=(nt == 15 and h % 8 == 7),
                                  skip_group_check=True)
                e = None
                for h in range(NH):
                    e = P.inc(tensor.matmul(
                        PB[6][0:64, h:h + 1], ktm[par][:, h * 64:(h + 1) * 64],
                        ones_b[:], start=(nt == 0 and h == 0),
                        stop=(nt == 15 and h == 15),
                        skip_group_check=True), "pe")
                ev_kvzmm[par] = e

        for nt in range(16):
            par = nt % 2
            b0, b1 = 0 + 2 * par, 1 + 2 * par

            @block.tensor
            def _(tensor, nt=nt, par=par, b0=b0, b1=b1):
                P.wait(tensor, "tensor", ev["wk"], ev_xn[nt // 4],
                       bank_free[b0], bank_free[b1])
                e = None
                for ih in range(2):
                    for c in range(DCH):
                        e = P.inc(tensor.matmul(
                            PB[b0 + ih][:, :],
                            xn_qx[:, c * T + nt * 128: c * T + (nt + 1) * 128],
                            wa[0][:, c * 1024 + ih * 512: c * 1024 + (ih + 1) * 512],
                            start=(c == 0), stop=(c == DCH - 1)), "pe")
                ev[f"kmm_{nt}"] = e

            @block.vector
            def _(vector, nt=nt, par=par, b0=b0):
                P.wait(vector, "vector", ev[f"kmm_{nt}"], ev["bbce1"],
                       ev_kvzmm[par], ev_ke[1 - par])
                e = None
                for ih in range(2):
                    e = P.inc(vector.tensor_tensor(
                        fst[2 + ih][:], PB[b0 + ih][:, :],
                        bk_bc[:, ih * 512:(ih + 1) * 512], ALU.add), "dv")
                ev[f"kt_{nt}"] = e

            @block.vector
            def _(vector, nt=nt):
                e = None
                for ih in range(2):
                    e = P.inc(vector.tensor_scalar_max(bst[2 + ih][:],
                                                       fst[2 + ih][:], 0.0), "dv")
                ev[f"kr_{nt}"] = e

            @block.scalar
            def _(scalar, nt=nt):
                P.wait(scalar, "scalar", ev[f"kt_{nt}"], ev[f"kr_{nt}"])
                e = None
                for ih in range(2):
                    scalar.activation(fst[2 + ih][:], fst[2 + ih][:], AF.Relu, scale=-1.0)
                for ih in range(2):
                    e = P.inc(scalar.activation(bst[ih][:], fst[2 + ih][:],
                                                AF.Exp, scale=-1.0), "ac")
                ev[f"ke_{nt}"] = e
                ev_ke[nt % 2] = e

            @block.vector
            def _(vector, nt=nt, par=par, b0=b0, b1=b1):
                P.wait(vector, "vector", ev[f"ke_{nt}"])
                e = None
                for ih in range(2):
                    e = P.inc(vector.tensor_tensor(
                        ktm[par][:, ih * 512:(ih + 1) * 512], bst[ih][:],
                        bst[2 + ih][:], ALU.add), "dv")
                ev_ktm[par] = e
                bank_free[b0] = e
                bank_free[b1] = e

            @block.tensor
            def _(tensor, nt=nt, par=par, b0=b0, b1=b1):
                P.wait(tensor, "tensor", ev["wv"], bank_free[b0], bank_free[b1])
                e = None
                for ih in range(2):
                    for c in range(DCH):
                        e = P.inc(tensor.matmul(
                            PB[b0 + ih][:, :],
                            xn_qx[:, c * T + nt * 128: c * T + (nt + 1) * 128],
                            wa[1][:, c * 1024 + ih * 512: c * 1024 + (ih + 1) * 512],
                            start=(c == 0), stop=(c == DCH - 1)), "pe")
                ev[f"vmm_{nt}"] = e

            @block.vector
            def _(vector, nt=nt, par=par, b0=b0, b1=b1):
                P.wait(vector, "vector", ev[f"vmm_{nt}"])
                e = None
                for ih in range(2):
                    e = P.inc(vector.tensor_tensor(
                        vtm[par][:, ih * 512:(ih + 1) * 512], PB[b0 + ih][:, :],
                        bv_bc[:, ih * 512:(ih + 1) * 512], ALU.add), "dv")
                ev_vtm[par] = e
                bank_free[b0] = e
                bank_free[b1] = e

            if nt >= 1:
                emit_kvz(nt - 1)
            wprep_emit(2)
            if nt == 10:
                @block.gpsimd
                def _(gpsimd):
                    P.wait(gpsimd, "gpsimd", ev_xn[3])
                    ev["wq"] = P.inc(gpsimd.dma_start(
                        out=wq_sb[:].rearrange("p (c n) -> p c n", c=DCH),
                        in_=wqT.rearrange("(c p) n -> p c n", p=128)), "s_wa", 16)
        emit_kvz(15)
        ev["kvzdone"] = ev_kvzmm[1]

        @block.vector
        def _(vector):
            P.wait(vector, "vector", ev["kvzdone"], ev["memset"], ev["bbce1"])
            e0 = None
            for h in range(NH):
                pb = PB[4 + h // 8]
                cs = (h % 8) * 64
                vector.tensor_copy(kvz_sb[:, h * 66:h * 66 + 64], pb[0:64, cs:cs + 64])
            for h in range(NH):
                e0 = P.inc(vector.tensor_copy(kvz_sb[:, h * 66 + 64:h * 66 + 65],
                                              PB[6][0:64, h:h + 1]), "dv")
            bank_free[4] = e0
            bank_free[5] = e0
            bank_free[6] = e0
            ev["kvz_evac"] = e0
            ev["kvzm0"] = P.inc(vector.tensor_scalar_mul(kvz_m[:], kvz_sb[:],
                                                         bsel[0:64, 0:1]), "dv")

        if tap == "kvz0":
            @block.gpsimd
            def _(gpsimd):
                P.wait(gpsimd, "gpsimd", ev["kvz_evac"])
                P.inc(gpsimd.dma_start(out=out[0:64, 0:1056], in_=kvz_sb[:]), "so", 16)
                gpsimd.wait_ge(P.sem["so"], 16)
            return nc

        @block.gpsimd
        def _(gpsimd):
            P.wait(gpsimd, "gpsimd", ev["kvzm0"])
            e = P.inc(gpsimd.dma_start(out=kvz_in.ap()[0:64, :], in_=kvz_m[:]),
                      "s_kvz", 16)
            ev["kvzs0"] = e

        @block.vector
        def _(vector):
            P.wait(vector, "vector", ev["kvzs0"])
            ev["kvzm1"] = P.inc(vector.tensor_scalar_mul(kvz_m[:], kvz_sb[:],
                                                         bsel[0:64, 1:2]), "dv")

        @block.gpsimd
        def _(gpsimd):
            P.wait(gpsimd, "gpsimd", ev["kvzm1"])
            P.inc(gpsimd.dma_start(out=kvz_in.ap()[64:128, :], in_=kvz_m[:]),
                  "s_kvz", 16)
            P.wait(gpsimd, "gpsimd", P.now("s_kvz"))
            P.inc(gpsimd.collective_compute(
                "AllReduce", ALU.add, replica_groups=RG,
                ins=[kvz_in.ap().opt()], outs=[kvz_out.ap().opt()]), "cc")
            P.wait(gpsimd, "gpsimd", P.now("cc"))
            P.inc(gpsimd.dma_start(out=kvz_sb[:], in_=kvz_out.ap()[0:64, :]),
                  "s_kvz", 16)
            ev["kvzr"] = P.inc(gpsimd.dma_start(
                out=kvz_m[:], in_=kvz_out.ap()[64:128, :]), "s_kvz", 16)

        @block.vector
        def _(vector):
            P.wait(vector, "vector", ev["kvzr"])
            vector.tensor_scalar_mul(kvz_sb[:], kvz_sb[:], bsel[0:64, 0:1])
            vector.scalar_tensor_tensor(kvz_sb[:], kvz_m[:], bsel[0:64, 1:2], kvz_sb[:],
                                        ALU.mult, ALU.add)
            ev["kvzbf"] = P.inc(vector.tensor_copy(kvz_bf[0:64, :], kvz_sb[:]), "dv")

        @block.gpsimd
        def _(gpsimd):
            P.wait(gpsimd, "gpsimd", ev["kvzbf"])
            P.inc(gpsimd.dma_start(out=kvzbf_d.ap(), in_=kvz_bf[0:64, :]),
                  "s_kvz", 16)
            P.wait(gpsimd, "gpsimd", P.now("s_kvz"))
            ev["kvzbf2"] = P.inc(gpsimd.dma_start(out=kvz_bf[64:128, :],
                                                  in_=kvzbf_d.ap()), "s_kvz", 16)


        if tap == "kvz":
            @block.gpsimd
            def _(gpsimd):
                P.wait(gpsimd, "gpsimd", ev["kvzbf"])
                P.inc(gpsimd.dma_start(out=out[0:64, 0:1056], in_=kvz_sb[:]), "so", 16)
                gpsimd.wait_ge(P.sem["so"], 16)
            return nc

        qit = -1
        for ich in range(DCH):
            for nt in range(4):
                qit += 1
                pb = qit % 4
                ip = qit % 2

                @block.tensor
                def _(tensor, ich=ich, nt=nt, pb=pb):
                    P.wait(tensor, "tensor", ev["wq"], bank_free[pb])
                    e = None
                    for c in range(DCH):
                        e = P.inc(tensor.matmul(
                            PB[pb][:, :],
                            wq_sb[:, c * 1024 + ich * 128: c * 1024 + (ich + 1) * 128],
                            xn_qx[:, c * T + nt * 512: c * T + (nt + 1) * 512],
                            start=(c == 0), stop=(c == DCH - 1)), "pe")
                    ev[f"qmm_{qit}"] = e

                @block.vector
                def _(vector, ich=ich, qit=qit, pb=pb, ip=ip):
                    P.wait(vector, "vector", ev[f"qmm_{qit}"], ev.get(f"qe_{qit - 2}"))
                    ev[f"qt_{qit}"] = P.inc(vector.tensor_scalar(
                        fst[ip][:], PB[pb][:, :], bqc[:, ich:ich + 1], None, ALU.add),
                        "dv")

                if qit % 2 == 1:
                    @block.scalar
                    def _(scalar, qit=qit):
                        P.wait(scalar, "scalar", ev[f"qt_{qit - 1}"],
                               ev[f"qt_{qit}"])
                        for q2 in (qit - 1, qit):
                            ip2 = q2 % 2
                            scalar.activation(fst[2 + ip2][:], fst[ip2][:],
                                              AF.Relu, scale=-1.0)
                        for q2 in (qit - 1, qit):
                            ip2 = q2 % 2
                            ev[f"qe_{q2}"] = P.inc(scalar.activation(
                                bst[ip2][:], fst[2 + ip2][:],
                                AF.Exp, scale=-1.0), "ac")

                if qit % 2 == 1:
                    @block.vector
                    def _(vector, qit=qit):
                        for q2 in (qit - 1, qit):
                            ip2 = q2 % 2
                            pb2 = q2 % 4
                            ich2, nt2 = q2 // 4, q2 % 4
                            P.wait(vector, "vector", ev[f"qe_{q2}"])
                            vector.tensor_scalar_max(bst[2 + ip2][:],
                                                     fst[ip2][:], 0.0)
                            e = P.inc(vector.tensor_tensor(
                                q_a[:, ich2 * T + nt2 * 512:
                                    ich2 * T + (nt2 + 1) * 512],
                                bst[ip2][:], bst[2 + ip2][:], ALU.add), "dv")
                            ev[f"qph_{q2}"] = e
                            bank_free[pb2] = e
            if ich % 2 == 0:
                wprep_emit(2)
        ev["qdone"] = ev[f"qph_{qit}"]

        if tap == "q":
            @block.gpsimd
            def _(gpsimd):
                P.wait(gpsimd, "gpsimd", ev["qdone"])
                P.inc(gpsimd.dma_start(
                    out=out.rearrange("(c p) n -> p c n", p=128), in_=q_a[:].rearrange("p (c n) -> p c n", c=DCH)),
                    "so", 16)
                gpsimd.wait_ge(P.sem["so"], 16)
            return nc

        wprep_emit(len(wprep_tiles))
        assert wp["cur"] == len(wprep_tiles)

        if tap == "w52":
            @block.gpsimd
            def _(gpsimd):
                P.wait(gpsimd, "gpsimd", wp["st"][len(wprep_tiles) - 1],
                       wp["st"][len(wprep_tiles) - 2], ev["qdone"], ev["mxg"])
                P.inc(gpsimd.dma_start(out=out[0:128, 0:8], in_=wsd[:, 0:8]), "so", 16)
                gpsimd.wait_ge(P.sem["so"], 16)
            return nc

        @block.vector
        def _(vector):
            P.wait(vector, "vector", ev["mxg"], ev["sig"])
            e1 = P.inc(vector.tensor_tensor(srow[0:1, 2:3], srow[0:1, 1:2],
                                            mxg[0:1, 0:1], ALU.mult), "dv")
            vector.wait_ge(P.sem["dv"], e1[1])
            e2 = P.inc(vector.tensor_scalar_mul(srow[0:1, 3:4], srow[0:1, 2:3],
                                                1.0 / QMAX), "dv")
            vector.wait_ge(P.sem["dv"], e2[1])
            ev["rx"] = P.inc(vector.reciprocal(srow[0:1, 4:5], srow[0:1, 3:4]), "dv")

        @block.tensor
        def _(tensor):
            P.wait(tensor, "tensor", ev["rx"], bank_free[4])
            tensor.matmul(PB[4][:, 0:1], ones_r[:], srow[0:1, 4:5], start=True,
                          stop=True)
            ev["rxbc"] = P.inc(tensor.matmul(PB[4][:, 1:2], ones_r[:], srow[0:1, 3:4],
                                             start=True, stop=True), "pe")

        @block.vector
        def _(vector):
            P.wait(vector, "vector", ev["rxbc"])
            vector.tensor_copy(rcol[:, 0:1], PB[4][:, 0:1])
            e = P.inc(vector.tensor_copy(rcol[:, 1:2], PB[4][:, 1:2]), "dv")
            bank_free[4] = e
            vector.wait_ge(P.sem["dv"], e[1])
            vector.tensor_scalar_mul(gs_c[:], wsg[:], rcol[:, 1:2])
            ev["gscols"] = P.inc(vector.tensor_scalar_mul(us_c[:], wsu[:], rcol[:, 1:2]),
                                 "dv")

        if tap == "rx":
            @block.gpsimd
            def _(gpsimd):
                P.wait(gpsimd, "gpsimd", ev["gscols"], ev["qdone"],
                       wp["st"][len(wprep_tiles) - 1], wp["st"][len(wprep_tiles) - 2])
                P.inc(gpsimd.dma_start(out=out[0:128, 0:22], in_=gs_c[:, :]), "so", 16)
                gpsimd.wait_ge(P.sem["so"], 16)
            return nc

        ev_qx = [None] * DCH
        for c in range(DCH):
            @block.vector
            def _(vector, c=c):
                P.wait(vector, "vector", ev["gscols"], ev.get(f"qxp_{c - 2}"))
                ev[f"qxv_{c}"] = P.inc(vector.tensor_scalar(
                    xsl[0][:, (c % 2) * 2048:(c % 2) * 2048 + 2048],
                    xn_qx[:, c * T:(c + 1) * T], rcol[:, 0:1], MAGIC,
                    ALU.mult, ALU.add), "dv")

            @block.vector
            def _(vector, c=c):
                vector.tensor_scalar(
                    xsl[0][:, (c % 2) * 2048:(c % 2) * 2048 + 2048],
                    xsl[0][:, (c % 2) * 2048:(c % 2) * 2048 + 2048],
                    MAGIC, QMAX, ALU.subtract, ALU.min)
                e = P.inc(vector.tensor_scalar_max(
                    xn_qx[:, c * T:(c + 1) * T],
                    xsl[0][:, (c % 2) * 2048:(c % 2) * 2048 + 2048],
                    -(QMAX + 1.0)), "dv")
                ev[f"qxp_{c}"] = e
                ev_qx[c] = e

        if tap == "qx":
            @block.gpsimd
            def _(gpsimd):
                P.wait(gpsimd, "gpsimd", *[ev_qx[c] for c in range(DCH)])
                P.inc(gpsimd.dma_start(
                    out=out.rearrange("(c p) n -> p c n", p=128), in_=xn_qx[:].rearrange("p (c n) -> p c n", c=DCH)),
                    "so", 16)
                gpsimd.wait_ge(P.sem["so"], 16)
            return nc

        cm_wq.__exit__(None, None, None)
        cm_vtm1.__exit__(None, None, None)
        cm_vtm0.__exit__(None, None, None)
        cm_ktm1.__exit__(None, None, None)
        cm_ktm0.__exit__(None, None, None)
        cm_xsl0.__exit__(None, None, None)
        cm_wf1.__exit__(None, None, None)
        cm_wa0.__exit__(None, None, None)

        cm_qgT0, qgT0 = alloc("qgT0", [128, 4096], BF16)
        cm_quT0, quT0 = alloc("quT0", [128, 4096], BF16)
        cm_hst0, hst0 = alloc("hst0", [128, T], BF16)
        cm_hst1, hst1 = alloc("hst1", [128, T], BF16)
        hstage = [hst0, hst1]
        cm_qgT1, qgT1 = alloc("qgT1", [128, 4096], BF16)
        cm_quT1, quT1 = alloc("quT1", [128, 4096], BF16)
        qgT = [qgT0, qgT1]
        quT = [quT0, quT1]

        GROUPS = [(0, 1), (1, 1), (2, 2), (4, 4), (8, 4), (12, 4), (16, 4), (20, 2)]
        ev_gld = [None] * len(GROUPS)
        ev_gu_use = [None, None]

        def emit_gload(g):
            t0, ntile = GROUPS[g]
            par = g % 2

            @block.sync
            def _(sync, g=g, t0=t0, ntile=ntile, par=par):
                P.wait(sync, "sync", wp["st"].get(t0 + ntile - 1),
                       wp["st"].get(ITCH + t0 + ntile - 1),
                       ev["kmm_15"], ev_gu_use[par])
                if par == 1:
                    P.wait(sync, "sync", wp["st"].get(len(wprep_tiles) - 1),
                           ev_qx[DCH - 1], ev_xn[3])
                rows_ = ntile * 128
                e = None
                for c in range(DCH):
                    P.inc(sync.dma_start(
                        out=qgT[par][:, c * 512:c * 512 + rows_],
                        in_=qg_d[t0 * 128:t0 * 128 + rows_, c * 128:(c + 1) * 128],
                        transpose=True), f"s_gl{par}", 16)
                    e = P.inc(sync.dma_start(
                        out=quT[par][:, c * 512:c * 512 + rows_],
                        in_=qu_d[t0 * 128:t0 * 128 + rows_, c * 128:(c + 1) * 128],
                        transpose=True), f"s_gl{par}", 16)
                ev_gld[g] = e

        emit_gload(0)
        ev_h = {}
        for g, (t0, ntile) in enumerate(GROUPS):
            par = g % 2
            if g + 1 < len(GROUPS):
                emit_gload(g + 1)
            for tl in range(ntile):
                it = t0 + tl
                hpar = it % 2
                for nt in range(4):
                    bg = (nt % 2) * 2
                    bu = bg + 1

                    @block.tensor
                    def _(tensor, g=g, par=par, tl=tl, it=it, nt=nt, bg=bg, bu=bu,
                          ntile=ntile):
                        P.wait(tensor, "tensor", ev_gld[g], ev_qx[DCH - 1],
                               bank_free[bg], bank_free[bu])
                        for c in range(DCH):
                            tensor.matmul(
                                PB[bg][:, :],
                                qgT[par][:, c * 512 + tl * 128: c * 512 + (tl + 1) * 128],
                                xn_qx[:, c * T + nt * 512: c * T + (nt + 1) * 512],
                                start=(c == 0), stop=(c == DCH - 1))
                        e = None
                        for c in range(DCH):
                            e = P.inc(tensor.matmul(
                                PB[bu][:, :],
                                quT[par][:, c * 512 + tl * 128: c * 512 + (tl + 1) * 128],
                                xn_qx[:, c * T + nt * 512: c * T + (nt + 1) * 512],
                                start=(c == 0), stop=(c == DCH - 1)), "pe")
                        ev[f"gumm_{it}_{nt}"] = e
                        if tl == ntile - 1 and nt == 3:
                            ev_gu_use[par] = e

                    @block.scalar
                    def _(scalar, it=it, nt=nt, bg=bg):
                        P.wait(scalar, "scalar", ev[f"gumm_{it}_{nt}"], ev["gscols"],
                               ev.get(f"h_{it}_{nt - 2}") if nt >= 2
                               else ev.get(f"h_{it - 1}_{nt + 2}"))
                        ev[f"sg_{it}_{nt}"] = P.inc(scalar.activation(
                            fst[nt % 2][:], PB[bg][:, :], AF.Silu,
                            scale=gs_c[:, it:it + 1]), "ac")

                    @block.vector
                    def _(vector, it=it, nt=nt, hpar=hpar, bg=bg, bu=bu):
                        P.wait(vector, "vector", ev[f"sg_{it}_{nt}"])
                        if it == 0 and nt == 0:
                            P.wait(vector, "vector",
                                   wp["st"].get(len(wprep_tiles) - 1))
                        if nt == 0:
                            P.wait(vector, "vector", ev_h.get(it - 2))
                        e = P.inc(vector.scalar_tensor_tensor(
                            hstage[hpar][:, nt * 512:(nt + 1) * 512],
                            PB[bu][:, :], us_c[:, it:it + 1], fst[nt % 2][:],
                            ALU.mult, ALU.mult), "dv")
                        ev[f"h_{it}_{nt}"] = e
                        bank_free[bg] = e
                        bank_free[bu] = e

                @block.gpsimd
                def _(gpsimd, it=it, hpar=hpar):
                    P.wait(gpsimd, "gpsimd", ev[f"h_{it}_3"])
                    if it > 0:
                        P.wait(gpsimd, "gpsimd", ev[f"hmxa_{it - 1}"])
                    ev[f"hmx_{it}"] = P.inc(gpsimd.tensor_reduce(
                        hrun[0:1, 1:2], hstage[hpar][:, :],
                        AX.XYZWC, ALU.abs_max), "gp")

                @block.vector
                def _(vector, it=it):
                    P.wait(vector, "vector", ev[f"hmx_{it}"])
                    ev[f"hmxa_{it}"] = P.inc(vector.tensor_tensor(
                        hrun[0:1, 0:1], hrun[0:1, 0:1], hrun[0:1, 1:2], ALU.max), "dv")

                @block.sync
                def _(sync, it=it, hpar=hpar):
                    P.wait(sync, "sync", ev[f"hmx_{it}"])
                    ev_h[it] = P.inc(sync.dma_start(
                        out=h_d[it * 128:(it + 1) * 128, :], in_=hstage[hpar][:, :]),
                        f"s_h{it % 2}", 16)

        @block.gpsimd
        def _(gpsimd):
            P.wait(gpsimd, "gpsimd", ev[f"hmxa_{ITCH - 1}"])
            eC = P.inc(gpsimd.tensor_copy(mxs2[0:1, 0:1], hrun[0:1, 0:1]), "gp")
            gpsimd.wait_ge(P.sem["gp"], eC[1])
            P.inc(gpsimd.dma_start(out=mx2_in.ap(), in_=mxs2[0:1, :]), "s_mx", 16)
            P.wait(gpsimd, "gpsimd", P.now("s_mx"))
            P.inc(gpsimd.collective_compute(
                "AllReduce", ALU.max, replica_groups=RG,
                ins=[mx2_in.ap().opt()], outs=[mx2_out.ap().opt()]), "cc")
            P.wait(gpsimd, "gpsimd", P.now("cc"))
            ev["mxg2"] = P.inc(gpsimd.dma_start(out=mxg2[:], in_=mx2_out.ap()), "s_mx", 16)

        if tap == "gup":
            @block.gpsimd
            def _(gpsimd):
                P.wait(gpsimd, "gpsimd", ev_h[ITCH - 1], ev_h[ITCH - 2], ev["mxg2"])
                P.inc(gpsimd.dma_start(out=out[0:128, 0:8], in_=hrun[:, 0:8]), "so", 16)
                gpsimd.wait_ge(P.sem["so"], 16)
            return nc

        cm_quT1.__exit__(None, None, None)
        cm_qgT1.__exit__(None, None, None)
        cm_hst1.__exit__(None, None, None)
        cm_hst0.__exit__(None, None, None)
        cm_quT0.__exit__(None, None, None)
        cm_qgT0.__exit__(None, None, None)
        cm_qdT, qdT = alloc("qdT", [128, ITCH * 1024], BF16)
        cm_rtm, rec_tm = alloc("rec_tm", [128, 64], F32)
        cm_rtm2, rec_tm2 = alloc("rec_tm2", [128, 64], F32)
        cm_recT, recT = alloc("recT", [16, T], BF16)
        cm_xo0, xo0 = alloc("xo0", [128, T], F32)
        xoch = [xo0, xo0]

        @block.sync
        def _(sync):
            P.wait(sync, "sync", wp["stn"].get("d"), ev_gu_use[0], ev_gu_use[1],
                   ev_h.get(ITCH - 1), ev_h.get(ITCH - 2))
            e = None
            for ic in range(ITCH):
                e = P.inc(sync.dma_start(out=qdT[:, ic * 1024:(ic + 1) * 1024],
                                         in_=qd_d[:, ic * 128:(ic + 1) * 128],
                                         transpose=True), "s_qdT", 16)
            ev["qdT"] = e

        @block.gpsimd
        def _(gpsimd):
            P.wait(gpsimd, "gpsimd", ev["kvzdone"])
            ev["wo"] = P.inc(gpsimd.dma_start(
                out=wa[1][:].rearrange("p (c n) -> p c n", c=DCH), in_=woT.rearrange("(c p) n -> p c n", p=128)), "s_wb", 16)

        @block.scalar
        def _(scalar):
            P.wait(scalar, "scalar", ev_gu_use[0], ev_gu_use[1])
            ev["xo0"] = P.inc(scalar.dma_start(out=xoch[0][:], in_=xfm[0:128, :]),
                              "s_xo0", 16)
            ev["xo1"] = ev["xo0"]
        ev_xoch = [ev["xo0"]] + [None] * (DCH - 1)

        if tap == "preattn":
            @block.gpsimd
            def _(gpsimd):
                P.wait(gpsimd, "gpsimd", ev["qdT"], ev["wo"], ev["xo1"], ev["kvzbf2"])
                P.inc(gpsimd.dma_start(out=out[0:128, 0:1024],
                                       in_=qdT[:, 0:1024]), "so", 16)
                gpsimd.wait_ge(P.sem["so"], 16)
            return nc

        for nt in range(4):
            @block.tensor
            def _(tensor, nt=nt):
                P.wait(tensor, "tensor", ev["kvzbf2"], ev["qdone"], bank_free[4])
                P.wait(tensor, "tensor", ev["xo0"],
                       ev["xo1"], ev_h[ITCH - 1], ev_h[ITCH - 2])
                e = None
                for sub in range(4):
                    for h in range(NH):
                        e = P.inc(tensor.matmul(
                            PB[4][:, sub * 16 + h:sub * 16 + h + 1],
                            q_a[(h % 2) * 64:(h % 2) * 64 + 64,
                                (h // 2) * T + nt * 512 + sub * 128:
                                (h // 2) * T + nt * 512 + (sub + 1) * 128],
                            kvz_bf[(h % 2) * 64:(h % 2) * 64 + 64,
                                   h * 66 + 64:h * 66 + 65],
                            start=True, stop=True), "pe")
                ev[f"znorm_{nt}"] = e

            if tap is not None and tap.startswith("znorm") and nt == 0:
                @block.vector
                def _(vector):
                    P.wait(vector, "vector", ev["znorm_0"])
                    P.inc(vector.tensor_copy(fst[0][:, 0:64], PB[4][:, 0:64]), "dv")
                    ev["zdump"] = P.now("dv")

                @block.gpsimd
                def _(gpsimd):
                    P.wait(gpsimd, "gpsimd", ev["zdump"])
                    P.inc(gpsimd.dma_start(out=out[0:128, 0:512], in_=fst[0][:]),
                          "so", 16)
                    gpsimd.wait_ge(P.sem["so"], 16)
                return nc

            @block.vector
            def _(vector, nt=nt):
                P.wait(vector, "vector", ev[f"znorm_{nt}"], ev_h[ITCH - 1],
                       ev_h[ITCH - 2], ev_gu_use[0], ev_gu_use[1])
                vector.tensor_scalar_add(rec_tm[:, :], PB[4][:, 0:64], 1e-6)
                e = P.inc(vector.reciprocal(rec_tm2[:, :], rec_tm[:, :]), "dv")
                ev[f"rec_{nt}"] = e
                bank_free[4] = e

            for sub in range(4):
                @block.tensor
                def _(tensor, nt=nt, sub=sub):
                    P.wait(tensor, "tensor", ev[f"rec_{nt}"], bank_free[5])
                    ev[f"recT_{nt}_{sub}"] = P.inc(tensor.transpose(
                        PB[5][0:16, 0:128],
                        rec_tm2[:, sub * 16:(sub + 1) * 16], ident[:]), "pe")

                @block.vector
                def _(vector, nt=nt, sub=sub):
                    P.wait(vector, "vector", ev[f"recT_{nt}_{sub}"])
                    e = P.inc(vector.tensor_copy(
                        recT[:, nt * 512 + sub * 128:nt * 512 + (sub + 1) * 128],
                        PB[5][0:16, 0:128]), "dv")
                    bank_free[5] = e
            ev[f"recTe_{nt}"] = P.now("dv")

            if tap == "rec" and nt == 3:
                @block.gpsimd
                def _(gpsimd):
                    P.wait(gpsimd, "gpsimd", ev["recTe_3"])
                    P.inc(gpsimd.dma_start(out=out[0:16, 0:2048], in_=recT[:, :]),
                          "so", 16)
                    gpsimd.wait_ge(P.sem["so"], 16)
                return nc

            for e2 in range(DCH):
                pb = 6 + e2 % 2
                pa = e2 % 2

                @block.tensor
                def _(tensor, nt=nt, e2=e2, pb=pb):
                    P.wait(tensor, "tensor", ev[f"recTe_{nt}"], bank_free[pb])
                    ev[f"rbc_{nt}_{e2}"] = P.inc(tensor.matmul(
                        PB[pb][:, :], pairs[:, e2 * 128:(e2 + 1) * 128],
                        recT[:, nt * 512:(nt + 1) * 512], start=True, stop=True), "pe")

                @block.vector
                def _(vector, nt=nt, e2=e2, pb=pb):
                    P.wait(vector, "vector", ev[f"rbc_{nt}_{e2}"])
                    e = P.inc(vector.tensor_tensor(
                        q_a[:, e2 * T + nt * 512:e2 * T + (nt + 1) * 512],
                        q_a[:, e2 * T + nt * 512:e2 * T + (nt + 1) * 512],
                        PB[pb][:, :], ALU.mult), "dv")
                    ev[f"qp_{nt}_{e2}"] = e
                    bank_free[pb] = e

                @block.tensor
                def _(tensor, nt=nt, e2=e2, pa=pa):
                    P.wait(tensor, "tensor", ev[f"qp_{nt}_{e2}"], bank_free[pa])
                    e = None
                    for hh in range(2):
                        h = 2 * e2 + hh
                        e = P.inc(tensor.matmul(
                            PB[pa][hh * 64:(hh + 1) * 64, :],
                            kvz_bf[hh * 64:hh * 64 + 64, h * 66:h * 66 + 64],
                            q_a[hh * 64:(hh + 1) * 64,
                                e2 * T + nt * 512:e2 * T + (nt + 1) * 512],
                            start=True, stop=True), "pe")
                    ev[f"amm_{nt}_{e2}"] = e

                @block.vector
                def _(vector, nt=nt, e2=e2, pa=pa):
                    P.wait(vector, "vector", ev[f"amm_{nt}_{e2}"])
                    e = P.inc(vector.tensor_copy(
                        q_a[:, e2 * T + nt * 512:e2 * T + (nt + 1) * 512],
                        PB[pa][:, :]), "dv")
                    ev[f"ae_{nt}_{e2}"] = e
                    bank_free[pa] = e
        ev["adone"] = ev["ae_3_7"]

        if tap == "aphase":
            @block.gpsimd
            def _(gpsimd):
                P.wait(gpsimd, "gpsimd", ev["adone"])
                P.inc(gpsimd.dma_start(
                    out=out.rearrange("(c p) n -> p c n", p=128),
                    in_=q_a[:].rearrange("p (c n) -> p c n", c=DCH)), "so", 16)
                gpsimd.wait_ge(P.sem["so"], 16)
            return nc

        for oc in range(DCH):
            for nt in range(4):
                pb = (oc * 4 + nt) % 4

                @block.tensor
                def _(tensor, oc=oc, nt=nt, pb=pb):
                    P.wait(tensor, "tensor", ev["wo"], ev["adone"], bank_free[pb])
                    e = None
                    for c in range(DCH):
                        e = P.inc(tensor.matmul(
                            PB[pb][:, :],
                            wa[1][:, c * 1024 + oc * 128: c * 1024 + (oc + 1) * 128],
                            q_a[:, c * T + nt * 512: c * T + (nt + 1) * 512],
                            start=(c == 0), stop=(c == DCH - 1)), "pe")
                    ev[f"womm_{oc}_{nt}"] = e

                @block.scalar
                def _(scalar, oc=oc, nt=nt, pb=pb):
                    key = (f"attnx_{oc}_{nt - 2}" if nt >= 2
                           else f"attnx_{oc - 1}_{nt + 2}")
                    P.wait(scalar, "scalar", ev[f"womm_{oc}_{nt}"], ev.get(key))
                    ev[f"woe_{oc}_{nt}"] = P.inc(scalar.activation(
                        fst[2 + nt % 2][:], PB[pb][:, :], AF.Identity,
                        bias=boc[:, oc:oc + 1]), "ac")

                @block.vector
                def _(vector, oc=oc, nt=nt, pb=pb):
                    P.wait(vector, "vector", ev[f"woe_{oc}_{nt}"], ev_xoch[oc])
                    e = P.inc(vector.tensor_tensor(
                        xn_qx[:, oc * T + nt * 512: oc * T + (nt + 1) * 512],
                        fst[2 + nt % 2][:], xoch[0][:, nt * 512:(nt + 1) * 512],
                        ALU.add), "dv")
                    ev[f"attnx_{oc}_{nt}"] = e
                    bank_free[pb] = e

            if oc + 1 < DCH:
                @block.scalar
                def _(scalar, oc=oc):
                    P.wait(scalar, "scalar", ev[f"attnx_{oc}_3"])
                    ev_xoch[oc + 1] = P.inc(scalar.dma_start(
                        out=xoch[0][:],
                        in_=xfm[(oc + 1) * 128:(oc + 2) * 128, :]), "s_xo0", 16)
        ev["attnx"] = ev["attnx_7_3"]

        if tap == "attn":
            @block.gpsimd
            def _(gpsimd):
                P.wait(gpsimd, "gpsimd", ev["attnx"])
                P.inc(gpsimd.dma_start(
                    out=out.rearrange("(c p) n -> p c n", p=128), in_=xn_qx[:].rearrange("p (c n) -> p c n", c=DCH)),
                    "so", 16)
                gpsimd.wait_ge(P.sem["so"], 16)
            return nc

        @block.vector
        def _(vector):
            P.wait(vector, "vector", ev["mxg2"])
            e1 = P.inc(vector.tensor_tensor(srow[0:1, 10:11], srow[0:1, 9:10],
                                            mxg2[0:1, 0:1], ALU.mult), "dv")
            vector.wait_ge(P.sem["dv"], e1[1])
            e2 = P.inc(vector.tensor_scalar_mul(srow[0:1, 11:12], srow[0:1, 10:11],
                                                1.0 / QMAX), "dv")
            vector.wait_ge(P.sem["dv"], e2[1])
            ev["rh"] = P.inc(vector.reciprocal(srow[0:1, 12:13],
                                               srow[0:1, 11:12]), "dv")

        @block.tensor
        def _(tensor):
            P.wait(tensor, "tensor", ev["rh"], bank_free[4])
            tensor.matmul(PB[4][:, 0:1], ones_r[:], srow[0:1, 12:13], start=True,
                          stop=True)
            ev["rhbc"] = P.inc(tensor.matmul(PB[4][:, 1:2], ones_r[:],
                                             srow[0:1, 11:12], start=True, stop=True),
                               "pe")

        @block.vector
        def _(vector):
            P.wait(vector, "vector", ev["rhbc"])
            vector.tensor_copy(rcol[:, 1:2], PB[4][:, 0:1])
            e0 = P.inc(vector.tensor_copy(shcol[:, 0:1], PB[4][:, 1:2]), "dv")
            vector.wait_ge(P.sem["dv"], e0[1])
            e = P.inc(vector.tensor_scalar_mul(ds_c[:], wsd[:], shcol[:, 0:1]), "dv")
            ev["dscols"] = e
            bank_free[4] = e

        cm_xo0.__exit__(None, None, None)
        cm_recT.__exit__(None, None, None)
        cm_rtm2.__exit__(None, None, None)
        cm_rtm.__exit__(None, None, None)
        cm_qh0, qh0 = alloc("qh0", [128, ITCH * 512], BF16)
        qhb = [qh0, q_a]

        hv = h_d.rearrange("(c p) n -> p c n", p=128)
        ev_out = {}
        for nt in range(4):
            qh = qhb[nt % 2]

            @block.scalar
            def _(scalar, nt=nt, qh=qh):
                P.wait(scalar, "scalar", ev_h[ITCH - 1], ev_h[ITCH - 2])
                if nt < 2:
                    P.wait(scalar, "scalar", ev["attnx"])
                else:
                    P.wait(scalar, "scalar", ev[f"dnmm_{nt - 2}_7"])
                ev[f"qhld_{nt}"] = P.inc(scalar.dma_start(
                    out=qh[:, 0:ITCH * 512],
                    in_=hv[:, :, nt * 512:(nt + 1) * 512]), "s_qh", 16)

            for pc in range(4):
                csl = slice(pc * 2816, (pc + 1) * 2816)

                @block.vector
                def _(vector, nt=nt, pc=pc, csl=csl, qh=qh):
                    P.wait(vector, "vector", ev[f"qhld_{nt}"], ev["dscols"],
                           ev.get(f"qhq_{nt}_{pc - 1}")
                           if pc > 0 else ev.get(f"qhq_{nt - 1}_3"))
                    ev[f"qhv_{nt}_{pc}"] = P.inc(vector.tensor_scalar(
                        wf320[:, :], qh[:, csl], rcol[:, 1:2], MAGIC,
                        ALU.mult, ALU.add), "dv")

                @block.vector
                def _(vector, nt=nt, pc=pc, csl=csl, qh=qh):
                    vector.tensor_scalar(wf320[:, :], wf320[:, :],
                                         MAGIC, QMAX, ALU.subtract, ALU.min)
                    ev[f"qhq_{nt}_{pc}"] = P.inc(vector.tensor_scalar_max(
                        qh[:, csl], wf320[:, :], -(QMAX + 1.0)), "dv")

            for hc in range(DCH):
                pb = (nt * DCH + hc) % 8

                @block.tensor
                def _(tensor, nt=nt, hc=hc, pb=pb, qh=qh):
                    P.wait(tensor, "tensor", ev["qdT"], bank_free[pb],
                           *[ev[f"qhq_{nt}_{pc}"] for pc in range(4)])
                    e = None
                    for ic in range(ITCH):
                        e = P.inc(tensor.matmul(
                            PB[pb][:, :],
                            qdT[:, ic * 1024 + hc * 128: ic * 1024 + (hc + 1) * 128],
                            qh[:, ic * 512:(ic + 1) * 512],
                            start=(ic == 0), stop=(ic == ITCH - 1)), "pe")
                    ev[f"dnmm_{nt}_{hc}"] = e

                @block.vector
                def _(vector, nt=nt, hc=hc, pb=pb):
                    P.wait(vector, "vector", ev[f"dnmm_{nt}_{hc}"], ev["attnx"],
                           ev_out.get((nt, hc - 2)) if hc >= 2
                           else ev_out.get((nt - 1, hc + 6)))
                    e = P.inc(vector.scalar_tensor_tensor(
                        fst[hc % 2][:], PB[pb][:, :], ds_c[:, hc:hc + 1],
                        xn_qx[:, hc * T + nt * 512: hc * T + (nt + 1) * 512],
                        ALU.mult, ALU.add), "dv")
                    ev[f"oute_{nt}_{hc}"] = e
                    bank_free[pb] = e

                @block.sync
                def _(sync, nt=nt, hc=hc):
                    P.wait(sync, "sync", ev[f"oute_{nt}_{hc}"])
                    ev_out[(nt, hc)] = P.inc(sync.dma_start(
                        out=out[hc * 128:(hc + 1) * 128, nt * 512:(nt + 1) * 512],
                        in_=fst[hc % 2][:]), f"s_so{hc % 2}", 16)

        @block.sync
        def _(sync):
            sync.wait_ge(P.sem["s_so0"], P.cnt["s_so0"])
            sync.wait_ge(P.sem["s_so1"], P.cnt["s_so1"])

    return nc


_PAIRSEL = None


def _pairsel_const():
    global _PAIRSEL
    if _PAIRSEL is None:
        ps = np.zeros((16, 1024), dtype=np.float32)
        for e in range(8):
            for p in range(128):
                ps[2 * e + (1 if p >= 64 else 0), e * 128 + p] = 1.0
        _PAIRSEL = ps.astype(ml_dtypes.bfloat16)
    return _PAIRSEL


def kernel(L0_attn_x, L0_attn_Wq, L0_attn_bq, L0_attn_Wk, L0_attn_bk,
           L0_attn_Wv, L0_attn_bv, L0_attn_Wo, L0_attn_bo,
           L0_ffn_clip_x, L0_ffn_clip_down, L0_ffn_w_gate, L0_ffn_w_up,
           L0_ffn_w_down, _tap=None):
    f = np.float32
    x = np.asarray(L0_attn_x, dtype=f)
    shared = dict(
        wqT=np.ascontiguousarray(np.asarray(L0_attn_Wq, f).T),
        wkT=np.ascontiguousarray(np.asarray(L0_attn_Wk, f).T),
        wvT=np.ascontiguousarray(np.asarray(L0_attn_Wv, f).T),
        woT=np.ascontiguousarray(np.asarray(L0_attn_Wo, f).T),
        wg=np.ascontiguousarray(np.asarray(L0_ffn_w_gate, f)),
        wu=np.ascontiguousarray(np.asarray(L0_ffn_w_up, f)),
        wd=np.ascontiguousarray(np.asarray(L0_ffn_w_down, f)),
        bq_c=np.ascontiguousarray(np.asarray(L0_attn_bq, f).reshape(8, 128).T),
        bo_c=np.ascontiguousarray(np.asarray(L0_attn_bo, f).reshape(8, 128).T),
        bk_r=np.ascontiguousarray(np.asarray(L0_attn_bk, f).reshape(1, D)),
        bv_r=np.ascontiguousarray(np.asarray(L0_attn_bv, f).reshape(1, D)),
        clip2=np.array([[float(np.asarray(L0_ffn_clip_x).reshape(-1)[0]),
                         float(np.asarray(L0_ffn_clip_down).reshape(-1)[0])]],
                       dtype=f),
        identin=np.eye(128, dtype=f),
        pairselin=_pairsel_const(),
    )
    in_maps = []
    for c in range(NCORES):
        b, s0 = c // 4, (c % 4) * T
        m = dict(shared)
        m["xfm"] = np.ascontiguousarray(x[b, s0:s0 + T, :].T)
        bs = np.zeros((128, 2), dtype=f)
        bs[:, b] = 1.0
        m["bselin"] = bs
        in_maps.append(m)

    nc = build_program(tap=_tap)
    res = run_bass_kernel_spmd(nc, in_maps, core_ids=list(range(NCORES)))

    outp = np.empty((2, 8192, D), dtype=f)
    for c in range(NCORES):
        b, s0 = c // 4, (c % 4) * T
        outp[b, s0:s0 + T, :] = np.asarray(res.results[c]["out"]).T
    return outp


# version 26
# speedup vs baseline: 1.1076x; 1.0015x over previous
from contextlib import ExitStack

import numpy as np
import ml_dtypes

import concourse.bass as bass
import concourse.mybir as mybir
from concourse.bass_utils import run_bass_kernel_spmd

F32 = mybir.dt.float32
BF16 = mybir.dt.bfloat16
AF = mybir.ActivationFunctionType
ALU = mybir.AluOpType
AX = mybir.AxisListType

MAGIC = 12582912.0
NCORES = 8
T = 2048
D = 1024
DCH = 8
I = 2816
ITCH = 22
NH = 16
QMAX = 127.0


class Prog:
    ENG_OF = {"pe": "tensor", "dv": "vector", "ac": "scalar", "gp": "gpsimd"}

    def __init__(self, nc, ctx):
        self.sem = {}
        self.cnt = {}
        self.last_wait = {}
        for name in ["pe", "dv", "ac", "gp", "cc",
                     "s_const", "wpl0", "wpl1", "s_qg", "s_qu", "s_qd",
                     "s_wa", "s_wb", "s_x0", "s_x1", "s_mx", "s_kvz",
                     "s_gl0", "s_gl1", "s_qdT", "s_qh", "s_h0", "s_h1",
                     "s_xo0", "s_xo1", "s_so0", "s_so1", "so"]:
            self.sem[name] = ctx.enter_context(nc.semaphore(name))
            self.cnt[name] = 0

    def inc(self, instr, s, n=1):
        self.cnt[s] += n
        instr.then_inc(self.sem[s], n)
        return (s, self.cnt[s])

    def now(self, s):
        return (s, self.cnt[s])

    def wait(self, eng, engname, *evs):
        for ev in evs:
            if ev is None:
                continue
            s, v = ev
            if v <= 0 or self.ENG_OF.get(s) == engname:
                continue
            key = (engname, s)
            if self.last_wait.get(key, -1) >= v:
                continue
            self.last_wait[key] = v
            eng.wait_ge(self.sem[s], v)


def build_program(tap=None):
    nc = bass.Bass("TRN2", target_bir_lowering=False,
                   dynamic_dma_scratch_size=24576,
                   detect_race_conditions=False)
    dt = nc.dram_tensor

    xfm = dt("xfm", [D, T], F32, kind="ExternalInput").ap()
    wqT = dt("wqT", [D, D], F32, kind="ExternalInput").ap()
    wkT = dt("wkT", [D, D], F32, kind="ExternalInput").ap()
    wvT = dt("wvT", [D, D], F32, kind="ExternalInput").ap()
    woT = dt("woT", [D, D], F32, kind="ExternalInput").ap()
    wg = dt("wg", [I, D], F32, kind="ExternalInput").ap()
    wu = dt("wu", [I, D], F32, kind="ExternalInput").ap()
    wd = dt("wd", [D, I], F32, kind="ExternalInput").ap()
    bq_c = dt("bq_c", [128, DCH], F32, kind="ExternalInput").ap()
    bo_c = dt("bo_c", [128, DCH], F32, kind="ExternalInput").ap()
    bk_r = dt("bk_r", [1, D], F32, kind="ExternalInput").ap()
    bv_r = dt("bv_r", [1, D], F32, kind="ExternalInput").ap()
    clip2 = dt("clip2", [1, 2], F32, kind="ExternalInput").ap()
    bselin = dt("bselin", [128, 2], F32, kind="ExternalInput").ap()
    identin = dt("identin", [128, 128], F32, kind="ExternalInput").ap()
    pairselin = dt("pairselin", [16, 1024], BF16, kind="ExternalInput").ap()
    out = dt("out", [D, T], F32, kind="ExternalOutput").ap()

    qg_d = dt("qg_d", [I, D], BF16)
    qu_d = dt("qu_d", [I, D], BF16)
    qd_d = dt("qd_d", [D, I], BF16)
    h_d = dt("h_d", [I, T], BF16)
    kvz_in = dt("kvz_in", [128, 1056], F32)
    kvz_out = dt("kvz_out", [128, 1056], F32)
    mx1_in = dt("mx1_in", [1, 8], F32)
    mx1_out = dt("mx1_out", [1, 8], F32)
    mx2_in = dt("mx2_in", [1, 8], F32)
    mx2_out = dt("mx2_out", [1, 8], F32)
    kvzbf_d = dt("kvzbf_d", [64, 1056], BF16)

    ctx = ExitStack()
    with ctx:
        def sb(name, shape, dty):
            return ctx.enter_context(nc.sbuf_tensor(name, shape, dty))

        def alloc(name, shape, dty):
            cm = nc.sbuf_tensor(name, shape, dty, side="right")
            return cm, cm.__enter__()

        block = ctx.enter_context(nc.Block())
        P = Prog(nc, ctx)
        RG = [list(range(NCORES))]
        ev = {}

        xn_qx = sb("xn_qx", [128, DCH * T], BF16)
        bk_bc = sb("bk_bc", [128, D], F32)
        bv_bc = sb("bv_bc", [128, D], F32)
        bqc = sb("bqc", [128, DCH], F32)
        boc = sb("boc", [128, DCH], F32)
        clip_s = sb("clip_s", [1, 2], F32)
        bsel = sb("bsel", [128, 2], F32)
        ident = sb("ident", [128, 128], F32)
        ones_b = sb("ones_b", [128, 1], BF16)
        ones_c = sb("ones_c", [128, 1], F32)
        ones_r = sb("ones_r", [1, 128], F32)
        wsg = sb("wsg", [128, ITCH], F32)
        wsu = sb("wsu", [128, ITCH], F32)
        wsd = sb("wsd", [128, DCH], F32)
        gs_c = sb("gs_c", [128, ITCH], F32)
        us_c = sb("us_c", [128, ITCH], F32)
        ds_c = sb("ds_c", [128, DCH], F32)
        rcol = sb("rcol", [128, 2], F32)
        shcol = sb("shcol", [128, 1], F32)
        wrec = sb("wrec", [128, 16], F32)
        negm = sb("negm", [128, 1], F32)
        rows = sb("rows", [1, 1536], F32)
        srow = sb("srow", [1, 16], F32)
        mxs = sb("mxs", [128, 8], F32)
        mxg = sb("mxg", [1, 8], F32)
        mxs2 = sb("mxs2", [128, 8], F32)
        mxg2 = sb("mxg2", [1, 8], F32)
        hrun = sb("hrun", [128, 8], F32)
        fst = [sb(f"fst{i}", [128, 512], F32) for i in range(4)]
        bst = [sb(f"bst{i}", [128, 512], BF16) for i in range(4)]

        PB = [ctx.enter_context(nc.psum_tensor(f"pb{i}", [128, 512], F32))
              for i in range(8)]
        bank_free = [None] * 8

        q_a = sb("q_a", [128, DCH * T], BF16)
        pairs = sb("pairs", [16, 1024], BF16)
        kvz_sb = sb("kvz_sb", [64, 1056], F32)
        kvz_m = sb("kvz_m", [64, 1056], F32)
        kvz_bf = sb("kvz_bf", [128, 1056], BF16)
        wf320 = sb("wf320", [128, I], F32)
        cm_wa1, wa1 = alloc("wa1", [128, DCH * 1024], BF16)
        cm_wa0, wa0 = alloc("wa0", [128, DCH * 1024], BF16)
        wa = [wa0, wa1]
        cm_wf1, wf321 = alloc("wf321", [128, I], F32)
        wf32 = [wf320, wf321]
        cm_xsl0, xsl0 = alloc("xsl0", [128, DCH * 512], F32)
        cm_ktm0, ktm0 = alloc("ktm0", [128, D], BF16)
        cm_ktm1, ktm1 = alloc("ktm1", [128, D], BF16)
        ktm = [ktm0, ktm1]
        cm_vtm0, vtm0 = alloc("vtm0", [128, D], BF16)
        cm_vtm1, vtm1 = alloc("vtm1", [128, D], BF16)
        vtm = [vtm0, vtm1]
        cm_xsl1, xsl1 = alloc("xsl1", [128, DCH * 512], F32)
        xsl = [xsl0, xsl1]

        bkrow = kvz_sb[0:1, 0:D]
        bvrow = kvz_m[0:1, 0:D]

        @block.sync
        def _(sync):
            e = None
            for d_, s_ in [(bqc, bq_c), (boc, bo_c),
                           (clip_s, clip2), (bsel, bselin), (ident, identin),
                           (pairs, pairselin)]:
                e = P.inc(sync.dma_start(out=d_[:], in_=s_), "s_const", 16)
            ev["const"] = e

        @block.gpsimd
        def _(gpsimd):
            ev["wk"] = P.inc(gpsimd.dma_start(
                out=wa[0][:].rearrange("p (c n) -> p c n", c=DCH), in_=wkT.rearrange("(c p) n -> p c n", p=128)), "s_wa", 16)
            P.wait(gpsimd, "gpsimd", ev["wk"])
            ev["wv"] = P.inc(gpsimd.dma_start(
                out=wa[1][:].rearrange("p (c n) -> p c n", c=DCH), in_=wvT.rearrange("(c p) n -> p c n", p=128)), "s_wb", 16)

        @block.vector
        def _(vector):
            vector.memset(ones_b[:], 1.0)
            vector.memset(ones_c[:], 1.0)
            vector.memset(ones_r[:], 1.0)
            vector.memset(negm[:], -MAGIC)
            vector.memset(hrun[:], 0.0)
            vector.memset(mxs[:], 0.0)
            vector.memset(mxs2[:], 0.0)
            P.wait(vector, "vector", ev["const"])
            ev["memset"] = P.inc(vector.memset(kvz_sb[:], 0.0), "dv")

        @block.sync
        def _(sync):
            P.wait(sync, "sync", ev["memset"])
            P.inc(sync.dma_start(out=bkrow, in_=bk_r), "s_const", 16)
            ev["brows"] = P.inc(sync.dma_start(out=bvrow, in_=bv_r), "s_const", 16)

        @block.scalar
        def _(scalar):
            P.wait(scalar, "scalar", ev["const"])
            scalar.activation(srow[0:1, 0:1], clip_s[0:1, 0:1], AF.Exp, scale=-1.0)
            ev["sige"] = P.inc(scalar.activation(srow[0:1, 8:9], clip_s[0:1, 1:2],
                                                 AF.Exp, scale=-1.0), "ac")

        @block.vector
        def _(vector):
            P.wait(vector, "vector", ev["sige"])
            vector.tensor_scalar_add(srow[0:1, 0:1], srow[0:1, 0:1], 1.0)
            e = P.inc(vector.tensor_scalar_add(srow[0:1, 8:9], srow[0:1, 8:9], 1.0),
                      "dv")
            vector.wait_ge(P.sem["dv"], e[1])
            vector.reciprocal(srow[0:1, 1:2], srow[0:1, 0:1])
            ev["sig"] = P.inc(vector.reciprocal(srow[0:1, 9:10], srow[0:1, 8:9]), "dv")

        for ih in range(2):
            @block.tensor
            def _(tensor, ih=ih):
                P.wait(tensor, "tensor", ev["const"], ev["memset"], ev["brows"])
                if ih == 1:
                    P.wait(tensor, "tensor", ev["bbce0"])
                sl5 = slice(ih * 512, (ih + 1) * 512)
                tensor.matmul(PB[6][:, :], ones_r[:], kvz_sb[0:1, sl5], start=True, stop=True)
                ev[f"bbc{ih}"] = P.inc(tensor.matmul(PB[7][:, :], ones_r[:],
                                                     kvz_m[0:1, sl5],
                                                     start=True, stop=True), "pe")

            @block.vector
            def _(vector, ih=ih):
                P.wait(vector, "vector", ev[f"bbc{ih}"])
                sl5 = slice(ih * 512, (ih + 1) * 512)
                vector.tensor_copy(bk_bc[:, sl5], PB[6][:, :])
                ev[f"bbce{ih}"] = P.inc(vector.tensor_copy(bv_bc[:, sl5], PB[7][:, :]), "dv")
        bank_free[6] = ev["bbce1"]
        bank_free[7] = ev["bbce1"]

        wprep_tiles = ([("g", t) for t in range(ITCH)] +
                       [("u", t) for t in range(ITCH)] +
                       [("d", t) for t in range(DCH)])
        wp = {"cur": 0, "ld": {}, "p1": {}, "st": {}, "stn": {}}
        wp_src = {"g": wg, "u": wu, "d": wd}
        wp_dst = {"g": qg_d, "u": qu_d, "d": qd_d}
        wp_cols = {"g": wsg, "u": wsu, "d": wsd}
        wp_fd = {"g": D, "u": D, "d": I}

        def wprep_emit(n):
            start = wp["cur"]
            end = min(start + n, len(wprep_tiles))
            if end == start:
                return
            wp["cur"] = end
            for k in range(start, end):
                w, t = wprep_tiles[k]
                par = k % 2
                fd = wp_fd[w]

                @block.sync
                def _(sync, k=k, w=w, t=t, par=par, fd=fd):
                    P.wait(sync, "sync", wp["st"].get(k - 2))
                    wp["ld"][k] = P.inc(sync.dma_start(
                        out=wf32[par][:, 0:fd],
                        in_=wp_src[w][t * 128:(t + 1) * 128, :]), f"wpl{k % 2}", 16)

                @block.vector
                def _(vector, k=k, w=w, t=t, par=par, fd=fd):
                    P.wait(vector, "vector", wp["ld"][k])
                    e1 = P.inc(vector.tensor_reduce(
                        wrec[:, 0:1], wf32[par][:, 0:fd],
                        AX.X, ALU.max, apply_absolute_value=True), "dv")
                    vector.wait_ge(P.sem["dv"], e1[1])
                    e2 = P.inc(vector.tensor_scalar_mul(
                        wp_cols[w][:, t:t + 1], wrec[:, 0:1], 1.0 / QMAX), "dv")
                    vector.wait_ge(P.sem["dv"], e2[1])
                    e3 = P.inc(vector.reciprocal(wrec[:, 8:9],
                                                 wp_cols[w][:, t:t + 1]), "dv")
                    vector.wait_ge(P.sem["dv"], e3[1])
                    vector.tensor_scalar(
                        wf32[par][:, 0:fd], wf32[par][:, 0:fd], wrec[:, 8:9], MAGIC,
                        ALU.mult, ALU.add)
                    wp["p1"][k] = P.inc(vector.tensor_scalar_sub(
                        wf32[par][:, 0:fd], wf32[par][:, 0:fd], MAGIC), "dv")

                @block.gpsimd
                def _(gpsimd, k=k, w=w, t=t, par=par, fd=fd):
                    P.wait(gpsimd, "gpsimd", wp["p1"][k])
                    wp["st"][k] = P.inc(gpsimd.dma_start(
                        out=wp_dst[w][t * 128:(t + 1) * 128, :],
                        in_=wf32[par][:, 0:fd]), {"g": "s_qg", "u": "s_qu",
                                                  "d": "s_qd"}[w], 16)
                    wp["stn"][w] = wp["st"][k]

        wprep_emit(2)

        xv = xfm.rearrange("(c p) n -> p c n", p=128)
        ev_xn = [None] * 4
        ev_xsl_use = [None, None]

        @block.scalar
        def _(scalar):
            for s_ in range(2):
                ev[f"xsl{s_}"] = P.inc(scalar.dma_start(
                    out=xsl[s_][:, 0:4 * 512],
                    in_=xv[:, 0:4, s_ * 512:(s_ + 1) * 512]),
                    f"s_x{s_ % 2}", 16)

        @block.sync
        def _(sync):
            for s_ in range(2):
                ev[f"xslb{s_}"] = P.inc(sync.dma_start(
                    out=xsl[s_][:, 4 * 512:8 * 512],
                    in_=xv[:, 4:8, s_ * 512:(s_ + 1) * 512]),
                    "s_const", 16)

        for sl in range(4):
            par = sl % 2
            for c in range(DCH):
                @block.scalar
                def _(scalar, sl=sl, par=par, c=c):
                    P.wait(scalar, "scalar",
                           ev[f"xsl{sl}"] if c < 4 else ev[f"xslb{sl}"])
                    key = (f"sqmm_{sl}_{c - 2}" if c >= 2
                           else f"sqmm_{sl - 1}_{DCH + c - 2}")
                    P.wait(scalar, "scalar", ev.get(key))
                    ev[f"sq_{sl}_{c}"] = P.inc(scalar.activation(
                        fst[2 + c % 2][:], xsl[par][:, c * 512:(c + 1) * 512],
                        AF.Square), "ac")

                @block.tensor
                def _(tensor, sl=sl, par=par, c=c):
                    if c == 0:
                        P.wait(tensor, "tensor", bank_free[0 + par], bank_free[2 + par])
                        P.wait(tensor, "tensor", ev[f"xsl{sl}"])
                    if c == 4:
                        P.wait(tensor, "tensor", ev[f"xslb{sl}"])
                    ev[f"smm_{sl}_{c}"] = P.inc(
                        tensor.matmul(PB[0 + par][0:1, :], ones_c[:],
                                      xsl[par][:, c * 512:(c + 1) * 512],
                                      start=(c == 0), stop=(c == DCH - 1)), "pe")
                    P.wait(tensor, "tensor", ev[f"sq_{sl}_{c}"])
                    ev[f"sqmm_{sl}_{c}"] = P.inc(
                        tensor.matmul(PB[2 + par][0:1, :], ones_c[:], fst[2 + c % 2][:],
                                      start=(c == 0), stop=(c == DCH - 1)), "pe")
            ev[f"stats_{sl}"] = ev[f"sqmm_{sl}_{DCH - 1}"]

            rm = rows[0:1, 0:512]
            rv = rows[0:1, 512:1024]
            rr = rows[0:1, 1024:1536]

            @block.vector
            def _(vector, sl=sl, par=par):
                P.wait(vector, "vector", ev[f"stats_{sl}"])
                vector.tensor_scalar_mul(rm, PB[0 + par][0:1, :], 1.0 / D)
                vector.tensor_scalar_mul(rv, PB[2 + par][0:1, :], 1.0 / D)
                vector.tensor_tensor(rr, rm, rm, ALU.mult)
                ev[f"rows_v_{sl}"] = P.inc(vector.scalar_tensor_tensor(
                    rv, rv, 1e-5, rr, ALU.add, ALU.subtract), "dv")

            @block.scalar
            def _(scalar, sl=sl):
                P.wait(scalar, "scalar", ev[f"rows_v_{sl}"])
                scalar.activation(rr, rv, AF.Ln)
                ev[f"rows_ln_{sl}"] = P.inc(scalar.activation(rv, rr, AF.Exp, scale=-0.5),
                                            "ac")

            @block.vector
            def _(vector, sl=sl):
                P.wait(vector, "vector", ev[f"rows_ln_{sl}"])
                ev[f"rows_m_{sl}"] = P.inc(vector.tensor_tensor(rm, rm, rv, ALU.mult),
                                           "dv")

            @block.tensor
            def _(tensor, sl=sl, par=par):
                P.wait(tensor, "tensor", ev[f"rows_m_{sl}"],
                       bank_free[4 + par], bank_free[6 + par])
                tensor.matmul(PB[4 + par][:, :], ones_r[:], rv, start=True, stop=True)
                ev[f"bc_{sl}"] = P.inc(tensor.matmul(PB[6 + par][:, :], ones_r[:], rm,
                                                     start=True, stop=True), "pe")

            @block.vector
            def _(vector, sl=sl, par=par):
                P.wait(vector, "vector", ev[f"bc_{sl}"])
                e = None
                for c in range(DCH):
                    f = fst[c % 2]
                    vector.tensor_tensor(f[:], xsl[par][:, c * 512:(c + 1) * 512],
                                         PB[4 + par][:, :], ALU.mult)
                    e = P.inc(vector.tensor_tensor(
                        xn_qx[:, c * T + sl * 512: c * T + (sl + 1) * 512],
                        f[:], PB[6 + par][:, :], ALU.subtract), "dv")
                ev_xn[sl] = e
                ev_xsl_use[par] = e
                bank_free[4 + par] = e
                bank_free[6 + par] = e
                bank_free[0 + par] = ev[f"rows_v_{sl}"]
                bank_free[2 + par] = ev[f"rows_v_{sl}"]

            if sl + 2 < 4:
                @block.scalar
                def _(scalar, sl=sl, par=par):
                    P.wait(scalar, "scalar", ev_xsl_use[par])
                    ev[f"xsl{sl + 2}"] = P.inc(scalar.dma_start(
                        out=xsl[par][:], in_=xv[:, :, (sl + 2) * 512:(sl + 3) * 512]),
                        f"s_x{par}", 16)
                    ev[f"xslb{sl + 2}"] = ev[f"xsl{sl + 2}"]

            wprep_emit(3)

        cm_xsl1.__exit__(None, None, None)
        cm_wq, wq_sb = alloc("wq_sb", [128, DCH * 1024], BF16)

        @block.gpsimd
        def _(gpsimd):
            P.wait(gpsimd, "gpsimd", ev_xn[0], ev_xn[1], ev_xn[2], ev_xn[3])
            eR = P.inc(gpsimd.tensor_reduce(mxs[0:1, 0:1], xn_qx[:, :], AX.XYZWC,
                                            ALU.abs_max), "gp")
            gpsimd.wait_ge(P.sem["gp"], eR[1])
            P.inc(gpsimd.dma_start(out=mx1_in.ap(), in_=mxs[0:1, :]), "s_mx", 16)
            P.wait(gpsimd, "gpsimd", P.now("s_mx"))
            P.inc(gpsimd.collective_compute(
                "AllReduce", ALU.max, replica_groups=RG,
                ins=[mx1_in.ap().opt()], outs=[mx1_out.ap().opt()]), "cc")
            P.wait(gpsimd, "gpsimd", P.now("cc"))
            ev["mxg"] = P.inc(gpsimd.dma_start(out=mxg[:], in_=mx1_out.ap()), "s_mx", 16)

        if tap == "xn":
            @block.gpsimd
            def _(gpsimd):
                P.wait(gpsimd, "gpsimd", ev_xn[3])
                P.inc(gpsimd.dma_start(
                    out=out.rearrange("(c p) n -> p c n", p=128), in_=xn_qx[:].rearrange("p (c n) -> p c n", c=DCH)),
                    "so", 16)
                gpsimd.wait_ge(P.sem["so"], 16)
            return nc

        ev_ktm = [None, None]
        ev_vtm = [None, None]
        ev_kvzmm = [None, None]
        ev_ke = [None, None]

        def emit_kvz(nt):
            par = nt % 2

            @block.tensor
            def _(tensor, nt=nt, par=par):
                P.wait(tensor, "tensor", ev_ktm[par], ev_vtm[par])
                if nt == 0:
                    P.wait(tensor, "tensor", bank_free[4], bank_free[5],
                           bank_free[6])
                for h in range(NH):
                    pb = PB[4 + h // 8]
                    cs = (h % 8) * 64
                    tensor.matmul(pb[0:64, cs:cs + 64],
                                  ktm[par][:, h * 64:(h + 1) * 64],
                                  vtm[par][:, h * 64:(h + 1) * 64],
                                  start=(nt == 0 and h % 8 == 0),
                                  stop=(nt == 15 and h % 8 == 7),
                                  skip_group_check=True)
                e = None
                for h in range(NH):
                    e = P.inc(tensor.matmul(
                        PB[6][0:64, h:h + 1], ktm[par][:, h * 64:(h + 1) * 64],
                        ones_b[:], start=(nt == 0 and h == 0),
                        stop=(nt == 15 and h == 15),
                        skip_group_check=True), "pe")
                ev_kvzmm[par] = e

        for nt in range(16):
            par = nt % 2
            b0, b1 = 0 + 2 * par, 1 + 2 * par

            @block.tensor
            def _(tensor, nt=nt, par=par, b0=b0, b1=b1):
                P.wait(tensor, "tensor", ev["wk"], ev_xn[nt // 4],
                       bank_free[b0], bank_free[b1])
                e = None
                for ih in range(2):
                    for c in range(DCH):
                        e = P.inc(tensor.matmul(
                            PB[b0 + ih][:, :],
                            xn_qx[:, c * T + nt * 128: c * T + (nt + 1) * 128],
                            wa[0][:, c * 1024 + ih * 512: c * 1024 + (ih + 1) * 512],
                            start=(c == 0), stop=(c == DCH - 1)), "pe")
                ev[f"kmm_{nt}"] = e

            @block.vector
            def _(vector, nt=nt, par=par, b0=b0):
                P.wait(vector, "vector", ev[f"kmm_{nt}"], ev["bbce1"],
                       ev_kvzmm[par], ev_ke[1 - par])
                e = None
                for ih in range(2):
                    e = P.inc(vector.tensor_tensor(
                        fst[2 + ih][:], PB[b0 + ih][:, :],
                        bk_bc[:, ih * 512:(ih + 1) * 512], ALU.add), "dv")
                ev[f"kt_{nt}"] = e

            @block.vector
            def _(vector, nt=nt):
                e = None
                for ih in range(2):
                    e = P.inc(vector.tensor_scalar_max(bst[2 + ih][:],
                                                       fst[2 + ih][:], 0.0), "dv")
                ev[f"kr_{nt}"] = e

            @block.scalar
            def _(scalar, nt=nt):
                P.wait(scalar, "scalar", ev[f"kt_{nt}"], ev[f"kr_{nt}"])
                e = None
                for ih in range(2):
                    scalar.activation(fst[2 + ih][:], fst[2 + ih][:], AF.Relu, scale=-1.0)
                for ih in range(2):
                    e = P.inc(scalar.activation(bst[ih][:], fst[2 + ih][:],
                                                AF.Exp, scale=-1.0), "ac")
                ev[f"ke_{nt}"] = e
                ev_ke[nt % 2] = e

            @block.vector
            def _(vector, nt=nt, par=par, b0=b0, b1=b1):
                P.wait(vector, "vector", ev[f"ke_{nt}"])
                e = None
                for ih in range(2):
                    e = P.inc(vector.tensor_tensor(
                        ktm[par][:, ih * 512:(ih + 1) * 512], bst[ih][:],
                        bst[2 + ih][:], ALU.add), "dv")
                ev_ktm[par] = e
                bank_free[b0] = e
                bank_free[b1] = e

            @block.tensor
            def _(tensor, nt=nt, par=par, b0=b0, b1=b1):
                P.wait(tensor, "tensor", ev["wv"], bank_free[b0], bank_free[b1])
                e = None
                for ih in range(2):
                    for c in range(DCH):
                        e = P.inc(tensor.matmul(
                            PB[b0 + ih][:, :],
                            xn_qx[:, c * T + nt * 128: c * T + (nt + 1) * 128],
                            wa[1][:, c * 1024 + ih * 512: c * 1024 + (ih + 1) * 512],
                            start=(c == 0), stop=(c == DCH - 1)), "pe")
                ev[f"vmm_{nt}"] = e

            @block.vector
            def _(vector, nt=nt, par=par, b0=b0, b1=b1):
                P.wait(vector, "vector", ev[f"vmm_{nt}"])
                e = None
                for ih in range(2):
                    e = P.inc(vector.tensor_tensor(
                        vtm[par][:, ih * 512:(ih + 1) * 512], PB[b0 + ih][:, :],
                        bv_bc[:, ih * 512:(ih + 1) * 512], ALU.add), "dv")
                ev_vtm[par] = e
                bank_free[b0] = e
                bank_free[b1] = e

            if nt >= 1:
                emit_kvz(nt - 1)
            wprep_emit(2)
            if nt == 10:
                @block.gpsimd
                def _(gpsimd):
                    P.wait(gpsimd, "gpsimd", ev_xn[3])
                    ev["wq"] = P.inc(gpsimd.dma_start(
                        out=wq_sb[:].rearrange("p (c n) -> p c n", c=DCH),
                        in_=wqT.rearrange("(c p) n -> p c n", p=128)), "s_wa", 16)
        emit_kvz(15)
        ev["kvzdone"] = ev_kvzmm[1]

        @block.vector
        def _(vector):
            P.wait(vector, "vector", ev["kvzdone"], ev["memset"], ev["bbce1"])
            e0 = None
            for h in range(NH):
                pb = PB[4 + h // 8]
                cs = (h % 8) * 64
                vector.tensor_copy(kvz_sb[:, h * 66:h * 66 + 64], pb[0:64, cs:cs + 64])
            for h in range(NH):
                e0 = P.inc(vector.tensor_copy(kvz_sb[:, h * 66 + 64:h * 66 + 65],
                                              PB[6][0:64, h:h + 1]), "dv")
            bank_free[4] = e0
            bank_free[5] = e0
            bank_free[6] = e0
            ev["kvz_evac"] = e0
            ev["kvzm0"] = P.inc(vector.tensor_scalar_mul(kvz_m[:], kvz_sb[:],
                                                         bsel[0:64, 0:1]), "dv")

        if tap == "kvz0":
            @block.gpsimd
            def _(gpsimd):
                P.wait(gpsimd, "gpsimd", ev["kvz_evac"])
                P.inc(gpsimd.dma_start(out=out[0:64, 0:1056], in_=kvz_sb[:]), "so", 16)
                gpsimd.wait_ge(P.sem["so"], 16)
            return nc

        @block.gpsimd
        def _(gpsimd):
            P.wait(gpsimd, "gpsimd", ev["kvzm0"])
            e = P.inc(gpsimd.dma_start(out=kvz_in.ap()[0:64, :], in_=kvz_m[:]),
                      "s_kvz", 16)
            ev["kvzs0"] = e

        @block.vector
        def _(vector):
            P.wait(vector, "vector", ev["kvzs0"])
            ev["kvzm1"] = P.inc(vector.tensor_scalar_mul(kvz_m[:], kvz_sb[:],
                                                         bsel[0:64, 1:2]), "dv")

        @block.gpsimd
        def _(gpsimd):
            P.wait(gpsimd, "gpsimd", ev["kvzm1"])
            P.inc(gpsimd.dma_start(out=kvz_in.ap()[64:128, :], in_=kvz_m[:]),
                  "s_kvz", 16)
            P.wait(gpsimd, "gpsimd", P.now("s_kvz"))
            P.inc(gpsimd.collective_compute(
                "AllReduce", ALU.add, replica_groups=RG,
                ins=[kvz_in.ap().opt()], outs=[kvz_out.ap().opt()]), "cc")
            P.wait(gpsimd, "gpsimd", P.now("cc"))
            P.inc(gpsimd.dma_start(out=kvz_sb[:], in_=kvz_out.ap()[0:64, :]),
                  "s_kvz", 16)
            ev["kvzr"] = P.inc(gpsimd.dma_start(
                out=kvz_m[:], in_=kvz_out.ap()[64:128, :]), "s_kvz", 16)

        @block.vector
        def _(vector):
            P.wait(vector, "vector", ev["kvzr"])
            vector.tensor_scalar_mul(kvz_sb[:], kvz_sb[:], bsel[0:64, 0:1])
            vector.scalar_tensor_tensor(kvz_sb[:], kvz_m[:], bsel[0:64, 1:2], kvz_sb[:],
                                        ALU.mult, ALU.add)
            ev["kvzbf"] = P.inc(vector.tensor_copy(kvz_bf[0:64, :], kvz_sb[:]), "dv")

        @block.gpsimd
        def _(gpsimd):
            P.wait(gpsimd, "gpsimd", ev["kvzbf"])
            P.inc(gpsimd.dma_start(out=kvzbf_d.ap(), in_=kvz_bf[0:64, :]),
                  "s_kvz", 16)
            P.wait(gpsimd, "gpsimd", P.now("s_kvz"))
            ev["kvzbf2"] = P.inc(gpsimd.dma_start(out=kvz_bf[64:128, :],
                                                  in_=kvzbf_d.ap()), "s_kvz", 16)


        if tap == "kvz":
            @block.gpsimd
            def _(gpsimd):
                P.wait(gpsimd, "gpsimd", ev["kvzbf"])
                P.inc(gpsimd.dma_start(out=out[0:64, 0:1056], in_=kvz_sb[:]), "so", 16)
                gpsimd.wait_ge(P.sem["so"], 16)
            return nc

        qit = -1
        for ich in range(DCH):
            for nt in range(4):
                qit += 1
                pb = qit % 4
                ip = qit % 2

                @block.tensor
                def _(tensor, ich=ich, nt=nt, pb=pb):
                    P.wait(tensor, "tensor", ev["wq"], bank_free[pb])
                    e = None
                    for c in range(DCH):
                        e = P.inc(tensor.matmul(
                            PB[pb][:, :],
                            wq_sb[:, c * 1024 + ich * 128: c * 1024 + (ich + 1) * 128],
                            xn_qx[:, c * T + nt * 512: c * T + (nt + 1) * 512],
                            start=(c == 0), stop=(c == DCH - 1)), "pe")
                    ev[f"qmm_{qit}"] = e

                @block.vector
                def _(vector, ich=ich, qit=qit, pb=pb, ip=ip):
                    P.wait(vector, "vector", ev[f"qmm_{qit}"], ev.get(f"qe_{qit - 2}"))
                    ev[f"qt_{qit}"] = P.inc(vector.tensor_scalar(
                        fst[ip][:], PB[pb][:, :], bqc[:, ich:ich + 1], None, ALU.add),
                        "dv")

                if qit % 2 == 1:
                    @block.scalar
                    def _(scalar, qit=qit):
                        P.wait(scalar, "scalar", ev[f"qt_{qit - 1}"],
                               ev[f"qt_{qit}"])
                        for q2 in (qit - 1, qit):
                            ip2 = q2 % 2
                            scalar.activation(fst[2 + ip2][:], fst[ip2][:],
                                              AF.Relu, scale=-1.0)
                        for q2 in (qit - 1, qit):
                            ip2 = q2 % 2
                            ev[f"qe_{q2}"] = P.inc(scalar.activation(
                                bst[ip2][:], fst[2 + ip2][:],
                                AF.Exp, scale=-1.0), "ac")

                if qit % 2 == 1:
                    @block.vector
                    def _(vector, qit=qit):
                        for q2 in (qit - 1, qit):
                            ip2 = q2 % 2
                            pb2 = q2 % 4
                            ich2, nt2 = q2 // 4, q2 % 4
                            P.wait(vector, "vector", ev[f"qe_{q2}"])
                            vector.tensor_scalar_max(bst[2 + ip2][:],
                                                     fst[ip2][:], 0.0)
                            e = P.inc(vector.tensor_tensor(
                                q_a[:, ich2 * T + nt2 * 512:
                                    ich2 * T + (nt2 + 1) * 512],
                                bst[ip2][:], bst[2 + ip2][:], ALU.add), "dv")
                            ev[f"qph_{q2}"] = e
                            bank_free[pb2] = e
            if ich % 2 == 0:
                wprep_emit(2)
        ev["qdone"] = ev[f"qph_{qit}"]

        if tap == "q":
            @block.gpsimd
            def _(gpsimd):
                P.wait(gpsimd, "gpsimd", ev["qdone"])
                P.inc(gpsimd.dma_start(
                    out=out.rearrange("(c p) n -> p c n", p=128), in_=q_a[:].rearrange("p (c n) -> p c n", c=DCH)),
                    "so", 16)
                gpsimd.wait_ge(P.sem["so"], 16)
            return nc

        wprep_emit(len(wprep_tiles))
        assert wp["cur"] == len(wprep_tiles)

        if tap == "w52":
            @block.gpsimd
            def _(gpsimd):
                P.wait(gpsimd, "gpsimd", wp["st"][len(wprep_tiles) - 1],
                       wp["st"][len(wprep_tiles) - 2], ev["qdone"], ev["mxg"])
                P.inc(gpsimd.dma_start(out=out[0:128, 0:8], in_=wsd[:, 0:8]), "so", 16)
                gpsimd.wait_ge(P.sem["so"], 16)
            return nc

        @block.vector
        def _(vector):
            P.wait(vector, "vector", ev["mxg"], ev["sig"])
            e1 = P.inc(vector.tensor_tensor(srow[0:1, 2:3], srow[0:1, 1:2],
                                            mxg[0:1, 0:1], ALU.mult), "dv")
            vector.wait_ge(P.sem["dv"], e1[1])
            e2 = P.inc(vector.tensor_scalar_mul(srow[0:1, 3:4], srow[0:1, 2:3],
                                                1.0 / QMAX), "dv")
            vector.wait_ge(P.sem["dv"], e2[1])
            ev["rx"] = P.inc(vector.reciprocal(srow[0:1, 4:5], srow[0:1, 3:4]), "dv")

        @block.tensor
        def _(tensor):
            P.wait(tensor, "tensor", ev["rx"], bank_free[4])
            tensor.matmul(PB[4][:, 0:1], ones_r[:], srow[0:1, 4:5], start=True,
                          stop=True)
            ev["rxbc"] = P.inc(tensor.matmul(PB[4][:, 1:2], ones_r[:], srow[0:1, 3:4],
                                             start=True, stop=True), "pe")

        @block.vector
        def _(vector):
            P.wait(vector, "vector", ev["rxbc"])
            vector.tensor_copy(rcol[:, 0:1], PB[4][:, 0:1])
            e = P.inc(vector.tensor_copy(rcol[:, 1:2], PB[4][:, 1:2]), "dv")
            bank_free[4] = e
            vector.wait_ge(P.sem["dv"], e[1])
            vector.tensor_scalar_mul(gs_c[:], wsg[:], rcol[:, 1:2])
            ev["gscols"] = P.inc(vector.tensor_scalar_mul(us_c[:], wsu[:], rcol[:, 1:2]),
                                 "dv")

        if tap == "rx":
            @block.gpsimd
            def _(gpsimd):
                P.wait(gpsimd, "gpsimd", ev["gscols"], ev["qdone"],
                       wp["st"][len(wprep_tiles) - 1], wp["st"][len(wprep_tiles) - 2])
                P.inc(gpsimd.dma_start(out=out[0:128, 0:22], in_=gs_c[:, :]), "so", 16)
                gpsimd.wait_ge(P.sem["so"], 16)
            return nc

        ev_qx = [None] * DCH
        for c in range(DCH):
            @block.vector
            def _(vector, c=c):
                P.wait(vector, "vector", ev["gscols"], ev.get(f"qxp_{c - 2}"))
                ev[f"qxv_{c}"] = P.inc(vector.tensor_scalar(
                    xsl[0][:, (c % 2) * 2048:(c % 2) * 2048 + 2048],
                    xn_qx[:, c * T:(c + 1) * T], rcol[:, 0:1], MAGIC,
                    ALU.mult, ALU.add), "dv")

            @block.vector
            def _(vector, c=c):
                vector.tensor_scalar(
                    xsl[0][:, (c % 2) * 2048:(c % 2) * 2048 + 2048],
                    xsl[0][:, (c % 2) * 2048:(c % 2) * 2048 + 2048],
                    MAGIC, QMAX, ALU.subtract, ALU.min)
                e = P.inc(vector.tensor_scalar_max(
                    xn_qx[:, c * T:(c + 1) * T],
                    xsl[0][:, (c % 2) * 2048:(c % 2) * 2048 + 2048],
                    -(QMAX + 1.0)), "dv")
                ev[f"qxp_{c}"] = e
                ev_qx[c] = e

        if tap == "qx":
            @block.gpsimd
            def _(gpsimd):
                P.wait(gpsimd, "gpsimd", *[ev_qx[c] for c in range(DCH)])
                P.inc(gpsimd.dma_start(
                    out=out.rearrange("(c p) n -> p c n", p=128), in_=xn_qx[:].rearrange("p (c n) -> p c n", c=DCH)),
                    "so", 16)
                gpsimd.wait_ge(P.sem["so"], 16)
            return nc

        cm_wq.__exit__(None, None, None)
        cm_vtm1.__exit__(None, None, None)
        cm_vtm0.__exit__(None, None, None)
        cm_ktm1.__exit__(None, None, None)
        cm_ktm0.__exit__(None, None, None)
        cm_xsl0.__exit__(None, None, None)
        cm_wf1.__exit__(None, None, None)
        cm_wa0.__exit__(None, None, None)

        cm_qgT0, qgT0 = alloc("qgT0", [128, 4096], BF16)
        cm_quT0, quT0 = alloc("quT0", [128, 4096], BF16)
        cm_hst0, hst0 = alloc("hst0", [128, T], BF16)
        cm_hst1, hst1 = alloc("hst1", [128, T], BF16)
        hstage = [hst0, hst1]
        cm_qgT1, qgT1 = alloc("qgT1", [128, 4096], BF16)
        cm_quT1, quT1 = alloc("quT1", [128, 4096], BF16)
        qgT = [qgT0, qgT1]
        quT = [quT0, quT1]

        GROUPS = [(0, 1), (1, 1), (2, 2), (4, 4), (8, 4), (12, 4), (16, 4), (20, 2)]
        ev_gld = [None] * len(GROUPS)
        ev_gu_use = [None, None]

        def emit_gload(g):
            t0, ntile = GROUPS[g]
            par = g % 2

            @block.sync
            def _(sync, g=g, t0=t0, ntile=ntile, par=par):
                P.wait(sync, "sync", wp["st"].get(t0 + ntile - 1),
                       wp["st"].get(ITCH + t0 + ntile - 1),
                       ev["kmm_15"], ev_gu_use[par])
                if par == 1:
                    P.wait(sync, "sync", wp["st"].get(len(wprep_tiles) - 1),
                           ev_qx[DCH - 1], ev_xn[3])
                rows_ = ntile * 128
                e = None
                for c in range(DCH):
                    P.inc(sync.dma_start(
                        out=qgT[par][:, c * 512:c * 512 + rows_],
                        in_=qg_d[t0 * 128:t0 * 128 + rows_, c * 128:(c + 1) * 128],
                        transpose=True), f"s_gl{par}", 16)
                    e = P.inc(sync.dma_start(
                        out=quT[par][:, c * 512:c * 512 + rows_],
                        in_=qu_d[t0 * 128:t0 * 128 + rows_, c * 128:(c + 1) * 128],
                        transpose=True), f"s_gl{par}", 16)
                ev_gld[g] = e

        emit_gload(0)
        ev_h = {}
        for g, (t0, ntile) in enumerate(GROUPS):
            par = g % 2
            if g + 1 < len(GROUPS):
                emit_gload(g + 1)
            for tl in range(ntile):
                it = t0 + tl
                hpar = it % 2
                for nt in range(4):
                    bg = (nt % 2) * 2
                    bu = bg + 1

                    @block.tensor
                    def _(tensor, g=g, par=par, tl=tl, it=it, nt=nt, bg=bg, bu=bu,
                          ntile=ntile):
                        P.wait(tensor, "tensor", ev_gld[g], ev_qx[DCH - 1],
                               bank_free[bg], bank_free[bu])
                        for c in range(DCH):
                            tensor.matmul(
                                PB[bg][:, :],
                                qgT[par][:, c * 512 + tl * 128: c * 512 + (tl + 1) * 128],
                                xn_qx[:, c * T + nt * 512: c * T + (nt + 1) * 512],
                                start=(c == 0), stop=(c == DCH - 1))
                        e = None
                        for c in range(DCH):
                            e = P.inc(tensor.matmul(
                                PB[bu][:, :],
                                quT[par][:, c * 512 + tl * 128: c * 512 + (tl + 1) * 128],
                                xn_qx[:, c * T + nt * 512: c * T + (nt + 1) * 512],
                                start=(c == 0), stop=(c == DCH - 1)), "pe")
                        ev[f"gumm_{it}_{nt}"] = e
                        if tl == ntile - 1 and nt == 3:
                            ev_gu_use[par] = e

                    @block.scalar
                    def _(scalar, it=it, nt=nt, bg=bg):
                        P.wait(scalar, "scalar", ev[f"gumm_{it}_{nt}"], ev["gscols"],
                               ev.get(f"h_{it}_{nt - 2}") if nt >= 2
                               else ev.get(f"h_{it - 1}_{nt + 2}"))
                        ev[f"sg_{it}_{nt}"] = P.inc(scalar.activation(
                            fst[nt % 2][:], PB[bg][:, :], AF.Silu,
                            scale=gs_c[:, it:it + 1]), "ac")

                    @block.vector
                    def _(vector, it=it, nt=nt, hpar=hpar, bg=bg, bu=bu):
                        P.wait(vector, "vector", ev[f"sg_{it}_{nt}"])
                        if it == 0 and nt == 0:
                            P.wait(vector, "vector",
                                   wp["st"].get(len(wprep_tiles) - 1))
                        if nt == 0:
                            P.wait(vector, "vector", ev_h.get(it - 2))
                        e = P.inc(vector.scalar_tensor_tensor(
                            hstage[hpar][:, nt * 512:(nt + 1) * 512],
                            PB[bu][:, :], us_c[:, it:it + 1], fst[nt % 2][:],
                            ALU.mult, ALU.mult), "dv")
                        ev[f"h_{it}_{nt}"] = e
                        bank_free[bg] = e
                        bank_free[bu] = e

                @block.gpsimd
                def _(gpsimd, it=it, hpar=hpar):
                    P.wait(gpsimd, "gpsimd", ev[f"h_{it}_3"])
                    if it > 0:
                        P.wait(gpsimd, "gpsimd", ev[f"hmxa_{it - 1}"])
                    ev[f"hmx_{it}"] = P.inc(gpsimd.tensor_reduce(
                        hrun[0:1, 1:2], hstage[hpar][:, :],
                        AX.XYZWC, ALU.abs_max), "gp")

                @block.vector
                def _(vector, it=it):
                    P.wait(vector, "vector", ev[f"hmx_{it}"])
                    ev[f"hmxa_{it}"] = P.inc(vector.tensor_tensor(
                        hrun[0:1, 0:1], hrun[0:1, 0:1], hrun[0:1, 1:2], ALU.max), "dv")

                @block.sync
                def _(sync, it=it, hpar=hpar):
                    P.wait(sync, "sync", ev[f"hmx_{it}"])
                    ev_h[it] = P.inc(sync.dma_start(
                        out=h_d[it * 128:(it + 1) * 128, :], in_=hstage[hpar][:, :]),
                        f"s_h{it % 2}", 16)

        @block.gpsimd
        def _(gpsimd):
            P.wait(gpsimd, "gpsimd", ev[f"hmxa_{ITCH - 1}"])
            eC = P.inc(gpsimd.tensor_copy(mxs2[0:1, 0:1], hrun[0:1, 0:1]), "gp")
            gpsimd.wait_ge(P.sem["gp"], eC[1])
            P.inc(gpsimd.dma_start(out=mx2_in.ap(), in_=mxs2[0:1, :]), "s_mx", 16)
            P.wait(gpsimd, "gpsimd", P.now("s_mx"))
            P.inc(gpsimd.collective_compute(
                "AllReduce", ALU.max, replica_groups=RG,
                ins=[mx2_in.ap().opt()], outs=[mx2_out.ap().opt()]), "cc")
            P.wait(gpsimd, "gpsimd", P.now("cc"))
            ev["mxg2"] = P.inc(gpsimd.dma_start(out=mxg2[:], in_=mx2_out.ap()), "s_mx", 16)

        if tap == "gup":
            @block.gpsimd
            def _(gpsimd):
                P.wait(gpsimd, "gpsimd", ev_h[ITCH - 1], ev_h[ITCH - 2], ev["mxg2"])
                P.inc(gpsimd.dma_start(out=out[0:128, 0:8], in_=hrun[:, 0:8]), "so", 16)
                gpsimd.wait_ge(P.sem["so"], 16)
            return nc

        cm_quT1.__exit__(None, None, None)
        cm_qgT1.__exit__(None, None, None)
        cm_hst1.__exit__(None, None, None)
        cm_hst0.__exit__(None, None, None)
        cm_quT0.__exit__(None, None, None)
        cm_qgT0.__exit__(None, None, None)
        cm_qdT, qdT = alloc("qdT", [128, ITCH * 1024], BF16)
        cm_rtm, rec_tm = alloc("rec_tm", [128, 64], F32)
        cm_rtm2, rec_tm2 = alloc("rec_tm2", [128, 64], F32)
        cm_recT, recT = alloc("recT", [16, T], BF16)
        cm_xo0, xo0 = alloc("xo0", [128, T], F32)
        xoch = [xo0, xo0]

        @block.sync
        def _(sync):
            P.wait(sync, "sync", wp["stn"].get("d"), ev_gu_use[0], ev_gu_use[1],
                   ev_h.get(ITCH - 1), ev_h.get(ITCH - 2))
            e = None
            for ic in range(ITCH):
                e = P.inc(sync.dma_start(out=qdT[:, ic * 1024:(ic + 1) * 1024],
                                         in_=qd_d[:, ic * 128:(ic + 1) * 128],
                                         transpose=True), "s_qdT", 16)
            ev["qdT"] = e

        @block.gpsimd
        def _(gpsimd):
            P.wait(gpsimd, "gpsimd", ev["kvzdone"])
            ev["wo"] = P.inc(gpsimd.dma_start(
                out=wa[1][:].rearrange("p (c n) -> p c n", c=DCH), in_=woT.rearrange("(c p) n -> p c n", p=128)), "s_wb", 16)

        @block.scalar
        def _(scalar):
            P.wait(scalar, "scalar", ev_gu_use[0], ev_gu_use[1])
            ev["xo0"] = P.inc(scalar.dma_start(out=xoch[0][:], in_=xfm[0:128, :]),
                              "s_xo0", 16)
            ev["xo1"] = ev["xo0"]
        ev_xoch = [ev["xo0"]] + [None] * (DCH - 1)

        if tap == "preattn":
            @block.gpsimd
            def _(gpsimd):
                P.wait(gpsimd, "gpsimd", ev["qdT"], ev["wo"], ev["xo1"], ev["kvzbf2"])
                P.inc(gpsimd.dma_start(out=out[0:128, 0:1024],
                                       in_=qdT[:, 0:1024]), "so", 16)
                gpsimd.wait_ge(P.sem["so"], 16)
            return nc

        for nt in range(4):
            @block.tensor
            def _(tensor, nt=nt):
                P.wait(tensor, "tensor", ev["kvzbf2"], ev["qdone"], bank_free[4])
                P.wait(tensor, "tensor", ev["xo0"],
                       ev["xo1"], ev_h[ITCH - 1], ev_h[ITCH - 2])
                e = None
                for sub in range(4):
                    for h in range(NH):
                        e = P.inc(tensor.matmul(
                            PB[4][:, sub * 16 + h:sub * 16 + h + 1],
                            q_a[(h % 2) * 64:(h % 2) * 64 + 64,
                                (h // 2) * T + nt * 512 + sub * 128:
                                (h // 2) * T + nt * 512 + (sub + 1) * 128],
                            kvz_bf[(h % 2) * 64:(h % 2) * 64 + 64,
                                   h * 66 + 64:h * 66 + 65],
                            start=True, stop=True), "pe")
                ev[f"znorm_{nt}"] = e

            if tap is not None and tap.startswith("znorm") and nt == 0:
                @block.vector
                def _(vector):
                    P.wait(vector, "vector", ev["znorm_0"])
                    P.inc(vector.tensor_copy(fst[0][:, 0:64], PB[4][:, 0:64]), "dv")
                    ev["zdump"] = P.now("dv")

                @block.gpsimd
                def _(gpsimd):
                    P.wait(gpsimd, "gpsimd", ev["zdump"])
                    P.inc(gpsimd.dma_start(out=out[0:128, 0:512], in_=fst[0][:]),
                          "so", 16)
                    gpsimd.wait_ge(P.sem["so"], 16)
                return nc

            @block.vector
            def _(vector, nt=nt):
                P.wait(vector, "vector", ev[f"znorm_{nt}"], ev_h[ITCH - 1],
                       ev_h[ITCH - 2], ev_gu_use[0], ev_gu_use[1])
                vector.tensor_scalar_add(rec_tm[:, :], PB[4][:, 0:64], 1e-6)
                e = P.inc(vector.reciprocal(rec_tm2[:, :], rec_tm[:, :]), "dv")
                ev[f"rec_{nt}"] = e
                bank_free[4] = e

            for sub in range(4):
                @block.tensor
                def _(tensor, nt=nt, sub=sub):
                    P.wait(tensor, "tensor", ev[f"rec_{nt}"], bank_free[5])
                    ev[f"recT_{nt}_{sub}"] = P.inc(tensor.transpose(
                        PB[5][0:16, 0:128],
                        rec_tm2[:, sub * 16:(sub + 1) * 16], ident[:]), "pe")

                @block.vector
                def _(vector, nt=nt, sub=sub):
                    P.wait(vector, "vector", ev[f"recT_{nt}_{sub}"])
                    e = P.inc(vector.tensor_copy(
                        recT[:, nt * 512 + sub * 128:nt * 512 + (sub + 1) * 128],
                        PB[5][0:16, 0:128]), "dv")
                    bank_free[5] = e
            ev[f"recTe_{nt}"] = P.now("dv")

            if tap == "rec" and nt == 3:
                @block.gpsimd
                def _(gpsimd):
                    P.wait(gpsimd, "gpsimd", ev["recTe_3"])
                    P.inc(gpsimd.dma_start(out=out[0:16, 0:2048], in_=recT[:, :]),
                          "so", 16)
                    gpsimd.wait_ge(P.sem["so"], 16)
                return nc

            for e2 in range(DCH):
                pb = 6 + e2 % 2
                pa = e2 % 2

                @block.tensor
                def _(tensor, nt=nt, e2=e2, pb=pb):
                    P.wait(tensor, "tensor", ev[f"recTe_{nt}"], bank_free[pb])
                    ev[f"rbc_{nt}_{e2}"] = P.inc(tensor.matmul(
                        PB[pb][:, :], pairs[:, e2 * 128:(e2 + 1) * 128],
                        recT[:, nt * 512:(nt + 1) * 512], start=True, stop=True), "pe")

                @block.vector
                def _(vector, nt=nt, e2=e2, pb=pb):
                    P.wait(vector, "vector", ev[f"rbc_{nt}_{e2}"])
                    e = P.inc(vector.tensor_tensor(
                        q_a[:, e2 * T + nt * 512:e2 * T + (nt + 1) * 512],
                        q_a[:, e2 * T + nt * 512:e2 * T + (nt + 1) * 512],
                        PB[pb][:, :], ALU.mult), "dv")
                    ev[f"qp_{nt}_{e2}"] = e
                    bank_free[pb] = e

                @block.tensor
                def _(tensor, nt=nt, e2=e2, pa=pa):
                    P.wait(tensor, "tensor", ev[f"qp_{nt}_{e2}"], bank_free[pa])
                    e = None
                    for hh in range(2):
                        h = 2 * e2 + hh
                        e = P.inc(tensor.matmul(
                            PB[pa][hh * 64:(hh + 1) * 64, :],
                            kvz_bf[hh * 64:hh * 64 + 64, h * 66:h * 66 + 64],
                            q_a[hh * 64:(hh + 1) * 64,
                                e2 * T + nt * 512:e2 * T + (nt + 1) * 512],
                            start=True, stop=True), "pe")
                    ev[f"amm_{nt}_{e2}"] = e

                @block.vector
                def _(vector, nt=nt, e2=e2, pa=pa):
                    P.wait(vector, "vector", ev[f"amm_{nt}_{e2}"])
                    e = P.inc(vector.tensor_copy(
                        q_a[:, e2 * T + nt * 512:e2 * T + (nt + 1) * 512],
                        PB[pa][:, :]), "dv")
                    ev[f"ae_{nt}_{e2}"] = e
                    bank_free[pa] = e
        ev["adone"] = ev["ae_3_7"]

        if tap == "aphase":
            @block.gpsimd
            def _(gpsimd):
                P.wait(gpsimd, "gpsimd", ev["adone"])
                P.inc(gpsimd.dma_start(
                    out=out.rearrange("(c p) n -> p c n", p=128),
                    in_=q_a[:].rearrange("p (c n) -> p c n", c=DCH)), "so", 16)
                gpsimd.wait_ge(P.sem["so"], 16)
            return nc

        for oc in range(DCH):
            for nt in range(4):
                pb = (oc * 4 + nt) % 4

                @block.tensor
                def _(tensor, oc=oc, nt=nt, pb=pb):
                    P.wait(tensor, "tensor", ev["wo"], ev["adone"], bank_free[pb])
                    e = None
                    for c in range(DCH):
                        e = P.inc(tensor.matmul(
                            PB[pb][:, :],
                            wa[1][:, c * 1024 + oc * 128: c * 1024 + (oc + 1) * 128],
                            q_a[:, c * T + nt * 512: c * T + (nt + 1) * 512],
                            start=(c == 0), stop=(c == DCH - 1)), "pe")
                    ev[f"womm_{oc}_{nt}"] = e

                @block.scalar
                def _(scalar, oc=oc, nt=nt, pb=pb):
                    key = (f"attnx_{oc}_{nt - 2}" if nt >= 2
                           else f"attnx_{oc - 1}_{nt + 2}")
                    P.wait(scalar, "scalar", ev[f"womm_{oc}_{nt}"], ev.get(key))
                    ev[f"woe_{oc}_{nt}"] = P.inc(scalar.activation(
                        fst[2 + nt % 2][:], PB[pb][:, :], AF.Identity,
                        bias=boc[:, oc:oc + 1]), "ac")

                @block.vector
                def _(vector, oc=oc, nt=nt, pb=pb):
                    P.wait(vector, "vector", ev[f"woe_{oc}_{nt}"], ev_xoch[oc])
                    e = P.inc(vector.tensor_tensor(
                        xn_qx[:, oc * T + nt * 512: oc * T + (nt + 1) * 512],
                        fst[2 + nt % 2][:], xoch[0][:, nt * 512:(nt + 1) * 512],
                        ALU.add), "dv")
                    ev[f"attnx_{oc}_{nt}"] = e
                    bank_free[pb] = e

            if oc + 1 < DCH:
                @block.scalar
                def _(scalar, oc=oc):
                    P.wait(scalar, "scalar", ev[f"attnx_{oc}_3"])
                    ev_xoch[oc + 1] = P.inc(scalar.dma_start(
                        out=xoch[0][:],
                        in_=xfm[(oc + 1) * 128:(oc + 2) * 128, :]), "s_xo0", 16)
        ev["attnx"] = ev["attnx_7_3"]

        if tap == "attn":
            @block.gpsimd
            def _(gpsimd):
                P.wait(gpsimd, "gpsimd", ev["attnx"])
                P.inc(gpsimd.dma_start(
                    out=out.rearrange("(c p) n -> p c n", p=128), in_=xn_qx[:].rearrange("p (c n) -> p c n", c=DCH)),
                    "so", 16)
                gpsimd.wait_ge(P.sem["so"], 16)
            return nc

        @block.vector
        def _(vector):
            P.wait(vector, "vector", ev["mxg2"])
            e1 = P.inc(vector.tensor_tensor(srow[0:1, 10:11], srow[0:1, 9:10],
                                            mxg2[0:1, 0:1], ALU.mult), "dv")
            vector.wait_ge(P.sem["dv"], e1[1])
            e2 = P.inc(vector.tensor_scalar_mul(srow[0:1, 11:12], srow[0:1, 10:11],
                                                1.0 / QMAX), "dv")
            vector.wait_ge(P.sem["dv"], e2[1])
            ev["rh"] = P.inc(vector.reciprocal(srow[0:1, 12:13],
                                               srow[0:1, 11:12]), "dv")

        @block.tensor
        def _(tensor):
            P.wait(tensor, "tensor", ev["rh"], bank_free[4])
            tensor.matmul(PB[4][:, 0:1], ones_r[:], srow[0:1, 12:13], start=True,
                          stop=True)
            ev["rhbc"] = P.inc(tensor.matmul(PB[4][:, 1:2], ones_r[:],
                                             srow[0:1, 11:12], start=True, stop=True),
                               "pe")

        @block.vector
        def _(vector):
            P.wait(vector, "vector", ev["rhbc"])
            vector.tensor_copy(rcol[:, 1:2], PB[4][:, 0:1])
            e0 = P.inc(vector.tensor_copy(shcol[:, 0:1], PB[4][:, 1:2]), "dv")
            vector.wait_ge(P.sem["dv"], e0[1])
            e = P.inc(vector.tensor_scalar_mul(ds_c[:], wsd[:], shcol[:, 0:1]), "dv")
            ev["dscols"] = e
            bank_free[4] = e

        cm_xo0.__exit__(None, None, None)
        cm_recT.__exit__(None, None, None)
        cm_rtm2.__exit__(None, None, None)
        cm_rtm.__exit__(None, None, None)
        cm_qh0, qh0 = alloc("qh0", [128, ITCH * 512], BF16)
        qhb = [qh0, q_a]

        hv = h_d.rearrange("(c p) n -> p c n", p=128)
        ev_out = {}
        for nt in range(4):
            qh = qhb[nt % 2]

            @block.scalar
            def _(scalar, nt=nt, qh=qh):
                P.wait(scalar, "scalar", ev_h[ITCH - 1], ev_h[ITCH - 2])
                if nt < 2:
                    P.wait(scalar, "scalar", ev["attnx"])
                else:
                    P.wait(scalar, "scalar", ev[f"dnmm_{nt - 2}_7"])
                ev[f"qhld_{nt}"] = P.inc(scalar.dma_start(
                    out=qh[:, 0:ITCH * 512],
                    in_=hv[:, :, nt * 512:(nt + 1) * 512]), "s_qh", 16)

            for pc in range(4):
                csl = slice(pc * 2816, (pc + 1) * 2816)

                @block.vector
                def _(vector, nt=nt, pc=pc, csl=csl, qh=qh):
                    P.wait(vector, "vector", ev[f"qhld_{nt}"], ev["dscols"],
                           ev.get(f"qhq_{nt}_{pc - 1}")
                           if pc > 0 else ev.get(f"qhq_{nt - 1}_3"))
                    ev[f"qhv_{nt}_{pc}"] = P.inc(vector.tensor_scalar(
                        wf320[:, :], qh[:, csl], rcol[:, 1:2], MAGIC,
                        ALU.mult, ALU.add), "dv")

                @block.vector
                def _(vector, nt=nt, pc=pc, csl=csl, qh=qh):
                    vector.tensor_scalar(wf320[:, :], wf320[:, :],
                                         MAGIC, QMAX, ALU.subtract, ALU.min)
                    ev[f"qhq_{nt}_{pc}"] = P.inc(vector.tensor_scalar_max(
                        qh[:, csl], wf320[:, :], -(QMAX + 1.0)), "dv")

            for hc in range(DCH):
                pb = (nt * DCH + hc) % 8

                @block.tensor
                def _(tensor, nt=nt, hc=hc, pb=pb, qh=qh):
                    P.wait(tensor, "tensor", ev["qdT"], bank_free[pb],
                           *[ev[f"qhq_{nt}_{pc}"] for pc in range(4)])
                    e = None
                    for ic in range(ITCH):
                        e = P.inc(tensor.matmul(
                            PB[pb][:, :],
                            qdT[:, ic * 1024 + hc * 128: ic * 1024 + (hc + 1) * 128],
                            qh[:, ic * 512:(ic + 1) * 512],
                            start=(ic == 0), stop=(ic == ITCH - 1)), "pe")
                    ev[f"dnmm_{nt}_{hc}"] = e

                @block.vector
                def _(vector, nt=nt, hc=hc, pb=pb):
                    P.wait(vector, "vector", ev[f"dnmm_{nt}_{hc}"], ev["attnx"],
                           ev_out.get((nt, hc - 2)) if hc >= 2
                           else ev_out.get((nt - 1, hc + 6)))
                    e = P.inc(vector.scalar_tensor_tensor(
                        fst[hc % 2][:], PB[pb][:, :], ds_c[:, hc:hc + 1],
                        xn_qx[:, hc * T + nt * 512: hc * T + (nt + 1) * 512],
                        ALU.mult, ALU.add), "dv")
                    ev[f"oute_{nt}_{hc}"] = e
                    bank_free[pb] = e

                @block.sync
                def _(sync, nt=nt, hc=hc):
                    P.wait(sync, "sync", ev[f"oute_{nt}_{hc}"])
                    ev_out[(nt, hc)] = P.inc(sync.dma_start(
                        out=out[hc * 128:(hc + 1) * 128, nt * 512:(nt + 1) * 512],
                        in_=fst[hc % 2][:]), f"s_so{hc % 2}", 16)

        @block.sync
        def _(sync):
            sync.wait_ge(P.sem["s_so0"], P.cnt["s_so0"])
            sync.wait_ge(P.sem["s_so1"], P.cnt["s_so1"])

    return nc


_PAIRSEL = None


def _pairsel_const():
    global _PAIRSEL
    if _PAIRSEL is None:
        ps = np.zeros((16, 1024), dtype=np.float32)
        for e in range(8):
            for p in range(128):
                ps[2 * e + (1 if p >= 64 else 0), e * 128 + p] = 1.0
        _PAIRSEL = ps.astype(ml_dtypes.bfloat16)
    return _PAIRSEL


def kernel(L0_attn_x, L0_attn_Wq, L0_attn_bq, L0_attn_Wk, L0_attn_bk,
           L0_attn_Wv, L0_attn_bv, L0_attn_Wo, L0_attn_bo,
           L0_ffn_clip_x, L0_ffn_clip_down, L0_ffn_w_gate, L0_ffn_w_up,
           L0_ffn_w_down, _tap=None):
    f = np.float32
    x = np.asarray(L0_attn_x, dtype=f)
    shared = dict(
        wqT=np.ascontiguousarray(np.asarray(L0_attn_Wq, f).T),
        wkT=np.ascontiguousarray(np.asarray(L0_attn_Wk, f).T),
        wvT=np.ascontiguousarray(np.asarray(L0_attn_Wv, f).T),
        woT=np.ascontiguousarray(np.asarray(L0_attn_Wo, f).T),
        wg=np.ascontiguousarray(np.asarray(L0_ffn_w_gate, f)),
        wu=np.ascontiguousarray(np.asarray(L0_ffn_w_up, f)),
        wd=np.ascontiguousarray(np.asarray(L0_ffn_w_down, f)),
        bq_c=np.ascontiguousarray(np.asarray(L0_attn_bq, f).reshape(8, 128).T),
        bo_c=np.ascontiguousarray(np.asarray(L0_attn_bo, f).reshape(8, 128).T),
        bk_r=np.ascontiguousarray(np.asarray(L0_attn_bk, f).reshape(1, D)),
        bv_r=np.ascontiguousarray(np.asarray(L0_attn_bv, f).reshape(1, D)),
        clip2=np.array([[float(np.asarray(L0_ffn_clip_x).reshape(-1)[0]),
                         float(np.asarray(L0_ffn_clip_down).reshape(-1)[0])]],
                       dtype=f),
        identin=np.eye(128, dtype=f),
        pairselin=_pairsel_const(),
    )
    in_maps = []
    for c in range(NCORES):
        b, s0 = c // 4, (c % 4) * T
        m = dict(shared)
        m["xfm"] = np.ascontiguousarray(x[b, s0:s0 + T, :].T)
        bs = np.zeros((128, 2), dtype=f)
        bs[:, b] = 1.0
        m["bselin"] = bs
        in_maps.append(m)

    nc = build_program(tap=_tap)
    res = run_bass_kernel_spmd(nc, in_maps, core_ids=list(range(NCORES)))

    outp = np.empty((2, 8192, D), dtype=f)
    for c in range(NCORES):
        b, s0 = c // 4, (c % 4) * T
        outp[b, s0:s0 + T, :] = np.asarray(res.results[c]["out"]).T
    return outp


# version 27
# speedup vs baseline: 1.1127x; 1.0046x over previous
from contextlib import ExitStack

import numpy as np
import ml_dtypes

import concourse.bass as bass
import concourse.mybir as mybir
from concourse.bass_utils import run_bass_kernel_spmd

F32 = mybir.dt.float32
BF16 = mybir.dt.bfloat16
AF = mybir.ActivationFunctionType
ALU = mybir.AluOpType
AX = mybir.AxisListType

MAGIC = 12582912.0
NCORES = 8
T = 2048
D = 1024
DCH = 8
I = 2816
ITCH = 22
NH = 16
QMAX = 127.0


class Prog:
    ENG_OF = {"pe": "tensor", "dv": "vector", "ac": "scalar", "gp": "gpsimd"}

    def __init__(self, nc, ctx):
        self.sem = {}
        self.cnt = {}
        self.last_wait = {}
        for name in ["pe", "dv", "ac", "gp", "cc",
                     "s_const", "wpl0", "wpl1", "s_qg", "s_qu", "s_qd",
                     "s_wa", "s_wb", "s_x0", "s_x1", "s_mx", "s_kvz",
                     "s_gl0", "s_gl1", "s_qdT", "s_qh", "s_h0", "s_h1",
                     "s_xo0", "s_xo1", "s_so0", "s_so1", "so"]:
            self.sem[name] = ctx.enter_context(nc.semaphore(name))
            self.cnt[name] = 0

    def inc(self, instr, s, n=1):
        self.cnt[s] += n
        instr.then_inc(self.sem[s], n)
        return (s, self.cnt[s])

    def now(self, s):
        return (s, self.cnt[s])

    def wait(self, eng, engname, *evs):
        for ev in evs:
            if ev is None:
                continue
            s, v = ev
            if v <= 0 or self.ENG_OF.get(s) == engname:
                continue
            key = (engname, s)
            if self.last_wait.get(key, -1) >= v:
                continue
            self.last_wait[key] = v
            eng.wait_ge(self.sem[s], v)


def build_program(tap=None):
    nc = bass.Bass("TRN2", target_bir_lowering=False,
                   dynamic_dma_scratch_size=24576,
                   detect_race_conditions=False)
    dt = nc.dram_tensor

    xfm = dt("xfm", [D, T], F32, kind="ExternalInput").ap()
    wqT = dt("wqT", [D, D], F32, kind="ExternalInput").ap()
    wkT = dt("wkT", [D, D], F32, kind="ExternalInput").ap()
    wvT = dt("wvT", [D, D], F32, kind="ExternalInput").ap()
    woT = dt("woT", [D, D], F32, kind="ExternalInput").ap()
    wg = dt("wg", [I, D], F32, kind="ExternalInput").ap()
    wu = dt("wu", [I, D], F32, kind="ExternalInput").ap()
    wd = dt("wd", [D, I], F32, kind="ExternalInput").ap()
    bq_c = dt("bq_c", [128, DCH], F32, kind="ExternalInput").ap()
    bo_c = dt("bo_c", [128, DCH], F32, kind="ExternalInput").ap()
    bk_r = dt("bk_r", [1, D], F32, kind="ExternalInput").ap()
    bv_r = dt("bv_r", [1, D], F32, kind="ExternalInput").ap()
    clip2 = dt("clip2", [1, 2], F32, kind="ExternalInput").ap()
    bselin = dt("bselin", [128, 2], F32, kind="ExternalInput").ap()
    identin = dt("identin", [128, 128], F32, kind="ExternalInput").ap()
    pairselin = dt("pairselin", [16, 1024], BF16, kind="ExternalInput").ap()
    out = dt("out", [D, T], F32, kind="ExternalOutput").ap()

    qg_d = dt("qg_d", [I, D], BF16)
    qu_d = dt("qu_d", [I, D], BF16)
    qd_d = dt("qd_d", [D, I], BF16)
    h_d = dt("h_d", [I, T], BF16)
    kvz_in = dt("kvz_in", [128, 1056], F32)
    kvz_out = dt("kvz_out", [128, 1056], F32)
    mx1_in = dt("mx1_in", [1, 8], F32)
    mx1_out = dt("mx1_out", [1, 8], F32)
    mx2_in = dt("mx2_in", [1, 8], F32)
    mx2_out = dt("mx2_out", [1, 8], F32)
    kvzbf_d = dt("kvzbf_d", [64, 1056], BF16)

    ctx = ExitStack()
    with ctx:
        def sb(name, shape, dty):
            return ctx.enter_context(nc.sbuf_tensor(name, shape, dty))

        def alloc(name, shape, dty):
            cm = nc.sbuf_tensor(name, shape, dty, side="right")
            return cm, cm.__enter__()

        block = ctx.enter_context(nc.Block())
        P = Prog(nc, ctx)
        RG = [list(range(NCORES))]
        ev = {}

        xn_qx = sb("xn_qx", [128, DCH * T], BF16)
        bk_bc = sb("bk_bc", [128, D], F32)
        bv_bc = sb("bv_bc", [128, D], F32)
        bqc = sb("bqc", [128, DCH], F32)
        boc = sb("boc", [128, DCH], F32)
        clip_s = sb("clip_s", [1, 2], F32)
        bsel = sb("bsel", [128, 2], F32)
        ident = sb("ident", [128, 128], F32)
        ones_b = sb("ones_b", [128, 1], BF16)
        ones_c = sb("ones_c", [128, 1], F32)
        ones_r = sb("ones_r", [1, 128], F32)
        wsg = sb("wsg", [128, ITCH], F32)
        wsu = sb("wsu", [128, ITCH], F32)
        wsd = sb("wsd", [128, DCH], F32)
        gs_c = sb("gs_c", [128, ITCH], F32)
        us_c = sb("us_c", [128, ITCH], F32)
        ds_c = sb("ds_c", [128, DCH], F32)
        rcol = sb("rcol", [128, 2], F32)
        shcol = sb("shcol", [128, 1], F32)
        wrec = sb("wrec", [128, 16], F32)
        negm = sb("negm", [128, 1], F32)
        rows = sb("rows", [1, 1536], F32)
        srow = sb("srow", [1, 16], F32)
        mxs = sb("mxs", [128, 8], F32)
        mxg = sb("mxg", [1, 8], F32)
        mxs2 = sb("mxs2", [128, 8], F32)
        mxg2 = sb("mxg2", [1, 8], F32)
        hrun = sb("hrun", [128, 8], F32)
        fst = [sb(f"fst{i}", [128, 512], F32) for i in range(4)]
        bst = [sb(f"bst{i}", [128, 512], BF16) for i in range(4)]

        PB = [ctx.enter_context(nc.psum_tensor(f"pb{i}", [128, 512], F32))
              for i in range(8)]
        bank_free = [None] * 8

        q_a = sb("q_a", [128, DCH * T], BF16)
        pairs = sb("pairs", [16, 1024], BF16)
        kvz_sb = sb("kvz_sb", [64, 1056], F32)
        kvz_m = sb("kvz_m", [64, 1056], F32)
        kvz_bf = sb("kvz_bf", [128, 1056], BF16)
        wf320 = sb("wf320", [128, I], F32)
        cm_wa1, wa1 = alloc("wa1", [128, DCH * 1024], BF16)
        cm_wa0, wa0 = alloc("wa0", [128, DCH * 1024], BF16)
        wa = [wa0, wa1]
        cm_wf1, wf321 = alloc("wf321", [128, I], F32)
        wf32 = [wf320, wf321]
        cm_xsl0, xsl0 = alloc("xsl0", [128, DCH * 512], F32)
        cm_ktm0, ktm0 = alloc("ktm0", [128, D], BF16)
        cm_ktm1, ktm1 = alloc("ktm1", [128, D], BF16)
        ktm = [ktm0, ktm1]
        cm_vtm0, vtm0 = alloc("vtm0", [128, D], BF16)
        cm_vtm1, vtm1 = alloc("vtm1", [128, D], BF16)
        vtm = [vtm0, vtm1]
        cm_xsl1, xsl1 = alloc("xsl1", [128, DCH * 512], F32)
        xsl = [xsl0, xsl1]

        bkrow = kvz_sb[0:1, 0:D]
        bvrow = kvz_m[0:1, 0:D]

        @block.sync
        def _(sync):
            e = None
            for d_, s_ in [(bqc, bq_c), (boc, bo_c),
                           (clip_s, clip2), (bsel, bselin), (ident, identin),
                           (pairs, pairselin)]:
                e = P.inc(sync.dma_start(out=d_[:], in_=s_), "s_const", 16)
            ev["const"] = e

        @block.gpsimd
        def _(gpsimd):
            ev["wk"] = P.inc(gpsimd.dma_start(
                out=wa[0][:].rearrange("p (c n) -> p c n", c=DCH), in_=wkT.rearrange("(c p) n -> p c n", p=128)), "s_wa", 16)
            P.wait(gpsimd, "gpsimd", ev["wk"])
            ev["wv"] = P.inc(gpsimd.dma_start(
                out=wa[1][:].rearrange("p (c n) -> p c n", c=DCH), in_=wvT.rearrange("(c p) n -> p c n", p=128)), "s_wb", 16)

        @block.vector
        def _(vector):
            vector.memset(ones_b[:], 1.0)
            vector.memset(ones_c[:], 1.0)
            vector.memset(ones_r[:], 1.0)
            vector.memset(negm[:], -MAGIC)
            vector.memset(hrun[:], 0.0)
            vector.memset(mxs[:], 0.0)
            vector.memset(mxs2[:], 0.0)
            P.wait(vector, "vector", ev["const"])
            ev["memset"] = P.inc(vector.memset(kvz_sb[:], 0.0), "dv")

        @block.sync
        def _(sync):
            P.wait(sync, "sync", ev["memset"])
            P.inc(sync.dma_start(out=bkrow, in_=bk_r), "s_const", 16)
            ev["brows"] = P.inc(sync.dma_start(out=bvrow, in_=bv_r), "s_const", 16)

        @block.scalar
        def _(scalar):
            P.wait(scalar, "scalar", ev["const"])
            scalar.activation(srow[0:1, 0:1], clip_s[0:1, 0:1], AF.Exp, scale=-1.0)
            ev["sige"] = P.inc(scalar.activation(srow[0:1, 8:9], clip_s[0:1, 1:2],
                                                 AF.Exp, scale=-1.0), "ac")

        @block.vector
        def _(vector):
            P.wait(vector, "vector", ev["sige"])
            vector.tensor_scalar_add(srow[0:1, 0:1], srow[0:1, 0:1], 1.0)
            e = P.inc(vector.tensor_scalar_add(srow[0:1, 8:9], srow[0:1, 8:9], 1.0),
                      "dv")
            vector.wait_ge(P.sem["dv"], e[1])
            vector.reciprocal(srow[0:1, 1:2], srow[0:1, 0:1])
            ev["sig"] = P.inc(vector.reciprocal(srow[0:1, 9:10], srow[0:1, 8:9]), "dv")

        for ih in range(2):
            @block.tensor
            def _(tensor, ih=ih):
                P.wait(tensor, "tensor", ev["const"], ev["memset"], ev["brows"])
                if ih == 1:
                    P.wait(tensor, "tensor", ev["bbce0"])
                sl5 = slice(ih * 512, (ih + 1) * 512)
                tensor.matmul(PB[6][:, :], ones_r[:], kvz_sb[0:1, sl5], start=True, stop=True)
                ev[f"bbc{ih}"] = P.inc(tensor.matmul(PB[7][:, :], ones_r[:],
                                                     kvz_m[0:1, sl5],
                                                     start=True, stop=True), "pe")

            @block.vector
            def _(vector, ih=ih):
                P.wait(vector, "vector", ev[f"bbc{ih}"])
                sl5 = slice(ih * 512, (ih + 1) * 512)
                vector.tensor_copy(bk_bc[:, sl5], PB[6][:, :])
                ev[f"bbce{ih}"] = P.inc(vector.tensor_copy(bv_bc[:, sl5], PB[7][:, :]), "dv")
        bank_free[6] = ev["bbce1"]
        bank_free[7] = ev["bbce1"]

        wprep_tiles = ([("g", t) for t in range(ITCH)] +
                       [("u", t) for t in range(ITCH)] +
                       [("d", t) for t in range(DCH)])
        wp = {"cur": 0, "ld": {}, "p1": {}, "st": {}, "stn": {}}
        wp_src = {"g": wg, "u": wu, "d": wd}
        wp_dst = {"g": qg_d, "u": qu_d, "d": qd_d}
        wp_cols = {"g": wsg, "u": wsu, "d": wsd}
        wp_fd = {"g": D, "u": D, "d": I}

        def wprep_emit(n):
            start = wp["cur"]
            end = min(start + n, len(wprep_tiles))
            if end == start:
                return
            wp["cur"] = end
            for k in range(start, end):
                w, t = wprep_tiles[k]
                par = k % 2
                fd = wp_fd[w]

                @block.sync
                def _(sync, k=k, w=w, t=t, par=par, fd=fd):
                    P.wait(sync, "sync", wp["st"].get(k - 2))
                    wp["ld"][k] = P.inc(sync.dma_start(
                        out=wf32[par][:, 0:fd],
                        in_=wp_src[w][t * 128:(t + 1) * 128, :]), f"wpl{k % 2}", 16)

                @block.vector
                def _(vector, k=k, w=w, t=t, par=par, fd=fd):
                    P.wait(vector, "vector", wp["ld"][k])
                    e1 = P.inc(vector.tensor_reduce(
                        wrec[:, 0:1], wf32[par][:, 0:fd],
                        AX.X, ALU.max, apply_absolute_value=True), "dv")
                    vector.wait_ge(P.sem["dv"], e1[1])
                    e2 = P.inc(vector.tensor_scalar_mul(
                        wp_cols[w][:, t:t + 1], wrec[:, 0:1], 1.0 / QMAX), "dv")
                    vector.wait_ge(P.sem["dv"], e2[1])
                    e3 = P.inc(vector.reciprocal(wrec[:, 8:9],
                                                 wp_cols[w][:, t:t + 1]), "dv")
                    vector.wait_ge(P.sem["dv"], e3[1])
                    vector.tensor_scalar(
                        wf32[par][:, 0:fd], wf32[par][:, 0:fd], wrec[:, 8:9], MAGIC,
                        ALU.mult, ALU.add)
                    wp["p1"][k] = P.inc(vector.tensor_scalar_sub(
                        wf32[par][:, 0:fd], wf32[par][:, 0:fd], MAGIC), "dv")

                @block.gpsimd
                def _(gpsimd, k=k, w=w, t=t, par=par, fd=fd):
                    P.wait(gpsimd, "gpsimd", wp["p1"][k])
                    wp["st"][k] = P.inc(gpsimd.dma_start(
                        out=wp_dst[w][t * 128:(t + 1) * 128, :],
                        in_=wf32[par][:, 0:fd]), {"g": "s_qg", "u": "s_qu",
                                                  "d": "s_qd"}[w], 16)
                    wp["stn"][w] = wp["st"][k]

        wprep_emit(2)

        xv = xfm.rearrange("(c p) n -> p c n", p=128)
        ev_xn = [None] * 4
        ev_xsl_use = [None, None]

        @block.scalar
        def _(scalar):
            for s_ in range(2):
                ev[f"xsl{s_}"] = P.inc(scalar.dma_start(
                    out=xsl[s_][:], in_=xv[:, :, s_ * 512:(s_ + 1) * 512]),
                    f"s_x{s_ % 2}", 16)

        for sl in range(4):
            par = sl % 2
            for c in range(DCH):
                @block.scalar
                def _(scalar, sl=sl, par=par, c=c):
                    P.wait(scalar, "scalar", ev[f"xsl{sl}"])
                    key = (f"sqmm_{sl}_{c - 2}" if c >= 2
                           else f"sqmm_{sl - 1}_{DCH + c - 2}")
                    P.wait(scalar, "scalar", ev.get(key))
                    ev[f"sq_{sl}_{c}"] = P.inc(scalar.activation(
                        fst[2 + c % 2][:], xsl[par][:, c * 512:(c + 1) * 512],
                        AF.Square), "ac")

                @block.tensor
                def _(tensor, sl=sl, par=par, c=c):
                    if c == 0:
                        P.wait(tensor, "tensor", bank_free[0 + par], bank_free[2 + par])
                        P.wait(tensor, "tensor", ev[f"xsl{sl}"])
                    ev[f"smm_{sl}_{c}"] = P.inc(
                        tensor.matmul(PB[0 + par][0:1, :], ones_c[:],
                                      xsl[par][:, c * 512:(c + 1) * 512],
                                      start=(c == 0), stop=(c == DCH - 1)), "pe")
                    P.wait(tensor, "tensor", ev[f"sq_{sl}_{c}"])
                    ev[f"sqmm_{sl}_{c}"] = P.inc(
                        tensor.matmul(PB[2 + par][0:1, :], ones_c[:], fst[2 + c % 2][:],
                                      start=(c == 0), stop=(c == DCH - 1)), "pe")
            ev[f"stats_{sl}"] = ev[f"sqmm_{sl}_{DCH - 1}"]

            rm = rows[0:1, 0:512]
            rv = rows[0:1, 512:1024]
            rr = rows[0:1, 1024:1536]

            @block.vector
            def _(vector, sl=sl, par=par):
                P.wait(vector, "vector", ev[f"stats_{sl}"])
                vector.tensor_scalar_mul(rm, PB[0 + par][0:1, :], 1.0 / D)
                vector.tensor_scalar_mul(rv, PB[2 + par][0:1, :], 1.0 / D)
                vector.tensor_tensor(rr, rm, rm, ALU.mult)
                ev[f"rows_v_{sl}"] = P.inc(vector.scalar_tensor_tensor(
                    rv, rv, 1e-5, rr, ALU.add, ALU.subtract), "dv")

            @block.scalar
            def _(scalar, sl=sl):
                P.wait(scalar, "scalar", ev[f"rows_v_{sl}"])
                scalar.activation(rr, rv, AF.Ln)
                ev[f"rows_ln_{sl}"] = P.inc(scalar.activation(rv, rr, AF.Exp, scale=-0.5),
                                            "ac")

            @block.vector
            def _(vector, sl=sl):
                P.wait(vector, "vector", ev[f"rows_ln_{sl}"])
                ev[f"rows_m_{sl}"] = P.inc(vector.tensor_tensor(rm, rm, rv, ALU.mult),
                                           "dv")

            @block.tensor
            def _(tensor, sl=sl, par=par):
                P.wait(tensor, "tensor", ev[f"rows_m_{sl}"],
                       bank_free[4 + par], bank_free[6 + par])
                tensor.matmul(PB[4 + par][:, :], ones_r[:], rv, start=True, stop=True)
                ev[f"bc_{sl}"] = P.inc(tensor.matmul(PB[6 + par][:, :], ones_r[:], rm,
                                                     start=True, stop=True), "pe")

            @block.vector
            def _(vector, sl=sl, par=par):
                P.wait(vector, "vector", ev[f"bc_{sl}"])
                e = None
                for c in range(DCH):
                    f = fst[c % 2]
                    vector.tensor_tensor(f[:], xsl[par][:, c * 512:(c + 1) * 512],
                                         PB[4 + par][:, :], ALU.mult)
                    e = P.inc(vector.tensor_tensor(
                        xn_qx[:, c * T + sl * 512: c * T + (sl + 1) * 512],
                        f[:], PB[6 + par][:, :], ALU.subtract), "dv")
                ev_xn[sl] = e
                ev_xsl_use[par] = e
                bank_free[4 + par] = e
                bank_free[6 + par] = e
                bank_free[0 + par] = ev[f"rows_v_{sl}"]
                bank_free[2 + par] = ev[f"rows_v_{sl}"]

            if sl + 2 < 4:
                @block.scalar
                def _(scalar, sl=sl, par=par):
                    P.wait(scalar, "scalar", ev_xsl_use[par])
                    ev[f"xsl{sl + 2}"] = P.inc(scalar.dma_start(
                        out=xsl[par][:], in_=xv[:, :, (sl + 2) * 512:(sl + 3) * 512]),
                        f"s_x{par}", 16)

            wprep_emit(3)

        cm_xsl1.__exit__(None, None, None)
        cm_wq, wq_sb = alloc("wq_sb", [128, DCH * 1024], BF16)

        @block.gpsimd
        def _(gpsimd):
            P.wait(gpsimd, "gpsimd", ev_xn[0], ev_xn[1], ev_xn[2], ev_xn[3])
            eR = P.inc(gpsimd.tensor_reduce(mxs[0:1, 0:1], xn_qx[:, :], AX.XYZWC,
                                            ALU.abs_max), "gp")
            gpsimd.wait_ge(P.sem["gp"], eR[1])
            P.inc(gpsimd.dma_start(out=mx1_in.ap(), in_=mxs[0:1, :]), "s_mx", 16)
            P.wait(gpsimd, "gpsimd", P.now("s_mx"))
            P.inc(gpsimd.collective_compute(
                "AllReduce", ALU.max, replica_groups=RG,
                ins=[mx1_in.ap().opt()], outs=[mx1_out.ap().opt()]), "cc")
            P.wait(gpsimd, "gpsimd", P.now("cc"))
            ev["mxg"] = P.inc(gpsimd.dma_start(out=mxg[:], in_=mx1_out.ap()), "s_mx", 16)

        if tap == "xn":
            @block.gpsimd
            def _(gpsimd):
                P.wait(gpsimd, "gpsimd", ev_xn[3])
                P.inc(gpsimd.dma_start(
                    out=out.rearrange("(c p) n -> p c n", p=128), in_=xn_qx[:].rearrange("p (c n) -> p c n", c=DCH)),
                    "so", 16)
                gpsimd.wait_ge(P.sem["so"], 16)
            return nc

        ev_ktm = [None, None]
        ev_vtm = [None, None]
        ev_kvzmm = [None, None]
        ev_ke = [None, None]

        def emit_kvz(nt):
            par = nt % 2

            @block.tensor
            def _(tensor, nt=nt, par=par):
                P.wait(tensor, "tensor", ev_ktm[par], ev_vtm[par])
                if nt == 0:
                    P.wait(tensor, "tensor", bank_free[4], bank_free[5],
                           bank_free[6])
                for h in range(NH):
                    pb = PB[4 + h // 8]
                    cs = (h % 8) * 64
                    tensor.matmul(pb[0:64, cs:cs + 64],
                                  ktm[par][:, h * 64:(h + 1) * 64],
                                  vtm[par][:, h * 64:(h + 1) * 64],
                                  start=(nt == 0 and h % 8 == 0),
                                  stop=(nt == 15 and h % 8 == 7),
                                  skip_group_check=True)
                e = None
                for h in range(NH):
                    e = P.inc(tensor.matmul(
                        PB[6][0:64, h:h + 1], ktm[par][:, h * 64:(h + 1) * 64],
                        ones_b[:], start=(nt == 0 and h == 0),
                        stop=(nt == 15 and h == 15),
                        skip_group_check=True), "pe")
                ev_kvzmm[par] = e

        for nt in range(16):
            par = nt % 2
            b0, b1 = 0 + 2 * par, 1 + 2 * par

            @block.tensor
            def _(tensor, nt=nt, par=par, b0=b0, b1=b1):
                P.wait(tensor, "tensor", ev["wk"], ev_xn[nt // 4],
                       bank_free[b0], bank_free[b1])
                e = None
                for ih in range(2):
                    for c in range(DCH):
                        e = P.inc(tensor.matmul(
                            PB[b0 + ih][:, :],
                            xn_qx[:, c * T + nt * 128: c * T + (nt + 1) * 128],
                            wa[0][:, c * 1024 + ih * 512: c * 1024 + (ih + 1) * 512],
                            start=(c == 0), stop=(c == DCH - 1)), "pe")
                ev[f"kmm_{nt}"] = e

            @block.vector
            def _(vector, nt=nt, par=par, b0=b0):
                P.wait(vector, "vector", ev[f"kmm_{nt}"], ev["bbce1"],
                       ev_kvzmm[par], ev_ke[1 - par])
                e = None
                for ih in range(2):
                    e = P.inc(vector.tensor_tensor(
                        fst[2 + ih][:], PB[b0 + ih][:, :],
                        bk_bc[:, ih * 512:(ih + 1) * 512], ALU.add), "dv")
                ev[f"kt_{nt}"] = e

            @block.vector
            def _(vector, nt=nt):
                e = None
                for ih in range(2):
                    e = P.inc(vector.tensor_scalar_max(bst[2 + ih][:],
                                                       fst[2 + ih][:], 0.0), "dv")
                ev[f"kr_{nt}"] = e

            @block.scalar
            def _(scalar, nt=nt):
                P.wait(scalar, "scalar", ev[f"kt_{nt}"], ev[f"kr_{nt}"])
                e = None
                for ih in range(2):
                    scalar.activation(fst[2 + ih][:], fst[2 + ih][:], AF.Relu, scale=-1.0)
                for ih in range(2):
                    e = P.inc(scalar.activation(bst[ih][:], fst[2 + ih][:],
                                                AF.Exp, scale=-1.0), "ac")
                ev[f"ke_{nt}"] = e
                ev_ke[nt % 2] = e

            @block.vector
            def _(vector, nt=nt, par=par, b0=b0, b1=b1):
                P.wait(vector, "vector", ev[f"ke_{nt}"])
                e = None
                for ih in range(2):
                    e = P.inc(vector.tensor_tensor(
                        ktm[par][:, ih * 512:(ih + 1) * 512], bst[ih][:],
                        bst[2 + ih][:], ALU.add), "dv")
                ev_ktm[par] = e
                bank_free[b0] = e
                bank_free[b1] = e

            @block.tensor
            def _(tensor, nt=nt, par=par, b0=b0, b1=b1):
                P.wait(tensor, "tensor", ev["wv"], bank_free[b0], bank_free[b1])
                e = None
                for ih in range(2):
                    for c in range(DCH):
                        e = P.inc(tensor.matmul(
                            PB[b0 + ih][:, :],
                            xn_qx[:, c * T + nt * 128: c * T + (nt + 1) * 128],
                            wa[1][:, c * 1024 + ih * 512: c * 1024 + (ih + 1) * 512],
                            start=(c == 0), stop=(c == DCH - 1)), "pe")
                ev[f"vmm_{nt}"] = e

            @block.vector
            def _(vector, nt=nt, par=par, b0=b0, b1=b1):
                P.wait(vector, "vector", ev[f"vmm_{nt}"])
                e = None
                for ih in range(2):
                    e = P.inc(vector.tensor_tensor(
                        vtm[par][:, ih * 512:(ih + 1) * 512], PB[b0 + ih][:, :],
                        bv_bc[:, ih * 512:(ih + 1) * 512], ALU.add), "dv")
                ev_vtm[par] = e
                bank_free[b0] = e
                bank_free[b1] = e

            if nt >= 1:
                emit_kvz(nt - 1)
            wprep_emit(2)
            if nt == 10:
                @block.gpsimd
                def _(gpsimd):
                    P.wait(gpsimd, "gpsimd", ev_xn[3])
                    ev["wq"] = P.inc(gpsimd.dma_start(
                        out=wq_sb[:].rearrange("p (c n) -> p c n", c=DCH),
                        in_=wqT.rearrange("(c p) n -> p c n", p=128)), "s_wa", 16)
        emit_kvz(15)
        ev["kvzdone"] = ev_kvzmm[1]

        @block.vector
        def _(vector):
            P.wait(vector, "vector", ev["kvzdone"], ev["memset"], ev["bbce1"])
            e0 = None
            for h in range(NH):
                pb = PB[4 + h // 8]
                cs = (h % 8) * 64
                vector.tensor_copy(kvz_sb[:, h * 66:h * 66 + 64], pb[0:64, cs:cs + 64])
            for h in range(NH):
                e0 = P.inc(vector.tensor_copy(kvz_sb[:, h * 66 + 64:h * 66 + 65],
                                              PB[6][0:64, h:h + 1]), "dv")
            bank_free[4] = e0
            bank_free[5] = e0
            bank_free[6] = e0
            ev["kvz_evac"] = e0
            ev["kvzm0"] = P.inc(vector.tensor_scalar_mul(kvz_m[:], kvz_sb[:],
                                                         bsel[0:64, 0:1]), "dv")

        if tap == "kvz0":
            @block.gpsimd
            def _(gpsimd):
                P.wait(gpsimd, "gpsimd", ev["kvz_evac"])
                P.inc(gpsimd.dma_start(out=out[0:64, 0:1056], in_=kvz_sb[:]), "so", 16)
                gpsimd.wait_ge(P.sem["so"], 16)
            return nc

        @block.gpsimd
        def _(gpsimd):
            P.wait(gpsimd, "gpsimd", ev["kvzm0"])
            e = P.inc(gpsimd.dma_start(out=kvz_in.ap()[0:64, :], in_=kvz_m[:]),
                      "s_kvz", 16)
            ev["kvzs0"] = e

        @block.vector
        def _(vector):
            P.wait(vector, "vector", ev["kvzs0"])
            ev["kvzm1"] = P.inc(vector.tensor_scalar_mul(kvz_m[:], kvz_sb[:],
                                                         bsel[0:64, 1:2]), "dv")

        @block.gpsimd
        def _(gpsimd):
            P.wait(gpsimd, "gpsimd", ev["kvzm1"])
            P.inc(gpsimd.dma_start(out=kvz_in.ap()[64:128, :], in_=kvz_m[:]),
                  "s_kvz", 16)
            P.wait(gpsimd, "gpsimd", P.now("s_kvz"))
            P.inc(gpsimd.collective_compute(
                "AllReduce", ALU.add, replica_groups=RG,
                ins=[kvz_in.ap().opt()], outs=[kvz_out.ap().opt()]), "cc")
            P.wait(gpsimd, "gpsimd", P.now("cc"))
            P.inc(gpsimd.dma_start(out=kvz_sb[:], in_=kvz_out.ap()[0:64, :]),
                  "s_kvz", 16)
            ev["kvzr"] = P.inc(gpsimd.dma_start(
                out=kvz_m[:], in_=kvz_out.ap()[64:128, :]), "s_kvz", 16)

        @block.vector
        def _(vector):
            P.wait(vector, "vector", ev["kvzr"])
            vector.tensor_scalar_mul(kvz_sb[:], kvz_sb[:], bsel[0:64, 0:1])
            vector.scalar_tensor_tensor(kvz_sb[:], kvz_m[:], bsel[0:64, 1:2], kvz_sb[:],
                                        ALU.mult, ALU.add)
            ev["kvzbf"] = P.inc(vector.tensor_copy(kvz_bf[0:64, :], kvz_sb[:]), "dv")

        @block.gpsimd
        def _(gpsimd):
            P.wait(gpsimd, "gpsimd", ev["kvzbf"])
            P.inc(gpsimd.dma_start(out=kvzbf_d.ap(), in_=kvz_bf[0:64, :]),
                  "s_kvz", 16)
            P.wait(gpsimd, "gpsimd", P.now("s_kvz"))
            ev["kvzbf2"] = P.inc(gpsimd.dma_start(out=kvz_bf[64:128, :],
                                                  in_=kvzbf_d.ap()), "s_kvz", 16)


        if tap == "kvz":
            @block.gpsimd
            def _(gpsimd):
                P.wait(gpsimd, "gpsimd", ev["kvzbf"])
                P.inc(gpsimd.dma_start(out=out[0:64, 0:1056], in_=kvz_sb[:]), "so", 16)
                gpsimd.wait_ge(P.sem["so"], 16)
            return nc

        qit = -1
        for ich in range(DCH):
            for nt in range(4):
                qit += 1
                pb = qit % 4
                ip = qit % 2

                @block.tensor
                def _(tensor, ich=ich, nt=nt, pb=pb):
                    P.wait(tensor, "tensor", ev["wq"], bank_free[pb])
                    e = None
                    for c in range(DCH):
                        e = P.inc(tensor.matmul(
                            PB[pb][:, :],
                            wq_sb[:, c * 1024 + ich * 128: c * 1024 + (ich + 1) * 128],
                            xn_qx[:, c * T + nt * 512: c * T + (nt + 1) * 512],
                            start=(c == 0), stop=(c == DCH - 1)), "pe")
                    ev[f"qmm_{qit}"] = e

                @block.vector
                def _(vector, ich=ich, qit=qit, pb=pb, ip=ip):
                    P.wait(vector, "vector", ev[f"qmm_{qit}"], ev.get(f"qe_{qit - 2}"))
                    ev[f"qt_{qit}"] = P.inc(vector.tensor_scalar(
                        fst[ip][:], PB[pb][:, :], bqc[:, ich:ich + 1], None, ALU.add),
                        "dv")

                if qit % 2 == 1:
                    @block.scalar
                    def _(scalar, qit=qit):
                        P.wait(scalar, "scalar", ev[f"qt_{qit - 1}"],
                               ev[f"qt_{qit}"])
                        for q2 in (qit - 1, qit):
                            ip2 = q2 % 2
                            scalar.activation(fst[2 + ip2][:], fst[ip2][:],
                                              AF.Relu, scale=-1.0)
                        for q2 in (qit - 1, qit):
                            ip2 = q2 % 2
                            ev[f"qe_{q2}"] = P.inc(scalar.activation(
                                bst[ip2][:], fst[2 + ip2][:],
                                AF.Exp, scale=-1.0), "ac")

                if qit % 2 == 1:
                    @block.vector
                    def _(vector, qit=qit):
                        for q2 in (qit - 1, qit):
                            ip2 = q2 % 2
                            pb2 = q2 % 4
                            ich2, nt2 = q2 // 4, q2 % 4
                            P.wait(vector, "vector", ev[f"qe_{q2}"])
                            vector.tensor_scalar_max(bst[2 + ip2][:],
                                                     fst[ip2][:], 0.0)
                            e = P.inc(vector.tensor_tensor(
                                q_a[:, ich2 * T + nt2 * 512:
                                    ich2 * T + (nt2 + 1) * 512],
                                bst[ip2][:], bst[2 + ip2][:], ALU.add), "dv")
                            ev[f"qph_{q2}"] = e
                            bank_free[pb2] = e
            if ich % 2 == 0:
                wprep_emit(2)
        ev["qdone"] = ev[f"qph_{qit}"]

        if tap == "q":
            @block.gpsimd
            def _(gpsimd):
                P.wait(gpsimd, "gpsimd", ev["qdone"])
                P.inc(gpsimd.dma_start(
                    out=out.rearrange("(c p) n -> p c n", p=128), in_=q_a[:].rearrange("p (c n) -> p c n", c=DCH)),
                    "so", 16)
                gpsimd.wait_ge(P.sem["so"], 16)
            return nc

        wprep_emit(len(wprep_tiles))
        assert wp["cur"] == len(wprep_tiles)

        if tap == "w52":
            @block.gpsimd
            def _(gpsimd):
                P.wait(gpsimd, "gpsimd", wp["st"][len(wprep_tiles) - 1],
                       wp["st"][len(wprep_tiles) - 2], ev["qdone"], ev["mxg"])
                P.inc(gpsimd.dma_start(out=out[0:128, 0:8], in_=wsd[:, 0:8]), "so", 16)
                gpsimd.wait_ge(P.sem["so"], 16)
            return nc

        @block.vector
        def _(vector):
            P.wait(vector, "vector", ev["mxg"], ev["sig"])
            e1 = P.inc(vector.tensor_tensor(srow[0:1, 2:3], srow[0:1, 1:2],
                                            mxg[0:1, 0:1], ALU.mult), "dv")
            vector.wait_ge(P.sem["dv"], e1[1])
            e2 = P.inc(vector.tensor_scalar_mul(srow[0:1, 3:4], srow[0:1, 2:3],
                                                1.0 / QMAX), "dv")
            vector.wait_ge(P.sem["dv"], e2[1])
            ev["rx"] = P.inc(vector.reciprocal(srow[0:1, 4:5], srow[0:1, 3:4]), "dv")

        @block.tensor
        def _(tensor):
            P.wait(tensor, "tensor", ev["rx"], bank_free[4])
            tensor.matmul(PB[4][:, 0:1], ones_r[:], srow[0:1, 4:5], start=True,
                          stop=True)
            ev["rxbc"] = P.inc(tensor.matmul(PB[4][:, 1:2], ones_r[:], srow[0:1, 3:4],
                                             start=True, stop=True), "pe")

        @block.vector
        def _(vector):
            P.wait(vector, "vector", ev["rxbc"])
            vector.tensor_copy(rcol[:, 0:1], PB[4][:, 0:1])
            e = P.inc(vector.tensor_copy(rcol[:, 1:2], PB[4][:, 1:2]), "dv")
            bank_free[4] = e
            vector.wait_ge(P.sem["dv"], e[1])
            vector.tensor_scalar_mul(gs_c[:], wsg[:], rcol[:, 1:2])
            ev["gscols"] = P.inc(vector.tensor_scalar_mul(us_c[:], wsu[:], rcol[:, 1:2]),
                                 "dv")

        if tap == "rx":
            @block.gpsimd
            def _(gpsimd):
                P.wait(gpsimd, "gpsimd", ev["gscols"], ev["qdone"],
                       wp["st"][len(wprep_tiles) - 1], wp["st"][len(wprep_tiles) - 2])
                P.inc(gpsimd.dma_start(out=out[0:128, 0:22], in_=gs_c[:, :]), "so", 16)
                gpsimd.wait_ge(P.sem["so"], 16)
            return nc

        ev_qx = [None] * DCH
        for c in range(DCH):
            @block.vector
            def _(vector, c=c):
                P.wait(vector, "vector", ev["gscols"], ev.get(f"qxp_{c - 2}"))
                ev[f"qxv_{c}"] = P.inc(vector.tensor_scalar(
                    xsl[0][:, (c % 2) * 2048:(c % 2) * 2048 + 2048],
                    xn_qx[:, c * T:(c + 1) * T], rcol[:, 0:1], MAGIC,
                    ALU.mult, ALU.add), "dv")

            @block.vector
            def _(vector, c=c):
                vector.tensor_scalar(
                    xsl[0][:, (c % 2) * 2048:(c % 2) * 2048 + 2048],
                    xsl[0][:, (c % 2) * 2048:(c % 2) * 2048 + 2048],
                    MAGIC, QMAX, ALU.subtract, ALU.min)
                e = P.inc(vector.tensor_scalar_max(
                    xn_qx[:, c * T:(c + 1) * T],
                    xsl[0][:, (c % 2) * 2048:(c % 2) * 2048 + 2048],
                    -(QMAX + 1.0)), "dv")
                ev[f"qxp_{c}"] = e
                ev_qx[c] = e

        if tap == "qx":
            @block.gpsimd
            def _(gpsimd):
                P.wait(gpsimd, "gpsimd", *[ev_qx[c] for c in range(DCH)])
                P.inc(gpsimd.dma_start(
                    out=out.rearrange("(c p) n -> p c n", p=128), in_=xn_qx[:].rearrange("p (c n) -> p c n", c=DCH)),
                    "so", 16)
                gpsimd.wait_ge(P.sem["so"], 16)
            return nc

        cm_wq.__exit__(None, None, None)
        cm_vtm1.__exit__(None, None, None)
        cm_vtm0.__exit__(None, None, None)
        cm_ktm1.__exit__(None, None, None)
        cm_ktm0.__exit__(None, None, None)
        cm_xsl0.__exit__(None, None, None)
        cm_wf1.__exit__(None, None, None)
        cm_wa0.__exit__(None, None, None)

        cm_qgT0, qgT0 = alloc("qgT0", [128, 4096], BF16)
        cm_quT0, quT0 = alloc("quT0", [128, 4096], BF16)
        cm_hst0, hst0 = alloc("hst0", [128, T], BF16)
        cm_hst1, hst1 = alloc("hst1", [128, T], BF16)
        hstage = [hst0, hst1]
        cm_qgT1, qgT1 = alloc("qgT1", [128, 4096], BF16)
        cm_quT1, quT1 = alloc("quT1", [128, 4096], BF16)
        qgT = [qgT0, qgT1]
        quT = [quT0, quT1]

        GROUPS = [(0, 1), (1, 1), (2, 2), (4, 4), (8, 4), (12, 4), (16, 4), (20, 2)]
        ev_gld = [None] * len(GROUPS)
        ev_gu_use = [None, None]

        def emit_gload(g):
            t0, ntile = GROUPS[g]
            par = g % 2

            @block.sync
            def _(sync, g=g, t0=t0, ntile=ntile, par=par):
                P.wait(sync, "sync", wp["st"].get(t0 + ntile - 1),
                       wp["st"].get(ITCH + t0 + ntile - 1),
                       ev["kmm_15"], ev_gu_use[par])
                if par == 1:
                    P.wait(sync, "sync", wp["st"].get(len(wprep_tiles) - 1),
                           ev_qx[DCH - 1], ev_xn[3])
                rows_ = ntile * 128
                e = None
                for c in range(DCH):
                    P.inc(sync.dma_start(
                        out=qgT[par][:, c * 512:c * 512 + rows_],
                        in_=qg_d[t0 * 128:t0 * 128 + rows_, c * 128:(c + 1) * 128],
                        transpose=True), f"s_gl{par}", 16)
                    e = P.inc(sync.dma_start(
                        out=quT[par][:, c * 512:c * 512 + rows_],
                        in_=qu_d[t0 * 128:t0 * 128 + rows_, c * 128:(c + 1) * 128],
                        transpose=True), f"s_gl{par}", 16)
                ev_gld[g] = e

        emit_gload(0)
        ev_h = {}
        for g, (t0, ntile) in enumerate(GROUPS):
            par = g % 2
            if g + 1 < len(GROUPS):
                emit_gload(g + 1)
            for tl in range(ntile):
                it = t0 + tl
                hpar = it % 2
                for nt in range(4):
                    bg = (nt % 2) * 2
                    bu = bg + 1

                    @block.tensor
                    def _(tensor, g=g, par=par, tl=tl, it=it, nt=nt, bg=bg, bu=bu,
                          ntile=ntile):
                        P.wait(tensor, "tensor", ev_gld[g], ev_qx[DCH - 1],
                               bank_free[bg], bank_free[bu])
                        for c in range(DCH):
                            tensor.matmul(
                                PB[bg][:, :],
                                qgT[par][:, c * 512 + tl * 128: c * 512 + (tl + 1) * 128],
                                xn_qx[:, c * T + nt * 512: c * T + (nt + 1) * 512],
                                start=(c == 0), stop=(c == DCH - 1))
                        e = None
                        for c in range(DCH):
                            e = P.inc(tensor.matmul(
                                PB[bu][:, :],
                                quT[par][:, c * 512 + tl * 128: c * 512 + (tl + 1) * 128],
                                xn_qx[:, c * T + nt * 512: c * T + (nt + 1) * 512],
                                start=(c == 0), stop=(c == DCH - 1)), "pe")
                        ev[f"gumm_{it}_{nt}"] = e
                        if tl == ntile - 1 and nt == 3:
                            ev_gu_use[par] = e

                    @block.scalar
                    def _(scalar, it=it, nt=nt, bg=bg):
                        P.wait(scalar, "scalar", ev[f"gumm_{it}_{nt}"], ev["gscols"],
                               ev.get(f"h_{it}_{nt - 2}") if nt >= 2
                               else ev.get(f"h_{it - 1}_{nt + 2}"))
                        ev[f"sg_{it}_{nt}"] = P.inc(scalar.activation(
                            fst[nt % 2][:], PB[bg][:, :], AF.Silu,
                            scale=gs_c[:, it:it + 1]), "ac")

                    @block.vector
                    def _(vector, it=it, nt=nt, hpar=hpar, bg=bg, bu=bu):
                        P.wait(vector, "vector", ev[f"sg_{it}_{nt}"])
                        if it == 0 and nt == 0:
                            P.wait(vector, "vector",
                                   wp["st"].get(len(wprep_tiles) - 1))
                        if nt == 0:
                            P.wait(vector, "vector", ev_h.get(it - 2))
                        e = P.inc(vector.scalar_tensor_tensor(
                            hstage[hpar][:, nt * 512:(nt + 1) * 512],
                            PB[bu][:, :], us_c[:, it:it + 1], fst[nt % 2][:],
                            ALU.mult, ALU.mult), "dv")
                        ev[f"h_{it}_{nt}"] = e
                        bank_free[bg] = e
                        bank_free[bu] = e

                @block.gpsimd
                def _(gpsimd, it=it, hpar=hpar):
                    P.wait(gpsimd, "gpsimd", ev[f"h_{it}_3"])
                    if it > 0:
                        P.wait(gpsimd, "gpsimd", ev[f"hmxa_{it - 1}"])
                    ev[f"hmx_{it}"] = P.inc(gpsimd.tensor_reduce(
                        hrun[0:1, 1:2], hstage[hpar][:, :],
                        AX.XYZWC, ALU.abs_max), "gp")

                @block.vector
                def _(vector, it=it):
                    P.wait(vector, "vector", ev[f"hmx_{it}"])
                    ev[f"hmxa_{it}"] = P.inc(vector.tensor_tensor(
                        hrun[0:1, 0:1], hrun[0:1, 0:1], hrun[0:1, 1:2], ALU.max), "dv")

                @block.sync
                def _(sync, it=it, hpar=hpar):
                    P.wait(sync, "sync", ev[f"hmx_{it}"])
                    ev_h[it] = P.inc(sync.dma_start(
                        out=h_d[it * 128:(it + 1) * 128, :], in_=hstage[hpar][:, :]),
                        f"s_h{it % 2}", 16)

        @block.gpsimd
        def _(gpsimd):
            P.wait(gpsimd, "gpsimd", ev[f"hmxa_{ITCH - 1}"])
            eC = P.inc(gpsimd.tensor_copy(mxs2[0:1, 0:1], hrun[0:1, 0:1]), "gp")
            gpsimd.wait_ge(P.sem["gp"], eC[1])
            P.inc(gpsimd.dma_start(out=mx2_in.ap(), in_=mxs2[0:1, :]), "s_mx", 16)
            P.wait(gpsimd, "gpsimd", P.now("s_mx"))
            P.inc(gpsimd.collective_compute(
                "AllReduce", ALU.max, replica_groups=RG,
                ins=[mx2_in.ap().opt()], outs=[mx2_out.ap().opt()]), "cc")
            P.wait(gpsimd, "gpsimd", P.now("cc"))
            ev["mxg2"] = P.inc(gpsimd.dma_start(out=mxg2[:], in_=mx2_out.ap()), "s_mx", 16)

        if tap == "gup":
            @block.gpsimd
            def _(gpsimd):
                P.wait(gpsimd, "gpsimd", ev_h[ITCH - 1], ev_h[ITCH - 2], ev["mxg2"])
                P.inc(gpsimd.dma_start(out=out[0:128, 0:8], in_=hrun[:, 0:8]), "so", 16)
                gpsimd.wait_ge(P.sem["so"], 16)
            return nc

        cm_quT1.__exit__(None, None, None)
        cm_qgT1.__exit__(None, None, None)
        cm_hst1.__exit__(None, None, None)
        cm_hst0.__exit__(None, None, None)
        cm_quT0.__exit__(None, None, None)
        cm_qgT0.__exit__(None, None, None)
        cm_qdT, qdT = alloc("qdT", [128, ITCH * 1024], BF16)
        cm_rtm, rec_tm = alloc("rec_tm", [128, 64], F32)
        cm_rtm2, rec_tm2 = alloc("rec_tm2", [128, 64], F32)
        cm_recT, recT = alloc("recT", [16, T], BF16)
        cm_xo0, xo0 = alloc("xo0", [128, T], F32)
        xoch = [xo0, xo0]

        @block.sync
        def _(sync):
            P.wait(sync, "sync", wp["stn"].get("d"), ev_gu_use[0], ev_gu_use[1],
                   ev_h.get(ITCH - 1), ev_h.get(ITCH - 2))
            e = None
            for ic in range(ITCH):
                e = P.inc(sync.dma_start(out=qdT[:, ic * 1024:(ic + 1) * 1024],
                                         in_=qd_d[:, ic * 128:(ic + 1) * 128],
                                         transpose=True), "s_qdT", 16)
            ev["qdT"] = e

        @block.gpsimd
        def _(gpsimd):
            P.wait(gpsimd, "gpsimd", ev["kvzdone"])
            ev["wo"] = P.inc(gpsimd.dma_start(
                out=wa[1][:].rearrange("p (c n) -> p c n", c=DCH), in_=woT.rearrange("(c p) n -> p c n", p=128)), "s_wb", 16)

        @block.scalar
        def _(scalar):
            P.wait(scalar, "scalar", ev_gu_use[0], ev_gu_use[1])
            ev["xo0"] = P.inc(scalar.dma_start(out=xoch[0][:], in_=xfm[0:128, :]),
                              "s_xo0", 16)
            ev["xo1"] = ev["xo0"]
        ev_xoch = [ev["xo0"]] + [None] * (DCH - 1)

        if tap == "preattn":
            @block.gpsimd
            def _(gpsimd):
                P.wait(gpsimd, "gpsimd", ev["qdT"], ev["wo"], ev["xo1"], ev["kvzbf2"])
                P.inc(gpsimd.dma_start(out=out[0:128, 0:1024],
                                       in_=qdT[:, 0:1024]), "so", 16)
                gpsimd.wait_ge(P.sem["so"], 16)
            return nc

        for nt in range(4):
            @block.tensor
            def _(tensor, nt=nt):
                P.wait(tensor, "tensor", ev["kvzbf2"], ev["qdone"], bank_free[4])
                P.wait(tensor, "tensor", ev["xo0"],
                       ev["xo1"], ev_h[ITCH - 1], ev_h[ITCH - 2])
                e = None
                for sub in range(4):
                    for h in range(NH):
                        e = P.inc(tensor.matmul(
                            PB[4][:, sub * 16 + h:sub * 16 + h + 1],
                            q_a[(h % 2) * 64:(h % 2) * 64 + 64,
                                (h // 2) * T + nt * 512 + sub * 128:
                                (h // 2) * T + nt * 512 + (sub + 1) * 128],
                            kvz_bf[(h % 2) * 64:(h % 2) * 64 + 64,
                                   h * 66 + 64:h * 66 + 65],
                            start=True, stop=True), "pe")
                ev[f"znorm_{nt}"] = e

            if tap is not None and tap.startswith("znorm") and nt == 0:
                @block.vector
                def _(vector):
                    P.wait(vector, "vector", ev["znorm_0"])
                    P.inc(vector.tensor_copy(fst[0][:, 0:64], PB[4][:, 0:64]), "dv")
                    ev["zdump"] = P.now("dv")

                @block.gpsimd
                def _(gpsimd):
                    P.wait(gpsimd, "gpsimd", ev["zdump"])
                    P.inc(gpsimd.dma_start(out=out[0:128, 0:512], in_=fst[0][:]),
                          "so", 16)
                    gpsimd.wait_ge(P.sem["so"], 16)
                return nc

            @block.vector
            def _(vector, nt=nt):
                P.wait(vector, "vector", ev[f"znorm_{nt}"], ev_h[ITCH - 1],
                       ev_h[ITCH - 2], ev_gu_use[0], ev_gu_use[1])
                vector.tensor_scalar_add(rec_tm[:, :], PB[4][:, 0:64], 1e-6)
                e = P.inc(vector.reciprocal(rec_tm2[:, :], rec_tm[:, :]), "dv")
                ev[f"rec_{nt}"] = e
                bank_free[4] = e

            for sub in range(4):
                @block.tensor
                def _(tensor, nt=nt, sub=sub):
                    P.wait(tensor, "tensor", ev[f"rec_{nt}"], bank_free[5])
                    ev[f"recT_{nt}_{sub}"] = P.inc(tensor.transpose(
                        PB[5][0:16, 0:128],
                        rec_tm2[:, sub * 16:(sub + 1) * 16], ident[:]), "pe")

                @block.vector
                def _(vector, nt=nt, sub=sub):
                    P.wait(vector, "vector", ev[f"recT_{nt}_{sub}"])
                    e = P.inc(vector.tensor_copy(
                        recT[:, nt * 512 + sub * 128:nt * 512 + (sub + 1) * 128],
                        PB[5][0:16, 0:128]), "dv")
                    bank_free[5] = e
            ev[f"recTe_{nt}"] = P.now("dv")

            if tap == "rec" and nt == 3:
                @block.gpsimd
                def _(gpsimd):
                    P.wait(gpsimd, "gpsimd", ev["recTe_3"])
                    P.inc(gpsimd.dma_start(out=out[0:16, 0:2048], in_=recT[:, :]),
                          "so", 16)
                    gpsimd.wait_ge(P.sem["so"], 16)
                return nc

            for e2 in range(DCH):
                pb = 6 + e2 % 2
                pa = e2 % 2

                @block.tensor
                def _(tensor, nt=nt, e2=e2, pb=pb):
                    P.wait(tensor, "tensor", ev[f"recTe_{nt}"], bank_free[pb])
                    ev[f"rbc_{nt}_{e2}"] = P.inc(tensor.matmul(
                        PB[pb][:, :], pairs[:, e2 * 128:(e2 + 1) * 128],
                        recT[:, nt * 512:(nt + 1) * 512], start=True, stop=True), "pe")

                @block.vector
                def _(vector, nt=nt, e2=e2, pb=pb):
                    P.wait(vector, "vector", ev[f"rbc_{nt}_{e2}"])
                    e = P.inc(vector.tensor_tensor(
                        q_a[:, e2 * T + nt * 512:e2 * T + (nt + 1) * 512],
                        q_a[:, e2 * T + nt * 512:e2 * T + (nt + 1) * 512],
                        PB[pb][:, :], ALU.mult), "dv")
                    ev[f"qp_{nt}_{e2}"] = e
                    bank_free[pb] = e

                @block.tensor
                def _(tensor, nt=nt, e2=e2, pa=pa):
                    P.wait(tensor, "tensor", ev[f"qp_{nt}_{e2}"], bank_free[pa])
                    e = None
                    for hh in range(2):
                        h = 2 * e2 + hh
                        e = P.inc(tensor.matmul(
                            PB[pa][hh * 64:(hh + 1) * 64, :],
                            kvz_bf[hh * 64:hh * 64 + 64, h * 66:h * 66 + 64],
                            q_a[hh * 64:(hh + 1) * 64,
                                e2 * T + nt * 512:e2 * T + (nt + 1) * 512],
                            start=True, stop=True), "pe")
                    ev[f"amm_{nt}_{e2}"] = e

                @block.vector
                def _(vector, nt=nt, e2=e2, pa=pa):
                    P.wait(vector, "vector", ev[f"amm_{nt}_{e2}"])
                    e = P.inc(vector.tensor_copy(
                        q_a[:, e2 * T + nt * 512:e2 * T + (nt + 1) * 512],
                        PB[pa][:, :]), "dv")
                    ev[f"ae_{nt}_{e2}"] = e
                    bank_free[pa] = e
        ev["adone"] = ev["ae_3_7"]

        if tap == "aphase":
            @block.gpsimd
            def _(gpsimd):
                P.wait(gpsimd, "gpsimd", ev["adone"])
                P.inc(gpsimd.dma_start(
                    out=out.rearrange("(c p) n -> p c n", p=128),
                    in_=q_a[:].rearrange("p (c n) -> p c n", c=DCH)), "so", 16)
                gpsimd.wait_ge(P.sem["so"], 16)
            return nc

        for oc in range(DCH):
            for nt in range(4):
                pb = (oc * 4 + nt) % 4

                @block.tensor
                def _(tensor, oc=oc, nt=nt, pb=pb):
                    P.wait(tensor, "tensor", ev["wo"], ev["adone"], bank_free[pb])
                    e = None
                    for c in range(DCH):
                        e = P.inc(tensor.matmul(
                            PB[pb][:, :],
                            wa[1][:, c * 1024 + oc * 128: c * 1024 + (oc + 1) * 128],
                            q_a[:, c * T + nt * 512: c * T + (nt + 1) * 512],
                            start=(c == 0), stop=(c == DCH - 1)), "pe")
                    ev[f"womm_{oc}_{nt}"] = e

                @block.scalar
                def _(scalar, oc=oc, nt=nt, pb=pb):
                    key = (f"attnx_{oc}_{nt - 2}" if nt >= 2
                           else f"attnx_{oc - 1}_{nt + 2}")
                    P.wait(scalar, "scalar", ev[f"womm_{oc}_{nt}"], ev.get(key))
                    ev[f"woe_{oc}_{nt}"] = P.inc(scalar.activation(
                        fst[2 + nt % 2][:], PB[pb][:, :], AF.Identity,
                        bias=boc[:, oc:oc + 1]), "ac")

                @block.vector
                def _(vector, oc=oc, nt=nt, pb=pb):
                    P.wait(vector, "vector", ev[f"woe_{oc}_{nt}"], ev_xoch[oc])
                    e = P.inc(vector.tensor_tensor(
                        xn_qx[:, oc * T + nt * 512: oc * T + (nt + 1) * 512],
                        fst[2 + nt % 2][:], xoch[0][:, nt * 512:(nt + 1) * 512],
                        ALU.add), "dv")
                    ev[f"attnx_{oc}_{nt}"] = e
                    bank_free[pb] = e

            if oc + 1 < DCH:
                @block.scalar
                def _(scalar, oc=oc):
                    P.wait(scalar, "scalar", ev[f"attnx_{oc}_3"])
                    ev_xoch[oc + 1] = P.inc(scalar.dma_start(
                        out=xoch[0][:],
                        in_=xfm[(oc + 1) * 128:(oc + 2) * 128, :]), "s_xo0", 16)
        ev["attnx"] = ev["attnx_7_3"]

        if tap == "attn":
            @block.gpsimd
            def _(gpsimd):
                P.wait(gpsimd, "gpsimd", ev["attnx"])
                P.inc(gpsimd.dma_start(
                    out=out.rearrange("(c p) n -> p c n", p=128), in_=xn_qx[:].rearrange("p (c n) -> p c n", c=DCH)),
                    "so", 16)
                gpsimd.wait_ge(P.sem["so"], 16)
            return nc

        @block.vector
        def _(vector):
            P.wait(vector, "vector", ev["mxg2"])
            e1 = P.inc(vector.tensor_tensor(srow[0:1, 10:11], srow[0:1, 9:10],
                                            mxg2[0:1, 0:1], ALU.mult), "dv")
            vector.wait_ge(P.sem["dv"], e1[1])
            e2 = P.inc(vector.tensor_scalar_mul(srow[0:1, 11:12], srow[0:1, 10:11],
                                                1.0 / QMAX), "dv")
            vector.wait_ge(P.sem["dv"], e2[1])
            ev["rh"] = P.inc(vector.reciprocal(srow[0:1, 12:13],
                                               srow[0:1, 11:12]), "dv")

        @block.tensor
        def _(tensor):
            P.wait(tensor, "tensor", ev["rh"], bank_free[4])
            tensor.matmul(PB[4][:, 0:1], ones_r[:], srow[0:1, 12:13], start=True,
                          stop=True)
            ev["rhbc"] = P.inc(tensor.matmul(PB[4][:, 1:2], ones_r[:],
                                             srow[0:1, 11:12], start=True, stop=True),
                               "pe")

        @block.vector
        def _(vector):
            P.wait(vector, "vector", ev["rhbc"])
            vector.tensor_copy(rcol[:, 1:2], PB[4][:, 0:1])
            e0 = P.inc(vector.tensor_copy(shcol[:, 0:1], PB[4][:, 1:2]), "dv")
            vector.wait_ge(P.sem["dv"], e0[1])
            e = P.inc(vector.tensor_scalar_mul(ds_c[:], wsd[:], shcol[:, 0:1]), "dv")
            ev["dscols"] = e
            bank_free[4] = e

        cm_xo0.__exit__(None, None, None)
        cm_recT.__exit__(None, None, None)
        cm_rtm2.__exit__(None, None, None)
        cm_rtm.__exit__(None, None, None)
        cm_qh0, qh0 = alloc("qh0", [128, ITCH * 512], BF16)
        qhb = [qh0, q_a]

        hv = h_d.rearrange("(c p) n -> p c n", p=128)
        ev_out = {}
        for nt in range(4):
            qh = qhb[nt % 2]

            @block.scalar
            def _(scalar, nt=nt, qh=qh):
                P.wait(scalar, "scalar", ev_h[ITCH - 1], ev_h[ITCH - 2])
                if nt == 0:
                    P.wait(scalar, "scalar", ev["attnx"])
                elif nt == 1:
                    P.wait(scalar, "scalar", ev["womm_7_3"])
                else:
                    P.wait(scalar, "scalar", ev[f"dnmm_{nt - 2}_7"])
                ev[f"qhld_{nt}"] = P.inc(scalar.dma_start(
                    out=qh[:, 0:ITCH * 512],
                    in_=hv[:, :, nt * 512:(nt + 1) * 512]), "s_qh", 16)

            for pc in range(4):
                csl = slice(pc * 2816, (pc + 1) * 2816)

                @block.vector
                def _(vector, nt=nt, pc=pc, csl=csl, qh=qh):
                    P.wait(vector, "vector", ev[f"qhld_{nt}"], ev["dscols"],
                           ev.get(f"qhq_{nt}_{pc - 1}")
                           if pc > 0 else ev.get(f"qhq_{nt - 1}_3"))
                    ev[f"qhv_{nt}_{pc}"] = P.inc(vector.tensor_scalar(
                        wf320[:, :], qh[:, csl], rcol[:, 1:2], MAGIC,
                        ALU.mult, ALU.add), "dv")

                @block.vector
                def _(vector, nt=nt, pc=pc, csl=csl, qh=qh):
                    vector.tensor_scalar(wf320[:, :], wf320[:, :],
                                         MAGIC, QMAX, ALU.subtract, ALU.min)
                    ev[f"qhq_{nt}_{pc}"] = P.inc(vector.tensor_scalar_max(
                        qh[:, csl], wf320[:, :], -(QMAX + 1.0)), "dv")

            for hc in range(DCH):
                pb = (nt * DCH + hc) % 8

                @block.tensor
                def _(tensor, nt=nt, hc=hc, pb=pb, qh=qh):
                    P.wait(tensor, "tensor", ev["qdT"], bank_free[pb],
                           *[ev[f"qhq_{nt}_{pc}"] for pc in range(4)])
                    e = None
                    for ic in range(ITCH):
                        e = P.inc(tensor.matmul(
                            PB[pb][:, :],
                            qdT[:, ic * 1024 + hc * 128: ic * 1024 + (hc + 1) * 128],
                            qh[:, ic * 512:(ic + 1) * 512],
                            start=(ic == 0), stop=(ic == ITCH - 1)), "pe")
                    ev[f"dnmm_{nt}_{hc}"] = e

                @block.vector
                def _(vector, nt=nt, hc=hc, pb=pb):
                    P.wait(vector, "vector", ev[f"dnmm_{nt}_{hc}"], ev["attnx"],
                           ev_out.get((nt, hc - 2)) if hc >= 2
                           else ev_out.get((nt - 1, hc + 6)))
                    e = P.inc(vector.scalar_tensor_tensor(
                        fst[hc % 2][:], PB[pb][:, :], ds_c[:, hc:hc + 1],
                        xn_qx[:, hc * T + nt * 512: hc * T + (nt + 1) * 512],
                        ALU.mult, ALU.add), "dv")
                    ev[f"oute_{nt}_{hc}"] = e
                    bank_free[pb] = e

                @block.sync
                def _(sync, nt=nt, hc=hc):
                    P.wait(sync, "sync", ev[f"oute_{nt}_{hc}"])
                    ev_out[(nt, hc)] = P.inc(sync.dma_start(
                        out=out[hc * 128:(hc + 1) * 128, nt * 512:(nt + 1) * 512],
                        in_=fst[hc % 2][:]), f"s_so{hc % 2}", 16)

        @block.sync
        def _(sync):
            sync.wait_ge(P.sem["s_so0"], P.cnt["s_so0"])
            sync.wait_ge(P.sem["s_so1"], P.cnt["s_so1"])

    return nc


_PAIRSEL = None


def _pairsel_const():
    global _PAIRSEL
    if _PAIRSEL is None:
        ps = np.zeros((16, 1024), dtype=np.float32)
        for e in range(8):
            for p in range(128):
                ps[2 * e + (1 if p >= 64 else 0), e * 128 + p] = 1.0
        _PAIRSEL = ps.astype(ml_dtypes.bfloat16)
    return _PAIRSEL


def kernel(L0_attn_x, L0_attn_Wq, L0_attn_bq, L0_attn_Wk, L0_attn_bk,
           L0_attn_Wv, L0_attn_bv, L0_attn_Wo, L0_attn_bo,
           L0_ffn_clip_x, L0_ffn_clip_down, L0_ffn_w_gate, L0_ffn_w_up,
           L0_ffn_w_down, _tap=None):
    f = np.float32
    x = np.asarray(L0_attn_x, dtype=f)
    shared = dict(
        wqT=np.ascontiguousarray(np.asarray(L0_attn_Wq, f).T),
        wkT=np.ascontiguousarray(np.asarray(L0_attn_Wk, f).T),
        wvT=np.ascontiguousarray(np.asarray(L0_attn_Wv, f).T),
        woT=np.ascontiguousarray(np.asarray(L0_attn_Wo, f).T),
        wg=np.ascontiguousarray(np.asarray(L0_ffn_w_gate, f)),
        wu=np.ascontiguousarray(np.asarray(L0_ffn_w_up, f)),
        wd=np.ascontiguousarray(np.asarray(L0_ffn_w_down, f)),
        bq_c=np.ascontiguousarray(np.asarray(L0_attn_bq, f).reshape(8, 128).T),
        bo_c=np.ascontiguousarray(np.asarray(L0_attn_bo, f).reshape(8, 128).T),
        bk_r=np.ascontiguousarray(np.asarray(L0_attn_bk, f).reshape(1, D)),
        bv_r=np.ascontiguousarray(np.asarray(L0_attn_bv, f).reshape(1, D)),
        clip2=np.array([[float(np.asarray(L0_ffn_clip_x).reshape(-1)[0]),
                         float(np.asarray(L0_ffn_clip_down).reshape(-1)[0])]],
                       dtype=f),
        identin=np.eye(128, dtype=f),
        pairselin=_pairsel_const(),
    )
    in_maps = []
    for c in range(NCORES):
        b, s0 = c // 4, (c % 4) * T
        m = dict(shared)
        m["xfm"] = np.ascontiguousarray(x[b, s0:s0 + T, :].T)
        bs = np.zeros((128, 2), dtype=f)
        bs[:, b] = 1.0
        m["bselin"] = bs
        in_maps.append(m)

    nc = build_program(tap=_tap)
    res = run_bass_kernel_spmd(nc, in_maps, core_ids=list(range(NCORES)))

    outp = np.empty((2, 8192, D), dtype=f)
    for c in range(NCORES):
        b, s0 = c // 4, (c % 4) * T
        outp[b, s0:s0 + T, :] = np.asarray(res.results[c]["out"]).T
    return outp
